# version 09
# baseline (speedup 1.0000x reference)
import numpy as np
import ml_dtypes
from contextlib import ExitStack

import concourse.bass as bass
import concourse.mybir as mybir
from concourse.bass_utils import run_bass_kernel_spmd

F32 = mybir.dt.float32
BF16 = mybir.dt.bfloat16
AF = mybir.ActivationFunctionType
ALU = mybir.AluOpType
AX = mybir.AxisListType

MAGIC = 12582912.0
NCORE = 8
B, T, D, HID = 16, 2048, 1024, 3072
NH, HDIM = 16, 64
BPC = B // NCORE
NBLK = T // 128
NT = T // 512
DC = D // 128
HC = HID // 128
TOK = BPC * T
NTT = TOK // 512


class Ctx:
    ENGINES = ("sync", "gpsimd", "vector", "scalar", "tensor")

    def __init__(self, nc, es):
        self.nc, self.es = nc, es
        self.sems, self.counts = {}, {}
        self.plan = {e: [] for e in self.ENGINES}

    def sem(self, name):
        if name not in self.sems:
            self.sems[name] = self.es.enter_context(self.nc.semaphore(name))
            self.counts[name] = 0
        return self.sems[name]

    def val(self, name):
        self.sem(name)
        return self.counts[name]

    def ctr(self, name):
        return self.counts.setdefault("#" + name, 0)

    def ctr_inc(self, name):
        self.counts["#" + name] = self.ctr(name) + 1

    def emit(self, eng, fn):
        self.plan[eng].append(fn)

    def wait(self, eng, name, value=None):
        h = self.sem(name)
        v = self.counts[name] if value is None else value
        if v > 0:
            self.emit(eng, lambda e, h=h, v=v: e.wait_ge(h, v))

    def prod(self, eng, name, k, fn):
        h = self.sem(name)
        self.counts[name] += k
        self.emit(eng, lambda e, h=h, k=k, fn=fn: fn(e).then_inc(h, k))

    def prodw(self, eng, name, k, fn):
        self.prod(eng, name, k, fn)
        self.wait(eng, name)

    def run_block(self, label=None):
        plan, self.plan = self.plan, {e: [] for e in self.ENGINES}
        import contextlib
        scope = self.nc.named_scope(label) if label else contextlib.nullcontext()
        with scope, self.nc.Block() as blk:
            blk.sync(lambda e: [f(e) for f in plan["sync"]])
            blk.gpsimd(lambda e: [f(e) for f in plan["gpsimd"]])
            blk.vector(lambda e: [f(e) for f in plan["vector"]])
            blk.scalar(lambda e: [f(e) for f in plan["scalar"]])
            blk.tensor(lambda e: [f(e) for f in plan["tensor"]])


def build_nc(dbg=False):
    nc = bass.Bass("TRN2", target_bir_lowering=False, detect_race_conditions=False)

    x_fm = nc.dram_tensor("x_fm", [DC, 128, TOK], F32, kind="ExternalInput").ap()
    w_inT = nc.dram_tensor("w_inT", [D, 3 * D], F32, kind="ExternalInput").ap()
    b_in = nc.dram_tensor("b_in", [3 * D], F32, kind="ExternalInput").ap()
    w_outT = nc.dram_tensor("w_outT", [D, D], F32, kind="ExternalInput").ap()
    b_out = nc.dram_tensor("b_out", [D], F32, kind="ExternalInput").ap()
    w_gate = nc.dram_tensor("w_gate", [HID, D], F32, kind="ExternalInput").ap()
    w_up = nc.dram_tensor("w_up", [HID, D], F32, kind="ExternalInput").ap()
    w_down = nc.dram_tensor("w_down", [D, HID], F32, kind="ExternalInput").ap()
    c_fm = nc.dram_tensor("c_fm", [DC, 128, BPC], F32, kind="ExternalInput").ap()
    wmod = nc.dram_tensor("wmod", [D, 6 * D], F32, kind="ExternalInput").ap()
    ident_bf = nc.dram_tensor("ident_bf", [128, 128], BF16, kind="ExternalInput").ap()
    ident_f = nc.dram_tensor("ident_f", [128, 128], F32, kind="ExternalInput").ap()
    masks = nc.dram_tensor("masks", [128, 3 * 384], BF16, kind="ExternalInput").ap()
    ones_r_f = nc.dram_tensor("ones_r_f", [1, 128], F32, kind="ExternalInput").ap()
    ones_r_bf = nc.dram_tensor("ones_r_bf", [1, 512], BF16, kind="ExternalInput").ap()
    invD_c_bf = nc.dram_tensor("invD_c_bf", [128, 1], BF16, kind="ExternalInput").ap()

    out_fm = nc.dram_tensor("out_fm", [DC, 128, TOK], F32, kind="ExternalOutput").ap()

    skind = "ExternalOutput" if dbg else "Internal"
    x1_h = nc.dram_tensor("x1_h", [DC, 128, TOK], F32, kind=skind).ap()
    h2_h = nc.dram_tensor("h2_h", [DC, 128, TOK], BF16, kind=skind).ap()
    upr_h = nc.dram_tensor("upr_h", [HC, 128, TOK], BF16, kind=skind).ap()
    gtr_h = nc.dram_tensor("gtr_h", [HC, 128, TOK], BF16, kind=skind).ap()
    pp_h = nc.dram_tensor("pp_h", [HC, 128, TOK], BF16, kind=skind).ap()
    oo_h = nc.dram_tensor("oo_h", [HC, 128, TOK], BF16, kind=skind).ap()
    scal_h = (nc.dram_tensor("scal_h", [1, 16], F32, kind="ExternalOutput").ap()
              if dbg else None)
    if dbg:
        tap_h = nc.dram_tensor("tap_h", [128, 4096], BF16, kind="ExternalOutput").ap()
        tap_q = nc.dram_tensor("tap_q", [128, 4096], BF16, kind="ExternalOutput").ap()
        tap_k = nc.dram_tensor("tap_k", [128, 4096], BF16, kind="ExternalOutput").ap()
        tap_v = nc.dram_tensor("tap_v", [128, 1024], BF16, kind="ExternalOutput").ap()
        tap_c = nc.dram_tensor("tap_c", [128, 1024], BF16, kind="ExternalOutput").ap()
        tap_ab = nc.dram_tensor("tap_ab", [128, 256], F32, kind="ExternalOutput").ap()
        tap_e = nc.dram_tensor("tap_e", [128, 384], BF16, kind="ExternalOutput").ap()
        tap_at = nc.dram_tensor("tap_at", [128, 384], BF16, kind="ExternalOutput").ap()
        tap_d = nc.dram_tensor("tap_d", [128, 128], BF16, kind="ExternalOutput").ap()
        tap_sm = nc.dram_tensor("tap_sm", [128, 4], F32, kind="ExternalOutput").ap()
    cc_in = nc.dram_tensor("cc_in", [1, 16], F32, kind="Internal")
    cc_out = nc.dram_tensor("cc_out", [1, 16], F32, kind="Internal")

    es = ExitStack()
    with es:
        C = Ctx(nc, es)
        sbuf = lambda n, s, dt=F32: es.enter_context(nc.sbuf_tensor(n, s, dt))

        idbf = sbuf("idbf", [128, 128], BF16)
        idf = sbuf("idf", [128, 128], F32)
        msk = sbuf("msk", [128, 3 * 384], BF16)
        onesrf = sbuf("onesrf", [1, 128], F32)
        onesrbf = sbuf("onesrbf", [1, 512], BF16)
        invDc = sbuf("invDc", [128, 1], BF16)
        bq_col = sbuf("bq_col", [128, 8], F32)
        bk_col = sbuf("bk_col", [128, 8], F32)
        bv_row = sbuf("bv_row", [1, 1024], BF16)
        bo_row = sbuf("bo_row", [1, 1024], BF16)
        modv = sbuf("modv", [128, 6 * 16], F32)
        scal = sbuf("scal", [128, 16], F32)
        runmax = sbuf("runmax", [128, 8], F32)
        rows = sbuf("rows", [1, 512], F32)
        ccrow = sbuf("ccrow", [1, 16], F32)
        gascr = sbuf("gascr", [128, 4], F32)
        mgc = sbuf("mgc", [128, 2], F32)
        maxbuf = sbuf("maxbuf", [128, 384], F32)

        MV_ONEPCA, MV_SA, MV_GA, MV_ONEPCB, MV_SB, MV_GB = range(6)

        def mv(which, c, s):
            j = which * 16 + c * BPC + s
            return modv[:, j: j + 1]

        SC_INVX, SC_KG, SC_SGT, SC_BGT, SC_KU, SC_K1, SC_K2, SC_SO = range(8)
        scv = lambda j: scal[:, j: j + 1]

        with ExitStack() as es0:
            sb0 = lambda n, s, dt=F32: es0.enter_context(nc.sbuf_tensor(n, s, dt))
            cfm = sb0("cfm", [128, DC * BPC])
            silc = sb0("silc", [128, DC * BPC])
            wmt = sb0("wmt", [128, 2 * 512])
            modr = sb0("modr", [BPC, 6 * D])
            bcol_raw = sb0("bcol_raw", [128, 16])
            bvr_f = sb0("bvr_f", [1, 1024])
            bor_f = sb0("bor_f", [1, 1024])
            mps = es0.enter_context(nc.psum_tensor("mps", [BPC, 512], F32))
            tps = es0.enter_context(nc.psum_tensor("tps", [128, 96], F32))

            loads = [(idbf, ident_bf), (idf, ident_f), (msk, masks),
                     (onesrf, ones_r_f), (onesrbf, ones_r_bf), (invDc, invD_c_bf),
                     (bvr_f, b_in[2 * D: 3 * D].unsqueeze(0)),
                     (bor_f, b_out.unsqueeze(0))]
            for dst, src in loads:
                C.prod("sync", "p0ld", 16, lambda e, d=dst, s=src: e.dma_start(out=d[:], in_=s))
            C.prod("sync", "p0ld", 16, lambda e: e.dma_start(
                out=cfm[:].rearrange("p (c s) -> p c s", s=BPC),
                in_=c_fm.rearrange("c p s -> p c s"),
                allow_slow_non_contiguous=True))
            C.prod("sync", "p0ld", 16, lambda e: e.dma_start(
                out=bcol_raw[:], in_=b_in[0: 2 * D].rearrange("(c p) -> p c", p=128),
                allow_slow_non_contiguous=True))

            C.wait("vector", "p0ld")
            C.prod("vector", "p0v", 1, lambda e: e.tensor_copy(bv_row[:], bvr_f[:]))
            C.prod("vector", "p0v", 1, lambda e: e.tensor_copy(bo_row[:], bor_f[:]))
            C.prod("vector", "p0v", 1, lambda e: e.tensor_scalar(
                bq_col[:], bcol_raw[:, 0:8], 0.125, None, ALU.mult))
            C.prod("vector", "p0v", 1, lambda e: e.tensor_copy(bk_col[:], bcol_raw[:, 8:16]))
            C.prod("vector", "p0v", 1, lambda e: e.memset(runmax[:], 0.0))
            C.prod("vector", "p0v", 1, lambda e: e.memset(ccrow[:], 0.0))
            C.prod("vector", "p0v", 1, lambda e: e.memset(mgc[:, 0:1], MAGIC))
            C.prod("vector", "p0v", 1, lambda e: e.memset(mgc[:, 1:2], -MAGIC))
            C.prod("vector", "p0v", 1, lambda e: e.memset(maxbuf[:], 0.0))
            C.wait("scalar", "p0ld")
            C.prod("scalar", "p0s", 1, lambda e: e.activation(silc[:], cfm[:], AF.Silu))

            C.wait("tensor", "p0s")
            for j in range(12):
                C.wait("tensor", "p0ev", j)
                for dc in range(DC):
                    i = j * DC + dc
                    slot = i % 2
                    C.wait("sync", "p0mm", max(0, i - 1))
                    C.prod("sync", f"p0wld{slot}", 16, lambda e, j=j, dc=dc, slot=slot: e.dma_start(
                        out=wmt[:, slot * 512: slot * 512 + 512],
                        in_=wmod[dc * 128:(dc + 1) * 128, j * 512:(j + 1) * 512]))
                    C.wait("tensor", f"p0wld{slot}")
                    C.prod("tensor", "p0mm", 1, lambda e, dc=dc, slot=slot: e.matmul(
                        mps[:, 0:512], silc[:, dc * BPC:(dc + 1) * BPC],
                        wmt[:, slot * 512: slot * 512 + 512],
                        start=(dc == 0), stop=(dc == DC - 1)))
                C.wait("vector", "p0mm")
                C.prod("vector", "p0ev", 1, lambda e, j=j: e.tensor_copy(
                    modr[:, j * 512:(j + 1) * 512], mps[:, 0:512]))

            C.wait("tensor", "p0ev")
            for k in range(48):
                C.prod("tensor", "p0tr", 1, lambda e, k=k: e.transpose(
                    tps[:, k * BPC:(k + 1) * BPC],
                    modr[:, k * 128:(k + 1) * 128], idf[0:BPC, 0:BPC]))
            C.wait("vector", "p0tr")
            C.wait("scalar", "p0tr")

            def tcol(v, c):
                k = v * 8 + c
                return tps[:, k * BPC:(k + 1) * BPC]

            for c in range(DC):
                C.prod("vector", "p0d", 1, lambda e, c=c: e.tensor_copy(
                    modv[:, MV_SA * 16 + c * BPC: MV_SA * 16 + (c + 1) * BPC], tcol(0, c)))
                C.prod("vector", "p0d", 1, lambda e, c=c: e.tensor_scalar(
                    modv[:, MV_ONEPCA * 16 + c * BPC: MV_ONEPCA * 16 + (c + 1) * BPC],
                    tcol(1, c), 1.0, None, ALU.add))
                C.prod("vector", "p0d", 1, lambda e, c=c: e.tensor_copy(
                    modv[:, MV_SB * 16 + c * BPC: MV_SB * 16 + (c + 1) * BPC], tcol(3, c)))
                C.prod("vector", "p0d", 1, lambda e, c=c: e.tensor_scalar(
                    modv[:, MV_ONEPCB * 16 + c * BPC: MV_ONEPCB * 16 + (c + 1) * BPC],
                    tcol(4, c), 1.0, None, ALU.add))
                for (vec, mvidx) in ((2, MV_GA), (5, MV_GB)):
                    C.prod("scalar", "p0ga", 1, lambda e, vec=vec, c=c: e.activation(
                        gascr[:, 0:BPC], tcol(vec, c), AF.Abs))
                    C.prod("scalar", "p0ga", 1, lambda e, vec=vec, c=c: e.activation(
                        gascr[:, 2:2 + BPC], tcol(vec, c), AF.Sign))
                    C.wait("vector", "p0ga")
                    C.prodw("vector", "p0gb", 1, lambda e: e.tensor_scalar(
                        gascr[:, 0:BPC], gascr[:, 0:BPC], 0.25, None, ALU.max))
                    C.prod("vector", "p0gb", 1, lambda e, mvidx=mvidx, c=c: e.tensor_tensor(
                        modv[:, mvidx * 16 + c * BPC: mvidx * 16 + (c + 1) * BPC],
                        gascr[:, 0:BPC], gascr[:, 2:2 + BPC], ALU.mult))
                    C.wait("scalar", "p0gb")
            C.run_block("P0")

        with ExitStack() as esA:
            sbA = lambda n, s, dt=F32: esA.enter_context(nc.sbuf_tensor(n, s, dt))
            winT = [sbA(f"winT{c}", [128, 3 * D], BF16) for c in range(DC)]
            woutT = [sbA(f"woutT{c}", [128, D], BF16) for c in range(DC)]
            xfm = [sbA(f"xfm{i}", [128, DC * 512]) for i in range(2)]
            hbuf = [sbA(f"hbuf{i}", [128, DC * 512], BF16) for i in range(2)]
            QT = [sbA(f"QT{i}", [128, DC * 512], BF16) for i in range(2)]
            KT = [sbA(f"KT{i}", [128, DC * 512], BF16) for i in range(2)]
            VT = [sbA(f"VT{i}", [128, 1024], BF16) for i in range(6)]
            ctxT = [sbA(f"ctxT{i}", [128, DC * 128], BF16) for i in range(2)]
            xbf = [sbA(f"xbf{i}", [128, 1024], BF16) for i in range(2)]
            xsq = [sbA(f"xsq{i}", [128, 1024], BF16) for i in range(2)]
            ubuf = [sbA(f"ubuf{i}", [128, 1024]) for i in range(2)]
            u2buf = [sbA(f"u2buf{i}", [128, 1024]) for i in range(2)]
            bcsb = [sbA(f"bcsb{i}", [128, 256]) for i in range(2)]
            h2st = [sbA(f"h2st{i}", [128, 1024], BF16) for i in range(2)]
            expb = [sbA(f"expb{i}", [128, 384], BF16) for i in range(2)]
            mexp = [sbA(f"mexp{i}", [128, 384], BF16) for i in range(2)]
            atb = [sbA(f"atb{i}", [128, 384], BF16) for i in range(2)]
            Dt = [sbA(f"Dt{i}", [128, 128], BF16) for i in range(2)]
            nmx = sbA("nmx", [128, 4])
            sume = sbA("sume", [128, 4])

            st_ps = esA.enter_context(nc.psum_tensor("st_ps", [128, 256], F32))
            mm_ps = [esA.enter_context(nc.psum_tensor(f"mm_ps{i}", [128, 512], F32))
                     for i in range(2)]
            sc_ps = [esA.enter_context(nc.psum_tensor(f"sc_ps{i}", [128, 384], F32))
                     for i in range(2)]
            at_ps = [esA.enter_context(nc.psum_tensor(f"at_ps{i}", [128, 384], F32))
                     for i in range(2)]
            cx_ps = esA.enter_context(nc.psum_tensor("cx_ps", [128, 128], F32))

            for _eng in Ctx.ENGINES:
                C.wait(_eng, "p0gb")
                C.wait(_eng, "p0ld")
                C.wait(_eng, "p0v")

            for c in range(DC):
                C.prod("gpsimd", "wina", 16, lambda e, c=c: e.dma_start(
                    out=winT[c][:], in_=w_inT[c * 128:(c + 1) * 128, :]))
            for c in range(DC):
                C.prod("gpsimd", "wouta", 16, lambda e, c=c: e.dma_start(
                    out=woutT[c][:], in_=w_outT[c * 128:(c + 1) * 128, :]))

            tok0 = lambda s, t4: s * T + t4 * 512
            h2sp_marks = []
            C.counts.setdefault("mmuse", 0)

            def acquire_mm_bank():
                bank = C.counts["mmuse"] % 2
                C.wait("tensor", f"mmfree{bank}", C.val(f"mmfree{bank}"))
                C.counts["mmuse"] += 1
                return bank

            def emit_ln(s, b, second):
                t4, b4 = b // 4, b % 4
                slot, r2 = t4 % 2, b % 2
                xin = (xfm[slot].ap()
                       .rearrange("p (c w) -> p c w", w=512)[:, :, b4 * 128:(b4 + 1) * 128])
                C.prod("vector", "vxbf", 1, lambda e, xin=xin, r2=r2: e.tensor_copy(
                    xbf[r2][:].rearrange("p (c w) -> p c w", w=128), xin))
                C.prod("scalar", "sxsq", 1, lambda e, xin=xin, r2=r2: e.activation(
                    xsq[r2][:].rearrange("p (c w) -> p c w", w=128), xin, AF.Square))
                C.wait("tensor", "vbc", C.val("vbc"))
                C.wait("tensor", "vxbf")
                C.wait("tensor", "sxsq")
                for dc in range(DC):
                    C.prod("tensor", "tst", 1, lambda e, dc=dc, r2=r2: e.matmul(
                        st_ps[0:1, 0:128], invDc[:], xbf[r2][:, dc * 128:(dc + 1) * 128],
                        start=(dc == 0), stop=False))
                for dc in range(DC):
                    C.prod("tensor", "tst", 1, lambda e, dc=dc, r2=r2: e.matmul(
                        st_ps[0:1, 128:256], invDc[:], xsq[r2][:, dc * 128:(dc + 1) * 128],
                        start=False, stop=(dc == DC - 1)))
                C.wait("vector", "tst")
                C.prodw("vector", "vst", 1, lambda e: e.tensor_copy(rows[0:1, 0:256], st_ps[0:1, 0:256]))
                C.prodw("vector", "vst", 1, lambda e: e.tensor_tensor(
                    rows[0:1, 256:384], rows[0:1, 0:128], rows[0:1, 0:128], ALU.mult))
                C.prodw("vector", "vst", 1, lambda e: e.tensor_tensor(
                    rows[0:1, 256:384], rows[0:1, 128:256], rows[0:1, 256:384], ALU.subtract))
                C.prodw("vector", "vst", 1, lambda e: e.tensor_scalar(
                    rows[0:1, 256:384], rows[0:1, 256:384], 1e-5, None, ALU.add))
                C.wait("scalar", "vst")
                C.prodw("scalar", "sst", 1, lambda e: e.activation(
                    rows[0:1, 384:512], rows[0:1, 256:384], AF.Ln))
                C.prodw("scalar", "sst", 1, lambda e: e.activation(
                    rows[0:1, 384:512], rows[0:1, 384:512], AF.Exp, scale=-0.5))
                C.wait("vector", "sst")
                C.prodw("vector", "vst2", 1, lambda e: e.tensor_tensor(
                    rows[0:1, 256:384], rows[0:1, 0:128], rows[0:1, 384:512], ALU.mult))
                C.prodw("vector", "vst2", 1, lambda e: e.tensor_scalar(
                    rows[0:1, 256:384], rows[0:1, 256:384], -1.0, None, ALU.mult))
                C.wait("tensor", "vst2")
                C.prod("tensor", "tbc", 1, lambda e: e.matmul(
                    st_ps[:, 0:128], onesrf[:], rows[0:1, 384:512], start=True, stop=False))
                C.prod("tensor", "tbc", 1, lambda e: e.matmul(
                    st_ps[:, 128:256], onesrf[:], rows[0:1, 256:384], start=False, stop=True))
                C.wait("vector", "tbc")
                C.prod("vector", "vbc", 1, lambda e, r2=r2: e.tensor_copy(bcsb[r2][:], st_ps[:]))
                C.wait("gpsimd", "vbc")
                a_bc = bcsb[r2][:, 0:128].unsqueeze(1).broadcast_to([128, 8, 128])
                b_bc = bcsb[r2][:, 128:256].unsqueeze(1).broadcast_to([128, 8, 128])
                C.prod("gpsimd", "gu", 1, lambda e, xin=xin, a_bc=a_bc, r2=r2: e.tensor_tensor(
                    ubuf[r2][:].rearrange("p (c w) -> p c w", w=128), xin, a_bc, ALU.mult))
                C.prod("gpsimd", "gu", 1, lambda e, b_bc=b_bc, r2=r2: e.tensor_tensor(
                    u2buf[r2][:].rearrange("p (c w) -> p c w", w=128),
                    ubuf[r2][:].rearrange("p (c w) -> p c w", w=128), b_bc, ALU.add))
                C.wait("vector", "gu")
                if second:
                    C.wait("vector", f"h2sp{r2}", C.val(f"h2sp{r2}"))
                for c in range(DC):
                    u2c = u2buf[r2][:, c * 128:(c + 1) * 128]
                    if not second:
                        hdst = hbuf[slot][:, c * 512 + b4 * 128: c * 512 + b4 * 128 + 128]
                        C.prod("vector", "vh", 1, lambda e, u2c=u2c, hdst=hdst, c=c, s=s:
                               e.tensor_scalar(hdst, u2c, mv(MV_ONEPCA, c, s),
                                               mv(MV_SA, c, s), ALU.mult, ALU.add))
                    else:
                        hdst = h2st[r2][:, c * 128:(c + 1) * 128]
                        C.prod("vector", "vh2", 1, lambda e, u2c=u2c, hdst=hdst, c=c, s=s:
                               e.tensor_scalar(hdst, u2c, mv(MV_ONEPCB, c, s),
                                               mv(MV_SB, c, s), ALU.mult, ALU.add))
                if not second:
                    C.wait("vector", "vh")
                    for c in range(DC):
                        hdst = hbuf[slot][:, c * 512 + b4 * 128: c * 512 + b4 * 128 + 128]
                        xdst = xfm[slot][:, c * 512 + b4 * 128: c * 512 + b4 * 128 + 128]
                        C.prod("vector", "vx1", 1, lambda e, hdst=hdst, xdst=xdst, c=c, s=s:
                               e.scalar_tensor_tensor(xdst, hdst, mv(MV_GA, c, s), xdst,
                                                      ALU.mult, ALU.add))
                if second:
                    C.wait("vector", "vh2")
                    mcol = s * NBLK + b
                    C.prod("vector", "vh2m", 1, lambda e, r2=r2, mcol=mcol: e.tensor_reduce(
                        maxbuf[:, mcol:mcol + 1], h2st[r2][:], AX.X, ALU.max,
                        apply_absolute_value=True))
                    C.wait("gpsimd", "vh2", C.val("vh2"))
                    tk = s * T + b * 128
                    for c in range(DC):
                        C.prod("gpsimd", f"h2sp{r2}", 16, lambda e, c=c, tk=tk, r2=r2: e.dma_start(
                            out=h2_h[c, :, tk: tk + 128], in_=h2st[r2][:, c * 128:(c + 1) * 128]))

            def emit_qkv(s, t4):
                slot = t4 % 2
                C.wait("tensor", "wina", 16 * DC)
                C.wait("tensor", "vh", C.val("vh"))
                for ec in range(16):
                    bank = acquire_mm_bank()
                    for dc in range(DC):
                        C.prod("tensor", "tmm", 1, lambda e, ec=ec, dc=dc, bank=bank, slot=slot:
                               e.matmul(mm_ps[bank][:],
                                        winT[dc][:, ec * 128:(ec + 1) * 128],
                                        hbuf[slot][:, dc * 512:(dc + 1) * 512],
                                        start=(dc == 0), stop=(dc == DC - 1)))
                    C.wait("scalar", "tmm")
                    if ec < 8:
                        dst = QT[slot][:, ec * 512:(ec + 1) * 512]
                        bias, scale = bq_col[:, ec:ec + 1], 0.125
                    else:
                        dst = KT[slot][:, (ec - 8) * 512:(ec - 7) * 512]
                        bias, scale = bk_col[:, ec - 8:ec - 7], 1.0
                    C.prod("scalar", f"mmfree{bank}", 1, lambda e, dst=dst, bias=bias,
                           scale=scale, bank=bank: e.activation(dst, mm_ps[bank][:], AF.Identity,
                                                                bias=bias, scale=scale))
                for b4 in range(4):
                    b = t4 * 4 + b4
                    for g in range(2):
                        bank = acquire_mm_bank()
                        for dc in range(DC):
                            C.prod("tensor", "tmm", 1, lambda e, b4=b4, g=g, dc=dc, bank=bank,
                                   slot=slot: e.matmul(
                                       mm_ps[bank][:],
                                       hbuf[slot][:, dc * 512 + b4 * 128: dc * 512 + b4 * 128 + 128],
                                       winT[dc][:, 2 * D + g * 512: 2 * D + (g + 1) * 512],
                                       start=(dc == 0), stop=False))
                        C.prod("tensor", "tmm", 1, lambda e, g=g, bank=bank: e.matmul(
                            mm_ps[bank][:], onesrbf[0:1, 0:128],
                            bv_row[0:1, g * 512:(g + 1) * 512], start=False, stop=True))
                        C.wait("scalar", "tmm")
                        C.prod("scalar", f"mmfree{bank}", 1, lambda e, b=b, g=g, bank=bank:
                               e.activation(VT[b % 6][:, g * 512:(g + 1) * 512],
                                            mm_ps[bank][:], AF.Identity))

            attmk = {}

            def emit_attention(s, b):
                t4 = b // 4
                kb0 = min(max(b - 1, 0), NBLK - 3)
                mvar = 1 if b == 0 else (2 if b == NBLK - 1 else 0)
                C.wait("tensor", "mmfree0", C.val("mmfree0"))
                C.wait("tensor", "mmfree1", C.val("mmfree1"))
                G0 = C.ctr("hctr")

                def St(h):
                    G = G0 + h
                    sl = G % 2
                    hc, hr = h // 2, (h % 2) * 64
                    if ("se", G - 2) in attmk:
                        C.wait("tensor", "sexp", attmk[("se", G - 2)])
                    qap = QT[t4 % 2][hr:hr + 64,
                                     hc * 512 + (b % 4) * 128: hc * 512 + (b % 4) * 128 + 128]
                    for kk in range(3):
                        kb = kb0 + kk
                        kap = KT[(kb // 4) % 2][hr:hr + 64,
                                                hc * 512 + (kb % 4) * 128: hc * 512 + (kb % 4) * 128 + 128]
                        C.prod("tensor", "tsc", 1, lambda e, sl=sl, kk=kk, qap=qap, kap=kap:
                               e.matmul(sc_ps[sl][:, kk * 128:(kk + 1) * 128], qap, kap,
                                        start=(kk == 0), stop=(kk == 2)))
                    attmk[("st", G)] = C.val("tsc")

                def Vn(h):
                    G = G0 + h
                    sl = G % 2
                    C.wait("vector", "tsc", attmk[("st", G)])
                    C.prod("vector", "vnm", 1, lambda e, sl=sl: e.tensor_reduce(
                        nmx[:, sl:sl + 1], sc_ps[sl][:], AX.X, ALU.max, negate=True))
                    attmk[("vn", G)] = C.val("vnm")

                def Se(h):
                    G = G0 + h
                    sl = G % 2
                    C.wait("scalar", "vnm", attmk[("vn", G)])
                    if ("ta", G - 2) in attmk:
                        C.wait("scalar", "tat", attmk[("ta", G - 2)])
                    C.prod("scalar", "sexp", 1, lambda e, sl=sl: e.activation(
                        expb[sl][:], sc_ps[sl][:], AF.Exp, bias=nmx[:, sl:sl + 1]))
                    attmk[("se", G)] = C.val("sexp")

                def Vd(h):
                    G = G0 + h
                    sl = G % 2
                    C.wait("vector", "sexp", attmk[("se", G)])
                    if dbg and s == 0 and b == 0 and h == 0:
                        C.wait("gpsimd", "sexp", attmk[("se", G)])
                        C.prod("gpsimd", "tapd", 16, lambda e, sl=sl: e.dma_start(
                            out=tap_e, in_=expb[sl][:]))
                        C.wait("gpsimd", "tapd")
                        for _e2 in ("vector", "scalar", "tensor"):
                            C.wait(_e2, "tapd")
                    if ("ta", G - 2) in attmk:
                        C.wait("vector", "tat", attmk[("ta", G - 2)])
                    C.prod("vector", "vmk", 1, lambda e, sl=sl, mvar=mvar: e.scalar_tensor_tensor(
                        mexp[sl][:], expb[sl][:], 1.0, msk[:, mvar * 384:(mvar + 1) * 384],
                        ALU.mult, ALU.mult, accum_out=sume[:, sl:sl + 1]))
                    C.wait("scalar", "vmk")
                    C.prod("scalar", "srcp", 1, lambda e, sl=sl: e.activation(
                        nmx[:, sl + 2:sl + 3], sume[:, sl:sl + 1], AF.Ln))
                    C.prod("scalar", "srcp", 1, lambda e, sl=sl: e.activation(
                        sume[:, sl + 2:sl + 3], nmx[:, sl + 2:sl + 3], AF.Exp, scale=-1.0))
                    C.wait("vector", "srcp")
                    C.prod("vector", "vD", 1, lambda e, sl=sl: e.tensor_scalar(
                        Dt[sl][:], idbf[:], sume[:, sl + 2:sl + 3], None, ALU.mult))
                    attmk[("vd", G)] = C.val("vD")
                    if dbg and s == 0 and b == 0 and h == 0:
                        C.wait("gpsimd", "vD", C.val("vD"))
                        C.prod("gpsimd", "tapd", 16, lambda e: e.dma_start(
                            out=tap_sm, in_=sume[:]))
                        C.prod("gpsimd", "tapd", 16, lambda e, sl=sl: e.dma_start(
                            out=tap_at, in_=mexp[sl][:]))
                        C.prod("gpsimd", "tapd", 16, lambda e, sl=sl: e.dma_start(
                            out=tap_d, in_=Dt[sl][:]))
                        C.wait("gpsimd", "tapd")
                        for _e2 in ("vector", "scalar", "tensor"):
                            C.wait(_e2, "tapd")

                def Ta(h):
                    G = G0 + h
                    sl = G % 2
                    C.wait("tensor", "vD", attmk[("vd", G)])
                    if ("va", G - 2) in attmk:
                        C.wait("tensor", "vat", attmk[("va", G - 2)])
                    for kk in range(3):
                        C.prod("tensor", "tat", 1, lambda e, sl=sl, kk=kk: e.matmul(
                            at_ps[sl][:, kk * 128:(kk + 1) * 128],
                            mexp[sl][:, kk * 128:(kk + 1) * 128], Dt[sl][:],
                            start=(kk == 0), stop=(kk == 2)))
                    attmk[("ta", G)] = C.val("tat")

                def Va(h):
                    G = G0 + h
                    sl = G % 2
                    C.wait("vector", "tat", attmk[("ta", G)])
                    if ("tc", G - 2) in attmk:
                        C.wait("vector", "tcx", attmk[("tc", G - 2)])
                    C.prod("vector", "vat", 1, lambda e, sl=sl: e.tensor_copy(
                        atb[sl][:], at_ps[sl][:]))
                    attmk[("va", G)] = C.val("vat")

                def Tc(h):
                    G = G0 + h
                    sl = G % 2
                    hr = (h % 2) * 64
                    C.wait("tensor", "vat", attmk[("va", G)])
                    if h % 2 == 0 and ("vc", G - 1) in attmk:
                        C.wait("tensor", "vcx", attmk[("vc", G - 1)])
                    for kk in range(3):
                        C.prod("tensor", "tcx", 1, lambda e, kk=kk, h=h, hr=hr, kb0=kb0, sl=sl:
                               e.matmul(cx_ps[hr:hr + 64, :],
                                        VT[(kb0 + kk) % 6][:, h * 64:(h + 1) * 64],
                                        atb[sl][:, kk * 128:(kk + 1) * 128],
                                        start=(kk == 0), stop=(kk == 2)))
                    attmk[("tc", G)] = C.val("tcx")

                def Vc(h):
                    G = G0 + h
                    C.wait("vector", "tcx", attmk[("tc", G)])
                    C.prod("vector", "vcx", 1, lambda e, h=h, b=b: e.tensor_copy(
                        ctxT[b % 2][:, (h // 2) * 128:(h // 2 + 1) * 128], cx_ps[:]))
                    attmk[("vc", G)] = C.val("vcx")
                    attmk[("vc", G - 1)] = C.val("vcx")

                for i in range(NH + 2):
                    if i < NH:
                        St(i)
                    if 0 <= i - 1 < NH:
                        Vn(i - 1)
                        Se(i - 1)
                    if 0 <= i - 2 < NH:
                        h = i - 2
                        Vd(h)
                        Ta(h)
                        Va(h)
                        Tc(h)
                        if h % 2 == 1:
                            Vc(h)
                for h in range(NH):
                    C.ctr_inc("hctr")

            def emit_proj(s, b):
                t4, b4 = b // 4, b % 4
                slot = t4 % 2
                C.wait("tensor", "wouta", 16 * DC)
                C.wait("tensor", "vcx", C.val("vcx"))
                for dc in range(DC):
                    bank = acquire_mm_bank()
                    for ec in range(DC):
                        C.prod("tensor", "tmm", 1, lambda e, dc=dc, ec=ec, bank=bank, b=b:
                               e.matmul(mm_ps[bank][:, 0:128],
                                        woutT[ec][:, dc * 128:(dc + 1) * 128],
                                        ctxT[b % 2][:, ec * 128:(ec + 1) * 128],
                                        start=(ec == 0), stop=False))
                    C.prod("tensor", "tmm", 1, lambda e, dc=dc, bank=bank: e.matmul(
                        mm_ps[bank][:, 0:128], bo_row[0:1, dc * 128:(dc + 1) * 128],
                        onesrbf[0:1, 0:128], start=False, stop=True))
                    C.wait("vector", "tmm")
                    xdst = xfm[slot][:, dc * 512 + b4 * 128: dc * 512 + b4 * 128 + 128]
                    C.prod("vector", f"mmfree{bank}", 1, lambda e, dc=dc, xdst=xdst, bank=bank,
                           s=s: e.scalar_tensor_tensor(xdst, mm_ps[bank][:, 0:128],
                                                       mv(MV_GA, dc, s), xdst, ALU.mult, ALU.add))

            def spill_x1(s, t4):
                slot = t4 % 2
                C.wait("gpsimd", "mmfree0", C.val("mmfree0"))
                C.wait("gpsimd", "mmfree1", C.val("mmfree1"))
                for c in range(DC):
                    C.prod("gpsimd", f"x1sp{slot}", 16, lambda e, c=c, s=s, t4=t4, slot=slot: e.dma_start(
                        out=x1_h[c, :, tok0(s, t4): tok0(s, t4) + 512],
                        in_=xfm[slot][:, c * 512:(c + 1) * 512]))

            if dbg:
                def emit_tap(dst, src_ap, width):
                    for nm in ("vh", "vh2", "vcx", "vbc"):
                        C.wait("gpsimd", nm, C.val(nm))
                    for nm in ("mmfree0", "mmfree1"):
                        C.wait("gpsimd", nm, C.val(nm))
                    C.prod("gpsimd", "tapd", 16, lambda e, dst=dst, src_ap=src_ap, width=width:
                           e.dma_start(out=dst[:, 0:width], in_=src_ap))
                    C.wait("gpsimd", "tapd")
                    for _e2 in ("vector", "scalar", "tensor"):
                        C.wait(_e2, "tapd")

            for s in range(BPC):
                att_next = 0
                for t4 in range(NT):
                    slot = t4 % 2
                    C.wait("sync", f"x1sp{slot}", C.val(f"x1sp{slot}"))
                    C.wait("sync", "vh2m", C.val("vh2m"))
                    for c in range(DC):
                        C.prod("sync", f"qx{slot}", 16, lambda e, c=c, s=s, t4=t4, slot=slot:
                               e.dma_start(out=xfm[slot][:, c * 512:(c + 1) * 512],
                                           in_=x_fm[c, :, tok0(s, t4): tok0(s, t4) + 512]))
                    C.wait("vector", f"qx{slot}")
                    C.wait("scalar", f"qx{slot}")
                    C.wait("gpsimd", f"qx{slot}")
                    C.wait("vector", "mmfree", max(0, C.val("mmfree") - 0) if False else 0)
                    for b4 in range(4):
                        emit_ln(s, t4 * 4 + b4, second=False)
                    emit_qkv(s, t4)
                    if dbg and s == 0 and t4 == 0:
                        emit_tap(tap_h, hbuf[0][:], 4096)
                        emit_tap(tap_q, QT[0][:], 4096)
                        emit_tap(tap_k, KT[0][:], 4096)
                        emit_tap(tap_v, VT[0][:], 1024)
                        emit_tap(tap_ab, bcsb[0][:], 256)
                    hi = t4 * 4 + 2 if t4 < NT - 1 else NBLK - 1
                    while att_next <= hi:
                        b = att_next
                        emit_attention(s, b)
                        if dbg and s == 0 and b == 0:
                            emit_tap(tap_c, ctxT[0][:], 1024)
                        emit_proj(s, b)
                        emit_ln(s, b, second=True)
                        if b % 4 == 3:
                            spill_x1(s, b // 4)
                        att_next += 1
            C.run_block("A")

        with ExitStack() as esR:
            r_ps = esR.enter_context(nc.psum_tensor("r_ps", [1, 128], F32))
            rb_ps = esR.enter_context(nc.psum_tensor("rb_ps", [128, 16], F32))

            C.wait("vector", "vh2m", C.val("vh2m"))
            C.prodw("vector", "vh2m", 1, lambda e: e.tensor_reduce(
                runmax[:, 0:1], maxbuf[:, 0:BPC * NBLK], AX.X, ALU.max))
            C.wait("tensor", "vh2m", C.val("vh2m"))
            C.prod("tensor", "tarX", 1, lambda e: e.matmul(
                r_ps[:], runmax[:, 0:1], idf[:], start=True, stop=True))
            C.wait("vector", "tarX")
            C.prodw("vector", "varX", 1, lambda e: e.tensor_reduce(
                ccrow[0:1, 0:1], r_ps[:], AX.X, ALU.max))
            C.wait("gpsimd", "varX")
            C.prod("gpsimd", "ccwX", 16, lambda e: e.dma_start(out=cc_in.ap(), in_=ccrow[:]))
            C.wait("gpsimd", "ccwX")
            C.prod("gpsimd", "arQ", 1, lambda e: e.collective_compute(
                "AllReduce", ALU.max, replica_groups=[list(range(NCORE))],
                ins=[cc_in.ap()], outs=[cc_out.ap()]))
            C.wait("gpsimd", "arQ")
            C.prod("gpsimd", "ccrX", 16, lambda e: e.dma_start(out=ccrow[:], in_=cc_out.ap()))
            C.wait("vector", "ccrX")
            C.prodw("vector", "ardX", 1, lambda e: e.reciprocal(rows[0:1, 0:1], ccrow[0:1, 0:1]))
            C.prodw("vector", "ardX", 1, lambda e: e.tensor_scalar(
                rows[0:1, 0:1], rows[0:1, 0:1], 32767.0, None, ALU.mult))
            C.wait("tensor", "ardX")
            C.prod("tensor", "trbX", 1, lambda e: e.matmul(
                rb_ps[:, 0:1], onesrf[:], rows[0:1, 0:1], start=True, stop=True))
            C.wait("vector", "trbX")
            C.prod("vector", "vrbX", 1, lambda e: e.tensor_copy(scv(SC_INVX), rb_ps[:, 0:1]))
            if dbg:
                C.wait("gpsimd", "vrbX")
                C.prod("gpsimd", "dbgsc", 16, lambda e: e.dma_start(out=scal_h, in_=ccrow[:]))
            C.run_block("AR1")

        MG = MAGIC
        with ExitStack() as esF:
            sbF = lambda n, sh, dt=F32: esF.enter_context(nc.sbuf_tensor(n, sh, dt))
            wdT = [sbF(f"wdT{h}", [128, D], BF16) for h in range(HC)]

            with ExitStack() as esB:
                sbB = lambda n, sh, dt=F32: esB.enter_context(nc.sbuf_tensor(n, sh, dt))
                Rb = [sbB(f"Rb{c}", [128, TOK], BF16) for c in range(DC)]
                wraw = [sbB(f"wraw{i}", [128, D]) for i in range(2)]
                wtmp = sbB("wtmp", [128, D])
                wq32 = sbB("wq32", [128, D])
                blkmx = sbB("blkmx", [128, 32])
                binv = sbB("binv", [128, 32])
                bscl = sbB("bscl", [128, 32])
                wstr = [sbB(f"wstr{i}", [128, DC * 128], BF16) for i in range(2)]
                qtmp = [sbB(f"qtmp{i}", [128, 1024]) for i in range(2)]
                spst = [sbB(f"spst{i}", [128, TOK], BF16) for i in range(2)]
                mmB = [esB.enter_context(nc.psum_tensor(f"mmB{i}", [128, 512], F32))
                       for i in range(2)]
                wtps = esB.enter_context(nc.psum_tensor("wtps", [128, 1024], F32))

                for _eng in Ctx.ENGINES:
                    for nm in ("x1sp0", "x1sp1", "h2sp0", "h2sp1"):
                        C.wait(_eng, nm)
                    C.wait(_eng, "vrbX")

                h2in = [esB.enter_context(nc.sbuf_tensor(f"h2in{i}", [128, 1024], BF16))
                        for i in range(2)]
                for dc in range(DC):
                    for half in range(4):
                        hw = 1024
                        r2 = (dc * 4 + half) % 2
                        C.wait("sync", "b0q", max(0, C.val("b0q") - 1))
                        C.prod("sync", f"ldA{r2}", 16, lambda e, dc=dc, half=half, r2=r2:
                               e.dma_start(out=h2in[r2][:],
                                           in_=h2_h[dc, :, half * hw:(half + 1) * hw]))
                        C.wait("scalar", f"ldA{r2}")
                        C.wait("scalar", "b0q", max(0, C.val("b0q") - 1))
                        C.prod("scalar", "b0s", 1, lambda e, r2=r2: e.activation(
                            qtmp[r2][:], h2in[r2][:], AF.Identity, bias=mgc[:, 0:1], scale=scv(SC_INVX)))
                        C.wait("vector", "b0s")
                        C.prod("vector", "b0q", 1, lambda e, dc=dc, half=half, r2=r2:
                               e.tensor_scalar(Rb[dc][:, half * hw:(half + 1) * hw],
                                               qtmp[r2][:], -MG, None, ALU.add))
                C.wait("tensor", "b0q", C.val("b0q"))
                bqm_marks = []
                bdq_marks = []
                for wi, (wsrc, wdst, rmcol) in enumerate(
                        ((w_up, upr_h, 1), (w_gate, gtr_h, 2))):
                    for hc in range(HC):
                        wsl = (wi * HC + hc) % 2
                        C.wait("sync", "bdq", bdq_marks[-2] if len(bdq_marks) >= 2 else 0)
                        C.prod("sync", f"ldA{wsl}", 16, lambda e, wsrc=wsrc, hc=hc, wsl=wsl:
                               e.dma_start(out=wraw[wsl][:], in_=wsrc[hc * 128:(hc + 1) * 128, :]))
                        C.wait("vector", f"ldA{wsl}")
                        C.prodw("vector", "bdq", 1, lambda e, wsl=wsl: e.tensor_reduce(
                            blkmx[:], wraw[wsl][:].rearrange("p (a b) -> p a b", b=32),
                            AX.X, ALU.max, apply_absolute_value=True))
                        C.prodw("vector", "bdq", 1, lambda e: e.tensor_scalar(
                            bscl[:], blkmx[:], 1.0 / 7.0, 1e-12, ALU.mult, ALU.max))
                        C.prodw("vector", "bdq", 1, lambda e: e.reciprocal(binv[:], bscl[:]))
                        C.prod("vector", "bdq", 1, lambda e, wsl=wsl: e.tensor_tensor(
                            wtmp[:].rearrange("p (a b) -> p a b", b=32),
                            wraw[wsl][:].rearrange("p (a b) -> p a b", b=32),
                            binv[:].unsqueeze(2).broadcast_to([128, 32, 32]), ALU.mult))
                        bdq_marks.append(C.val("bdq"))
                        C.prod("vector", "bdq", 1, lambda e: e.tensor_scalar(
                            wtmp[:], wtmp[:], MG, -MG, ALU.add, ALU.add))
                        C.prod("vector", "bdq", 1, lambda e: e.tensor_tensor(
                            wq32[:].rearrange("p (a b) -> p a b", b=32),
                            wtmp[:].rearrange("p (a b) -> p a b", b=32),
                            bscl[:].unsqueeze(2).broadcast_to([128, 32, 32]), ALU.mult))
                        C.wait("tensor", "bdq")
                        C.wait("tensor", "bst", C.val("bst"))
                        for dc in range(DC):
                            C.prod("tensor", "btr", 1, lambda e, dc=dc: e.transpose(
                                wtps[:, dc * 128:(dc + 1) * 128], wq32[:, dc * 128:(dc + 1) * 128],
                                idf[:]))
                        C.wait("scalar", "btr")
                        C.wait("scalar", "bqm", bqm_marks[-2] if len(bqm_marks) >= 2 else 0)
                        C.prod("scalar", "bst", 1, lambda e, wsl=wsl: e.copy(
                            wstr[wsl][:], wtps[:]))
                        C.wait("tensor", "bst", C.val("bst"))
                        ssl = C.ctr("bse") % 2
                        C.ctr_inc("bse")
                        C.wait("scalar", f"sd{ssl}", C.val(f"sd{ssl}"))
                        for tt in range(NTT):
                            bank = C.counts["mmuse"] % 2
                            C.wait("tensor", f"mmfree{bank}", C.val(f"mmfree{bank}"))
                            C.counts["mmuse"] += 1
                            for dc in range(DC):
                                C.prod("tensor", "bqm", 1, lambda e, dc=dc, tt=tt, bank=bank,
                                       wsl=wsl: e.matmul(
                                           mmB[bank][:],
                                           wstr[wsl][:, dc * 128:(dc + 1) * 128],
                                           Rb[dc][:, tt * 512:(tt + 1) * 512],
                                           start=(dc == 0), stop=(dc == DC - 1)))
                            C.wait("scalar", "bqm")
                            C.prod("scalar", f"mmfree{bank}", 1, lambda e, bank=bank, ssl=ssl,
                                   tt=tt: e.copy(spst[ssl][:, tt * 512:(tt + 1) * 512],
                                                 mmB[bank][:]))
                            C.wait("vector", f"mmfree{bank}")
                            bcol = wi * 192 + hc * 8 + tt
                            C.prod("vector", "bsmx0", 1, lambda e, ssl=ssl, tt=tt, bcol=bcol:
                                   e.tensor_reduce(maxbuf[:, bcol:bcol + 1],
                                                   spst[ssl][:, tt * 512:(tt + 1) * 512],
                                                   AX.X, ALU.max, apply_absolute_value=True))
                        C.wait("gpsimd", "mmfree0", C.val("mmfree0"))
                        C.wait("gpsimd", "mmfree1", C.val(f"mmfree1"))
                        C.prod("gpsimd", f"sd{ssl}", 16, lambda e, wdst=wdst, hc=hc, ssl=ssl:
                               e.dma_start(out=wdst[hc, :, :], in_=spst[ssl][:]))
                        bqm_marks.append(C.val("bqm"))
            with ExitStack() as esR2:
                r_ps2 = esR2.enter_context(nc.psum_tensor("r_ps2", [1, 128], F32))
                rb_ps2 = esR2.enter_context(nc.psum_tensor("rb_ps2", [128, 16], F32))
                C.wait("vector", "bsmx0", C.val("bsmx0"))
                C.prod("vector", "bvmx", 1, lambda e: e.tensor_reduce(
                    runmax[:, 1:2], maxbuf[:, 0:192], AX.X, ALU.max))
                C.prodw("vector", "bvmx", 1, lambda e: e.tensor_reduce(
                    runmax[:, 2:3], maxbuf[:, 192:384], AX.X, ALU.max))
                C.prod("vector", "bvmx", 1, lambda e: e.memset(maxbuf[:], 0.0))
                C.wait("tensor", "bvmx", C.val("bvmx"))
                for j, col in enumerate((1, 2)):
                    C.wait("tensor", "varX", C.val("varX"))
                    C.prod("tensor", "tarX", 1, lambda e, col=col: e.matmul(
                        r_ps2[:], runmax[:, col:col + 1], idf[:], start=True, stop=True))
                    C.wait("vector", "tarX")
                    C.prodw("vector", "varX", 1, lambda e, j=j: e.tensor_reduce(
                        ccrow[0:1, j:j + 1], r_ps2[:], AX.X, ALU.max))
                C.wait("gpsimd", "varX")
                C.prod("gpsimd", "ccwX", 16, lambda e: e.dma_start(out=cc_in.ap(), in_=ccrow[:]))
                C.wait("gpsimd", "ccwX")
                C.prod("gpsimd", "arQ", 1, lambda e: e.collective_compute(
                    "AllReduce", ALU.max, replica_groups=[list(range(NCORE))],
                    ins=[cc_in.ap()], outs=[cc_out.ap()]))
                C.wait("gpsimd", "arQ")
                C.prod("gpsimd", "ccrX", 16, lambda e: e.dma_start(out=ccrow[:], in_=cc_out.ap()))
                C.wait("vector", "ccrX")
                C.prodw("vector", "ardX", 1, lambda e: e.reciprocal(rows[0:1, 21:22], scal[0:1, SC_INVX:SC_INVX + 1]))
                C.prodw("vector", "ardX", 1, lambda e: e.tensor_tensor(
                    rows[0:1, 16:17], ccrow[0:1, 0:1], rows[0:1, 21:22], ALU.mult))
                C.prodw("vector", "ardX", 1, lambda e: e.tensor_scalar(
                    rows[0:1, 16:17], rows[0:1, 16:17], 1.0 / 32767.0, None, ALU.mult))
                C.prodw("vector", "ardX", 1, lambda e: e.tensor_tensor(
                    rows[0:1, 17:18], ccrow[0:1, 1:2], rows[0:1, 21:22], ALU.mult))
                C.prodw("vector", "ardX", 1, lambda e: e.tensor_scalar(
                    rows[0:1, 17:18], rows[0:1, 17:18], 1.0 / 32767.0, None, ALU.mult))
                C.prodw("vector", "ardX", 1, lambda e: e.reciprocal(rows[0:1, 18:19], ccrow[0:1, 1:2]))
                C.prodw("vector", "ardX", 1, lambda e: e.tensor_scalar(
                    rows[0:1, 18:19], rows[0:1, 18:19], 32767.0, None, ALU.mult))
                C.prodw("vector", "ardX", 1, lambda e: e.reciprocal(rows[0:1, 19:20], ccrow[0:1, 0:1]))
                C.prodw("vector", "ardX", 1, lambda e: e.tensor_scalar(
                    rows[0:1, 19:20], rows[0:1, 19:20], 32767.0, None, ALU.mult))
                C.prodw("vector", "ardX", 1, lambda e: e.tensor_scalar(
                    rows[0:1, 20:21], rows[0:1, 17:18], -MG, None, ALU.mult))
                C.wait("tensor", "ardX")
                C.prod("tensor", "trbX", 1, lambda e: e.matmul(
                    rb_ps2[:, 0:4], onesrf[:], rows[0:1, 17:21], start=True, stop=True))
                C.wait("vector", "trbX")
                C.prod("vector", "vrbX", 1, lambda e: e.tensor_copy(scal[:, SC_SGT:SC_SGT + 1], rb_ps2[:, 0:1]))
                C.prod("vector", "vrbX", 1, lambda e: e.tensor_copy(scal[:, SC_KG:SC_KG + 1], rb_ps2[:, 1:2]))
                C.prod("vector", "vrbX", 1, lambda e: e.tensor_copy(scal[:, SC_KU:SC_KU + 1], rb_ps2[:, 2:3]))
                C.prodw("vector", "vrbX", 1, lambda e: e.tensor_copy(scal[:, SC_BGT:SC_BGT + 1], rb_ps2[:, 3:4]))
                C.run_block() if False else None

            with ExitStack() as esC:
                sbC = lambda n, sh, dt=F32: esC.enter_context(nc.sbuf_tensor(n, sh, dt))
                gg = [sbC(f"gg{i}", [128, 512]) for i in range(2)]
                sg = [sbC(f"sg{i}", [128, 512]) for i in range(2)]
                t2b = [sbC(f"t2b{i}", [128, 512]) for i in range(2)]
                s2c = [sbC(f"s2c{i}", [128, 512]) for i in range(2)]
                wrawC = [sbC(f"wrawC{i}", [128, HID]) for i in range(2)]
                wtmpC = sbC("wtmpC", [128, HID])
                wqC = sbC("wqC", [128, HID])
                bmxC = sbC("bmxC", [128, 96])
                bivC = sbC("bivC", [128, 96])
                bscC = sbC("bscC", [128, 96])
                wtpsC = esC.enter_context(nc.psum_tensor("wtpsC", [128, 1024], F32))

                for _eng in Ctx.ENGINES:
                    C.wait(_eng, "vrbX")
                    C.wait(_eng, "sd0"); C.wait(_eng, "sd1")
                for rc in range(DC):
                    wsl = rc % 2
                    C.wait("sync", "cdqv", max(0, C.val("cdqv") - 1))
                    C.prod("sync", f"ldA{wsl}", 16, lambda e, rc=rc, wsl=wsl: e.dma_start(
                        out=wrawC[wsl][:], in_=w_down[rc * 128:(rc + 1) * 128, :]))
                    C.wait("vector", f"ldA{wsl}")
                    C.prodw("vector", "cdq", 1, lambda e, wsl=wsl: e.tensor_reduce(
                        bmxC[:], wrawC[wsl][:].rearrange("p (a b) -> p a b", b=32),
                        AX.X, ALU.max, apply_absolute_value=True))
                    C.prodw("vector", "cdq", 1, lambda e: e.tensor_scalar(
                        bscC[:], bmxC[:], 1.0 / 7.0, 1e-12, ALU.mult, ALU.max))
                    C.prodw("vector", "cdq", 1, lambda e: e.reciprocal(bivC[:], bscC[:]))
                    C.prod("vector", "cdq", 1, lambda e, wsl=wsl: e.tensor_tensor(
                        wtmpC[:].rearrange("p (a b) -> p a b", b=32),
                        wrawC[wsl][:].rearrange("p (a b) -> p a b", b=32),
                        bivC[:].unsqueeze(2).broadcast_to([128, 96, 32]), ALU.mult))
                    C.prod("vector", "cdq", 1, lambda e: e.tensor_scalar(
                        wtmpC[:], wtmpC[:], MG, -MG, ALU.add, ALU.add))
                    C.prod("vector", "cdqv", 1, lambda e: e.tensor_tensor(
                        wqC[:].rearrange("p (a b) -> p a b", b=32),
                        wtmpC[:].rearrange("p (a b) -> p a b", b=32),
                        bscC[:].unsqueeze(2).broadcast_to([128, 96, 32]), ALU.mult))
                    C.wait("tensor", "cdqv")
                    for grp in range(HC // 8):
                        C.wait("tensor", "cst", C.val("cst"))
                        for bb in range(8):
                            hh = grp * 8 + bb
                            C.prod("tensor", "ctr", 1, lambda e, hh=hh, bb=bb: e.transpose(
                                wtpsC[:, bb * 128:(bb + 1) * 128],
                                wqC[:, hh * 128:(hh + 1) * 128], idf[:]))
                        C.wait("scalar", "ctr")
                        for bb in range(8):
                            hh = grp * 8 + bb
                            C.prod("scalar", "cst", 1, lambda e, hh=hh, bb=bb, rc=rc: e.copy(
                                wdT[hh][:, rc * 128:(rc + 1) * 128],
                                wtpsC[:, bb * 128:(bb + 1) * 128]))
                grow = [esC.enter_context(nc.sbuf_tensor(f"grow{i}", [128, TOK], BF16))
                        for i in range(2)]
                pprow = [esC.enter_context(nc.sbuf_tensor(f"pprow{i}", [128, TOK], BF16))
                        for i in range(2)]
                for hc in range(HC):
                    r2 = hc % 2
                    C.wait("sync", f"sd{r2}", C.val(f"sd{r2}"))
                    C.wait("sync", "cpp", max(0, C.val("cpp") - NTT))
                    C.prod("sync", f"ldA{r2}", 16, lambda e, hc=hc, r2=r2: e.dma_start(
                        out=grow[r2][:], in_=gtr_h[hc, :, :]))
                    C.wait("vector", f"ldA{r2}")
                    C.wait("vector", f"sd{r2}", C.val(f"sd{r2}"))
                    C.wait("scalar", f"ldA{r2}")
                    C.wait("gpsimd", f"ldA{r2}")
                    for tt in range(NTT):
                        t2_ = (hc * NTT + tt) % 2
                        gsl = grow[r2][:, tt * 512:(tt + 1) * 512]
                        psl = pprow[r2][:, tt * 512:(tt + 1) * 512]
                        C.wait("gpsimd", "csg", max(0, C.val("csg") - 1))
                        C.wait("gpsimd", "cpp0", max(0, C.val("cpp0") - 1))
                        C.prod("gpsimd", "cgg", 1, lambda e, gsl=gsl, t2_=t2_: e.tensor_scalar(
                            gg[t2_][:], gsl, scv(SC_KG), MG, ALU.mult, ALU.add))
                        C.wait("scalar", "cgg")
                        C.wait("scalar", "cs2", max(0, C.val("cs2") - 1))
                        C.prod("scalar", "csg", 1, lambda e, t2_=t2_: e.activation(
                            sg[t2_][:], gg[t2_][:], AF.Sigmoid, bias=scv(SC_BGT), scale=scv(SC_SGT)))
                        C.prod("scalar", "ct2", 1, lambda e, t2_=t2_: e.activation(
                            t2b[t2_][:], sg[t2_][:], AF.Identity, bias=mgc[:, 0:1], scale=65536.0))
                        C.wait("gpsimd", "ct2")
                        C.prod("gpsimd", "cs2", 1, lambda e, t2_=t2_: e.tensor_scalar(
                            s2c[t2_][:], t2b[t2_][:], -MG, 65535.0, ALU.add, ALU.min))
                        C.wait("vector", "cs2")
                        C.prod("vector", "cpp0", 1, lambda e, t2_=t2_, psl=psl: e.scalar_tensor_tensor(
                            psl, gg[t2_][:], -MG, s2c[t2_][:], ALU.add, ALU.mult))
                        ccol = hc * 8 + tt
                        C.prod("vector", "cpp", 1, lambda e, psl=psl, ccol=ccol: e.tensor_reduce(
                            maxbuf[:, ccol:ccol + 1], psl, AX.X, ALU.max,
                            apply_absolute_value=True))
                    C.wait("gpsimd", "cpp0", C.val("cpp0"))
                    C.prod("gpsimd", f"sd{r2}", 16, lambda e, hc=hc, r2=r2: e.dma_start(
                        out=pp_h[hc, :, :], in_=pprow[r2][:]))

            with ExitStack() as esR3:
                r_ps3 = esR3.enter_context(nc.psum_tensor("r_ps3", [1, 128], F32))
                rb_ps3 = esR3.enter_context(nc.psum_tensor("rb_ps3", [128, 16], F32))
                C.wait("vector", "cpp", C.val("cpp"))
                C.prod("vector", "cpp", 1, lambda e: e.tensor_reduce(
                    runmax[:, 3:4], maxbuf[:, 0:192], AX.X, ALU.max))
                C.prodw("vector", "cpp", 1, lambda e: e.memset(maxbuf[:, 0:192], 0.0))
                C.wait("tensor", "cpp", C.val("cpp"))
                C.prod("tensor", "tarX", 1, lambda e: e.matmul(
                    r_ps3[:], runmax[:, 3:4], idf[:], start=True, stop=True))
                C.wait("vector", "tarX")
                C.prodw("vector", "varX", 1, lambda e: e.tensor_reduce(
                    ccrow[0:1, 0:1], r_ps3[:], AX.X, ALU.max))
                C.wait("gpsimd", "varX")
                C.prod("gpsimd", "ccwX", 16, lambda e: e.dma_start(out=cc_in.ap(), in_=ccrow[:]))
                C.wait("gpsimd", "ccwX")
                C.prod("gpsimd", "arQ", 1, lambda e: e.collective_compute(
                    "AllReduce", ALU.max, replica_groups=[list(range(NCORE))],
                    ins=[cc_in.ap()], outs=[cc_out.ap()]))
                C.wait("gpsimd", "arQ")
                C.prod("gpsimd", "ccrX", 16, lambda e: e.dma_start(out=ccrow[:], in_=cc_out.ap()))
                C.wait("vector", "ccrX")
                C.prodw("vector", "ardX", 1, lambda e: e.reciprocal(rows[0:1, 23:24], ccrow[0:1, 0:1]))
                C.prodw("vector", "ardX", 1, lambda e: e.tensor_scalar(
                    rows[0:1, 23:24], rows[0:1, 23:24], 32767.0, None, ALU.mult))
                C.prodw("vector", "ardX", 1, lambda e: e.tensor_tensor(
                    rows[0:1, 22:23], ccrow[0:1, 0:1], rows[0:1, 17:18], ALU.mult))
                C.prodw("vector", "ardX", 1, lambda e: e.tensor_scalar(
                    rows[0:1, 22:23], rows[0:1, 22:23], 1.0 / (65536.0 * 32767.0), None, ALU.mult))
                C.wait("tensor", "ardX")
                C.prod("tensor", "trbX", 1, lambda e: e.matmul(
                    rb_ps3[:, 0:1], onesrf[:], rows[0:1, 23:24], start=True, stop=True))
                C.wait("vector", "trbX")
                C.prodw("vector", "vrbX", 1, lambda e: e.tensor_copy(scv(SC_K1), rb_ps3[:, 0:1]))

            with ExitStack() as esD:
                sbD = lambda n, sh, dt=F32: esD.enter_context(nc.sbuf_tensor(n, sh, dt))
                t3b = [sbD(f"t3b{i}", [128, 512]) for i in range(2)]
                tub = [sbD(f"tub{i}", [128, 512]) for i in range(2)]
                ugb = [sbD(f"ugb{i}", [128, 512]) for i in range(2)]
                for _eng in Ctx.ENGINES:
                    C.wait(_eng, "vrbX")
                    C.wait(_eng, "sd0"); C.wait(_eng, "sd1")
                prow = [esD.enter_context(nc.sbuf_tensor(f"prow{i}", [128, TOK], BF16))
                        for i in range(2)]
                urow = [esD.enter_context(nc.sbuf_tensor(f"urow{i}", [128, TOK], BF16))
                        for i in range(2)]
                oorow = [esD.enter_context(nc.sbuf_tensor(f"oorow{i}", [128, TOK], BF16))
                         for i in range(2)]
                for hc in range(HC):
                    r2 = hc % 2
                    C.wait("sync", f"sd{r2}", C.val(f"sd{r2}"))
                    C.wait("sync", "doo", max(0, C.val("doo") - NTT))
                    C.prod("sync", f"ldA{r2}", 16, lambda e, hc=hc, r2=r2: e.dma_start(
                        out=prow[r2][:], in_=pp_h[hc, :, :]))
                    C.prod("sync", f"ldA{r2}", 16, lambda e, hc=hc, r2=r2: e.dma_start(
                        out=urow[r2][:], in_=upr_h[hc, :, :]))
                    C.wait("scalar", f"ldA{r2}")
                    C.wait("gpsimd", f"ldA{r2}")
                    C.wait("vector", f"sd{r2}", C.val(f"sd{r2}"))
                    for tt in range(NTT):
                        t2_ = (hc * NTT + tt) % 2
                        psl = prow[r2][:, tt * 512:(tt + 1) * 512]
                        usl = urow[r2][:, tt * 512:(tt + 1) * 512]
                        osl_ = oorow[r2][:, tt * 512:(tt + 1) * 512]
                        C.wait("scalar", "doo0", max(0, C.val("doo0") - 1))
                        C.prod("scalar", "dt3", 1, lambda e, t2_=t2_, psl=psl: e.activation(
                            t3b[t2_][:], psl, AF.Identity, bias=mgc[:, 0:1], scale=scv(SC_K1)))
                        C.wait("gpsimd", "doo0", max(0, C.val("doo0") - 1))
                        C.prod("gpsimd", "dtu", 1, lambda e, t2_=t2_, usl=usl: e.tensor_scalar(
                            tub[t2_][:], usl, scv(SC_KU), MG, ALU.mult, ALU.add))
                        C.prod("gpsimd", "dug", 1, lambda e, t2_=t2_: e.tensor_scalar(
                            ugb[t2_][:], tub[t2_][:], -MG, None, ALU.add))
                        C.wait("vector", "dug")
                        C.wait("vector", "dt3")
                        C.prod("vector", "doo0", 1, lambda e, t2_=t2_, osl_=osl_: e.scalar_tensor_tensor(
                            osl_, t3b[t2_][:], -MG, ugb[t2_][:], ALU.add, ALU.mult))
                        dcol = hc * 8 + tt
                        C.prod("vector", "doo", 1, lambda e, osl_=osl_, dcol=dcol: e.tensor_reduce(
                            maxbuf[:, dcol:dcol + 1], osl_, AX.X, ALU.max,
                            apply_absolute_value=True))
                    C.wait("gpsimd", "doo0", C.val("doo0"))
                    C.prod("gpsimd", f"sd{r2}", 16, lambda e, hc=hc, r2=r2: e.dma_start(
                        out=oo_h[hc, :, :], in_=oorow[r2][:]))

            with ExitStack() as esR4:
                r_ps4 = esR4.enter_context(nc.psum_tensor("r_ps4", [1, 128], F32))
                rb_ps4 = esR4.enter_context(nc.psum_tensor("rb_ps4", [128, 16], F32))
                C.wait("vector", "doo", C.val("doo"))
                C.prodw("vector", "doo", 1, lambda e: e.tensor_reduce(
                    runmax[:, 4:5], maxbuf[:, 0:192], AX.X, ALU.max))
                C.wait("tensor", "doo", C.val("doo"))
                C.prod("tensor", "tarX", 1, lambda e: e.matmul(
                    r_ps4[:], runmax[:, 4:5], idf[:], start=True, stop=True))
                C.wait("vector", "tarX")
                C.prodw("vector", "varX", 1, lambda e: e.tensor_reduce(
                    ccrow[0:1, 0:1], r_ps4[:], AX.X, ALU.max))
                C.wait("gpsimd", "varX")
                C.prod("gpsimd", "ccwX", 16, lambda e: e.dma_start(out=cc_in.ap(), in_=ccrow[:]))
                C.wait("gpsimd", "ccwX")
                C.prod("gpsimd", "arQ", 1, lambda e: e.collective_compute(
                    "AllReduce", ALU.max, replica_groups=[list(range(NCORE))],
                    ins=[cc_in.ap()], outs=[cc_out.ap()]))
                C.wait("gpsimd", "arQ")
                C.prod("gpsimd", "ccrX", 16, lambda e: e.dma_start(out=ccrow[:], in_=cc_out.ap()))
                C.wait("vector", "ccrX")
                C.prodw("vector", "ardX", 1, lambda e: e.reciprocal(rows[0:1, 24:25], ccrow[0:1, 0:1]))
                C.prodw("vector", "ardX", 1, lambda e: e.tensor_scalar(
                    rows[0:1, 24:25], rows[0:1, 24:25], 32767.0, None, ALU.mult))
                C.prodw("vector", "ardX", 1, lambda e: e.tensor_tensor(
                    rows[0:1, 25:26], ccrow[0:1, 0:1], rows[0:1, 22:23], ALU.mult))
                C.prodw("vector", "ardX", 1, lambda e: e.tensor_tensor(
                    rows[0:1, 25:26], rows[0:1, 25:26], rows[0:1, 16:17], ALU.mult))
                C.prodw("vector", "ardX", 1, lambda e: e.tensor_scalar(
                    rows[0:1, 25:26], rows[0:1, 25:26], 1.0 / 32767.0, None, ALU.mult))
                C.wait("tensor", "ardX")
                C.prod("tensor", "trbX", 1, lambda e: e.matmul(
                    rb_ps4[:, 0:2], onesrf[:], rows[0:1, 24:26], start=True, stop=True))
                C.wait("vector", "trbX")
                C.prod("vector", "vrbX", 1, lambda e: e.tensor_copy(scv(SC_K2), rb_ps4[:, 0:1]))
                C.prodw("vector", "vrbX", 1, lambda e: e.tensor_copy(scv(SC_SO), rb_ps4[:, 1:2]))
                C.prodw("vector", "vrbX", 1, lambda e: e.tensor_scalar(
                    modv[:, MV_GB * 16: MV_GB * 16 + 16],
                    modv[:, MV_GB * 16: MV_GB * 16 + 16], scv(SC_SO), None, ALU.mult))

            with ExitStack() as esE:
                sbE = lambda n, sh, dt=F32: esE.enter_context(nc.sbuf_tensor(n, sh, dt))
                oin = [sbE(f"oin{i}", [128, 512], BF16) for i in range(2)]
                t4b = [sbE(f"t4b{i}", [128, 512]) for i in range(2)]
                Ob = [sbE(f"Ob{i}", [128, 512], BF16) for i in range(HC)]
                x1t = [sbE(f"x1t{i}", [128, DC * 512]) for i in range(2)]
                outb = [sbE(f"outb{i}", [128, 512]) for i in range(4)]
                mmE = [esE.enter_context(nc.psum_tensor(f"mmE{i}", [128, 512], F32))
                       for i in range(2)]
                for _eng in Ctx.ENGINES:
                    C.wait(_eng, "vrbX")
                    C.wait(_eng, "sd0"); C.wait(_eng, "sd1")
                for tt in range(NTT):
                    s = tt // NT
                    xs = tt % 2
                    C.wait("sync", "mmfree0", C.val("mmfree0"))
                    C.wait("sync", "mmfree1", C.val("mmfree1"))
                    for dc in range(DC):
                        C.prod("sync", f"ex1{xs}", 16, lambda e, dc=dc, tt=tt, xs=xs: e.dma_start(
                            out=x1t[xs][:, dc * 512:(dc + 1) * 512],
                            in_=x1_h[dc, :, tt * 512:(tt + 1) * 512]))
                    for hc in range(HC):
                        r2 = (tt * HC + hc) % 2
                        C.wait("sync", "eO", max(0, C.val("eO") - 1))
                        C.prod("sync", f"ldA{r2}", 16, lambda e, hc=hc, tt=tt, r2=r2: e.dma_start(
                            out=oin[r2][:], in_=oo_h[hc, :, tt * 512:(tt + 1) * 512]))
                        C.wait("vector", f"ldA{r2}")
                        C.wait("vector", "eO", max(0, C.val("eO") - 1))
                        C.wait("vector", "emm", C.val("emm") - DC * NTT if C.val("emm") >= DC * NTT else 0)
                        C.prod("vector", "et4", 1, lambda e, r2=r2: e.tensor_scalar(
                            t4b[r2][:], oin[r2][:], scv(SC_K2), MG, ALU.mult, ALU.add))
                        C.wait("gpsimd", "et4")
                        C.wait("gpsimd", "emm", C.val("emm"))
                        C.prod("gpsimd", "eO", 1, lambda e, hc=hc, r2=r2: e.tensor_scalar(
                            Ob[hc][:], t4b[r2][:], -MG, None, ALU.add))
                    C.wait("tensor", "eO", C.val("eO"))
                    C.wait("tensor", "cst", C.val("cst"))
                    C.wait("tensor", f"ex1{xs}", C.val(f"ex1{xs}"))
                    for dc in range(DC):
                        bank = C.counts["mmuse"] % 2
                        C.wait("tensor", f"mmfree{bank}", C.val(f"mmfree{bank}"))
                        C.counts["mmuse"] += 1
                        for hc in range(HC):
                            C.prod("tensor", "emm", 1, lambda e, dc=dc, hc=hc, bank=bank:
                                   e.matmul(mmE[bank][:], wdT[hc][:, dc * 128:(dc + 1) * 128],
                                            Ob[hc][:], start=(hc == 0), stop=(hc == HC - 1)))
                        osl = C.ctr("eev") % 4
                        C.wait("vector", "emm")
                        C.wait("vector", f"sd{osl}", C.val(f"sd{osl}"))
                        C.wait("vector", f"ex1{xs}", C.val(f"ex1{xs}"))
                        C.prod("vector", f"mmfree{bank}", 1, lambda e, dc=dc, s=s, xs=xs, osl=osl,
                               bank=bank: e.scalar_tensor_tensor(
                                   outb[osl][:], mmE[bank][:], mv(MV_GB, dc, s),
                                   x1t[xs][:, dc * 512:(dc + 1) * 512], ALU.mult, ALU.add))
                        C.ctr_inc("eev")
                        C.wait("gpsimd", f"mmfree{bank}", C.val(f"mmfree{bank}"))
                        C.prod("gpsimd", f"sd{osl}", 16, lambda e, dc=dc, tt=tt, osl=osl:
                               e.dma_start(out=out_fm[dc, :, tt * 512:(tt + 1) * 512],
                                           in_=outb[osl][:]))
                for i in range(4):
                    C.wait("gpsimd", f"sd{i}", C.val(f"sd{i}"))
                C.run_block("BE")

    return nc



def _prep_inputs(inputs):
    f32 = np.float32
    bf16 = ml_dtypes.bfloat16
    x = np.asarray(inputs["L0_attn_x"], f32)
    b_in = np.asarray(inputs["L0_attn_b_in"], f32)
    b_out = np.asarray(inputs["L0_attn_b_out"], f32)
    w_inT = np.ascontiguousarray(np.asarray(inputs["L0_attn_w_in"], f32).T)
    w_outT = np.ascontiguousarray(np.asarray(inputs["L0_attn_w_out"], f32).T)
    w_gate = np.asarray(inputs["L0_ffn_w_gate"], f32)
    w_up = np.asarray(inputs["L0_ffn_w_up"], f32)
    w_down = np.asarray(inputs["L0_ffn_w_down"], f32)
    c = np.asarray(inputs["L0_c"], f32)
    wmod = np.asarray(inputs["L0_wmod"], f32)

    ident_f = np.eye(128, dtype=f32)
    ident_bf = ident_f.astype(bf16)
    ones_r_f = np.ones((1, 128), f32)
    ones_r_bf = np.ones((1, 512), f32).astype(bf16)
    invD_c = np.full((128, 1), 1.0 / D, f32).astype(bf16)

    q = np.arange(128)[:, None]
    j = np.arange(384)[None, :]
    m_int = np.where((j >= q) & (j <= q + 256), 1.0, 0.0).astype(f32)
    m_first = np.where(np.abs(q - j) <= 128, 1.0, 0.0).astype(f32)
    m_last = np.where(j >= q + 128, 1.0, 0.0).astype(f32)
    masks = np.concatenate([m_int, m_first, m_last], axis=1).astype(bf16)

    in_maps = []
    for core in range(NCORE):
        sl = slice(core * BPC, (core + 1) * BPC)
        x_fmc = np.ascontiguousarray(
            x[sl].reshape(BPC * T, DC, 128).transpose(1, 2, 0))
        c_fmc = np.ascontiguousarray(c[sl].T.reshape(DC, 128, BPC))
        in_maps.append({
            "x_fm": x_fmc, "w_inT": w_inT, "b_in": b_in, "w_outT": w_outT,
            "b_out": b_out, "w_gate": w_gate, "w_up": w_up, "w_down": w_down,
            "c_fm": c_fmc, "wmod": wmod,
            "ident_bf": ident_bf, "ident_f": ident_f, "masks": masks,
            "ones_r_f": ones_r_f, "ones_r_bf": ones_r_bf, "invD_c_bf": invD_c,
        })
    return in_maps


def kernel(**inputs):
    in_maps = _prep_inputs(inputs)
    nc = build_nc(False)
    res = run_bass_kernel_spmd(nc, in_maps, core_ids=list(range(NCORE)))
    out = np.empty((B, T, D), np.float32)
    for core in range(NCORE):
        o = res.results[core]["out_fm"]
        out[core * BPC:(core + 1) * BPC] = o.transpose(2, 0, 1).reshape(BPC, T, D)
    return out


# version 10
# speedup vs baseline: 1.4717x; 1.4717x over previous
import numpy as np
import ml_dtypes
from contextlib import ExitStack

import concourse.bass as bass
import concourse.mybir as mybir
from concourse.bass_utils import run_bass_kernel_spmd

F32 = mybir.dt.float32
BF16 = mybir.dt.bfloat16
AF = mybir.ActivationFunctionType
ALU = mybir.AluOpType
AX = mybir.AxisListType

MAGIC = 12582912.0
NCORE = 8
B, T, D, HID = 16, 2048, 1024, 3072
NH, HDIM = 16, 64
BPC = B // NCORE
NBLK = T // 128
NT = T // 512
DC = D // 128
HC = HID // 128
TOK = BPC * T
NTT = TOK // 512


class Ctx:
    ENGINES = ("sync", "gpsimd", "vector", "scalar", "tensor")

    def __init__(self, nc, es):
        self.nc, self.es = nc, es
        self.sems, self.counts = {}, {}
        self.plan = {e: [] for e in self.ENGINES}

    def sem(self, name):
        if name not in self.sems:
            self.sems[name] = self.es.enter_context(self.nc.semaphore(name))
            self.counts[name] = 0
        return self.sems[name]

    def val(self, name):
        self.sem(name)
        return self.counts[name]

    def ctr(self, name):
        return self.counts.setdefault("#" + name, 0)

    def ctr_inc(self, name):
        self.counts["#" + name] = self.ctr(name) + 1

    def emit(self, eng, fn):
        self.plan[eng].append(fn)

    def wait(self, eng, name, value=None):
        h = self.sem(name)
        v = self.counts[name] if value is None else value
        if v > 0:
            self.emit(eng, lambda e, h=h, v=v: e.wait_ge(h, v))

    def prod(self, eng, name, k, fn):
        h = self.sem(name)
        self.counts[name] += k
        self.emit(eng, lambda e, h=h, k=k, fn=fn: fn(e).then_inc(h, k))

    def prodw(self, eng, name, k, fn):
        self.prod(eng, name, k, fn)
        self.wait(eng, name)

    def run_block(self, label=None):
        plan, self.plan = self.plan, {e: [] for e in self.ENGINES}
        import contextlib
        scope = self.nc.named_scope(label) if label else contextlib.nullcontext()
        with scope, self.nc.Block() as blk:
            blk.sync(lambda e: [f(e) for f in plan["sync"]])
            blk.gpsimd(lambda e: [f(e) for f in plan["gpsimd"]])
            blk.vector(lambda e: [f(e) for f in plan["vector"]])
            blk.scalar(lambda e: [f(e) for f in plan["scalar"]])
            blk.tensor(lambda e: [f(e) for f in plan["tensor"]])


def build_nc(dbg=False):
    nc = bass.Bass("TRN2", target_bir_lowering=False, detect_race_conditions=False)

    x_fm = nc.dram_tensor("x_fm", [DC, 128, TOK], F32, kind="ExternalInput").ap()
    w_inT = nc.dram_tensor("w_inT", [D, 3 * D], F32, kind="ExternalInput").ap()
    b_in = nc.dram_tensor("b_in", [3 * D], F32, kind="ExternalInput").ap()
    w_outT = nc.dram_tensor("w_outT", [D, D], F32, kind="ExternalInput").ap()
    b_out = nc.dram_tensor("b_out", [D], F32, kind="ExternalInput").ap()
    w_gate = nc.dram_tensor("w_gate", [HID, D], F32, kind="ExternalInput").ap()
    w_up = nc.dram_tensor("w_up", [HID, D], F32, kind="ExternalInput").ap()
    w_down = nc.dram_tensor("w_down", [D, HID], F32, kind="ExternalInput").ap()
    c_fm = nc.dram_tensor("c_fm", [DC, 128, BPC], F32, kind="ExternalInput").ap()
    wmod = nc.dram_tensor("wmod", [D, 6 * D], F32, kind="ExternalInput").ap()
    ident_bf = nc.dram_tensor("ident_bf", [128, 128], BF16, kind="ExternalInput").ap()
    ident_f = nc.dram_tensor("ident_f", [128, 128], F32, kind="ExternalInput").ap()
    masks = nc.dram_tensor("masks", [128, 3 * 384], BF16, kind="ExternalInput").ap()
    ones_r_f = nc.dram_tensor("ones_r_f", [1, 128], F32, kind="ExternalInput").ap()
    ones_r_bf = nc.dram_tensor("ones_r_bf", [1, 512], BF16, kind="ExternalInput").ap()
    invD_c_bf = nc.dram_tensor("invD_c_bf", [128, 1], BF16, kind="ExternalInput").ap()

    out_fm = nc.dram_tensor("out_fm", [DC, 128, TOK], F32, kind="ExternalOutput").ap()

    skind = "ExternalOutput" if dbg else "Internal"
    x1_h = nc.dram_tensor("x1_h", [DC, 128, TOK], F32, kind=skind).ap()
    h2_h = nc.dram_tensor("h2_h", [DC, 128, TOK], BF16, kind=skind).ap()
    upr_h = nc.dram_tensor("upr_h", [HC, 128, TOK], BF16, kind=skind).ap()
    gtr_h = nc.dram_tensor("gtr_h", [HC, 128, TOK], BF16, kind=skind).ap()
    pp_h = nc.dram_tensor("pp_h", [HC, 128, TOK], BF16, kind=skind).ap()
    oo_h = nc.dram_tensor("oo_h", [HC, 128, TOK], BF16, kind=skind).ap()
    scal_h = (nc.dram_tensor("scal_h", [1, 16], F32, kind="ExternalOutput").ap()
              if dbg else None)
    if dbg:
        tap_h = nc.dram_tensor("tap_h", [128, 4096], BF16, kind="ExternalOutput").ap()
        tap_q = nc.dram_tensor("tap_q", [128, 4096], BF16, kind="ExternalOutput").ap()
        tap_k = nc.dram_tensor("tap_k", [128, 4096], BF16, kind="ExternalOutput").ap()
        tap_v = nc.dram_tensor("tap_v", [128, 1024], BF16, kind="ExternalOutput").ap()
        tap_c = nc.dram_tensor("tap_c", [128, 1024], BF16, kind="ExternalOutput").ap()
        tap_ab = nc.dram_tensor("tap_ab", [128, 256], F32, kind="ExternalOutput").ap()
        tap_e = nc.dram_tensor("tap_e", [128, 384], BF16, kind="ExternalOutput").ap()
        tap_at = nc.dram_tensor("tap_at", [128, 384], BF16, kind="ExternalOutput").ap()
        tap_d = nc.dram_tensor("tap_d", [128, 128], BF16, kind="ExternalOutput").ap()
        tap_sm = nc.dram_tensor("tap_sm", [128, 4], F32, kind="ExternalOutput").ap()
    cc_in = nc.dram_tensor("cc_in", [1, 16], F32, kind="Internal")
    cc_out = nc.dram_tensor("cc_out", [1, 16], F32, kind="Internal")

    es = ExitStack()
    with es:
        C = Ctx(nc, es)
        sbuf = lambda n, s, dt=F32: es.enter_context(nc.sbuf_tensor(n, s, dt))

        idbf = sbuf("idbf", [128, 128], BF16)
        idf = sbuf("idf", [128, 128], F32)
        msk = sbuf("msk", [128, 3 * 384], BF16)
        onesrf = sbuf("onesrf", [1, 128], F32)
        onesrbf = sbuf("onesrbf", [1, 512], BF16)
        invDc = sbuf("invDc", [128, 1], BF16)
        bq_col = sbuf("bq_col", [128, 8], F32)
        bk_col = sbuf("bk_col", [128, 8], F32)
        bv_row = sbuf("bv_row", [1, 1024], BF16)
        bo_row = sbuf("bo_row", [1, 1024], BF16)
        modv = sbuf("modv", [128, 6 * 16], F32)
        scal = sbuf("scal", [128, 16], F32)
        runmax = sbuf("runmax", [128, 8], F32)
        rows = sbuf("rows", [1, 512], F32)
        ccrow = sbuf("ccrow", [1, 16], F32)
        gascr = sbuf("gascr", [128, 4], F32)
        mgc = sbuf("mgc", [128, 2], F32)
        maxbuf = sbuf("maxbuf", [128, 384], F32)

        MV_ONEPCA, MV_SA, MV_GA, MV_ONEPCB, MV_SB, MV_GB = range(6)

        def mv(which, c, s):
            j = which * 16 + c * BPC + s
            return modv[:, j: j + 1]

        SC_INVX, SC_KG, SC_SGT, SC_BGT, SC_KU, SC_K1, SC_K2, SC_SO = range(8)
        scv = lambda j: scal[:, j: j + 1]

        with ExitStack() as es0:
            sb0 = lambda n, s, dt=F32: es0.enter_context(nc.sbuf_tensor(n, s, dt))
            cfm = sb0("cfm", [128, DC * BPC])
            silc = sb0("silc", [128, DC * BPC])
            wmt = sb0("wmt", [128, 2 * 512])
            modr = sb0("modr", [BPC, 6 * D])
            bcol_raw = sb0("bcol_raw", [128, 16])
            bvr_f = sb0("bvr_f", [1, 1024])
            bor_f = sb0("bor_f", [1, 1024])
            mps = es0.enter_context(nc.psum_tensor("mps", [BPC, 512], F32))
            tps = es0.enter_context(nc.psum_tensor("tps", [128, 96], F32))

            loads = [(idbf, ident_bf), (idf, ident_f), (msk, masks),
                     (onesrf, ones_r_f), (onesrbf, ones_r_bf), (invDc, invD_c_bf),
                     (bvr_f, b_in[2 * D: 3 * D].unsqueeze(0)),
                     (bor_f, b_out.unsqueeze(0))]
            for dst, src in loads:
                C.prod("sync", "p0ld", 16, lambda e, d=dst, s=src: e.dma_start(out=d[:], in_=s))
            C.prod("sync", "p0ld", 16, lambda e: e.dma_start(
                out=cfm[:].rearrange("p (c s) -> p c s", s=BPC),
                in_=c_fm.rearrange("c p s -> p c s"),
                allow_slow_non_contiguous=True))
            C.prod("sync", "p0ld", 16, lambda e: e.dma_start(
                out=bcol_raw[:], in_=b_in[0: 2 * D].rearrange("(c p) -> p c", p=128),
                allow_slow_non_contiguous=True))

            C.wait("vector", "p0ld")
            C.prod("vector", "p0v", 1, lambda e: e.tensor_copy(bv_row[:], bvr_f[:]))
            C.prod("vector", "p0v", 1, lambda e: e.tensor_copy(bo_row[:], bor_f[:]))
            C.prod("vector", "p0v", 1, lambda e: e.tensor_scalar(
                bq_col[:], bcol_raw[:, 0:8], 0.125, None, ALU.mult))
            C.prod("vector", "p0v", 1, lambda e: e.tensor_copy(bk_col[:], bcol_raw[:, 8:16]))
            C.prod("vector", "p0v", 1, lambda e: e.memset(runmax[:], 0.0))
            C.prod("vector", "p0v", 1, lambda e: e.memset(ccrow[:], 0.0))
            C.prod("vector", "p0v", 1, lambda e: e.memset(mgc[:, 0:1], MAGIC))
            C.prod("vector", "p0v", 1, lambda e: e.memset(mgc[:, 1:2], -MAGIC))
            C.prod("vector", "p0v", 1, lambda e: e.memset(maxbuf[:], 0.0))
            C.wait("scalar", "p0ld")
            C.prod("scalar", "p0s", 1, lambda e: e.activation(silc[:], cfm[:], AF.Silu))

            C.wait("tensor", "p0s")
            for j in range(12):
                C.wait("tensor", "p0ev", j)
                for dc in range(DC):
                    i = j * DC + dc
                    slot = i % 2
                    C.wait("sync", "p0mm", max(0, i - 1))
                    C.prod("sync", f"p0wld{slot}", 16, lambda e, j=j, dc=dc, slot=slot: e.dma_start(
                        out=wmt[:, slot * 512: slot * 512 + 512],
                        in_=wmod[dc * 128:(dc + 1) * 128, j * 512:(j + 1) * 512]))
                    C.wait("tensor", f"p0wld{slot}")
                    C.prod("tensor", "p0mm", 1, lambda e, dc=dc, slot=slot: e.matmul(
                        mps[:, 0:512], silc[:, dc * BPC:(dc + 1) * BPC],
                        wmt[:, slot * 512: slot * 512 + 512],
                        start=(dc == 0), stop=(dc == DC - 1)))
                C.wait("vector", "p0mm")
                C.prod("vector", "p0ev", 1, lambda e, j=j: e.tensor_copy(
                    modr[:, j * 512:(j + 1) * 512], mps[:, 0:512]))

            C.wait("tensor", "p0ev")
            for k in range(48):
                C.prod("tensor", "p0tr", 1, lambda e, k=k: e.transpose(
                    tps[:, k * BPC:(k + 1) * BPC],
                    modr[:, k * 128:(k + 1) * 128], idf[0:BPC, 0:BPC]))
            C.wait("vector", "p0tr")
            C.wait("scalar", "p0tr")

            def tcol(v, c):
                k = v * 8 + c
                return tps[:, k * BPC:(k + 1) * BPC]

            for c in range(DC):
                C.prod("vector", "p0d", 1, lambda e, c=c: e.tensor_copy(
                    modv[:, MV_SA * 16 + c * BPC: MV_SA * 16 + (c + 1) * BPC], tcol(0, c)))
                C.prod("vector", "p0d", 1, lambda e, c=c: e.tensor_scalar(
                    modv[:, MV_ONEPCA * 16 + c * BPC: MV_ONEPCA * 16 + (c + 1) * BPC],
                    tcol(1, c), 1.0, None, ALU.add))
                C.prod("vector", "p0d", 1, lambda e, c=c: e.tensor_copy(
                    modv[:, MV_SB * 16 + c * BPC: MV_SB * 16 + (c + 1) * BPC], tcol(3, c)))
                C.prod("vector", "p0d", 1, lambda e, c=c: e.tensor_scalar(
                    modv[:, MV_ONEPCB * 16 + c * BPC: MV_ONEPCB * 16 + (c + 1) * BPC],
                    tcol(4, c), 1.0, None, ALU.add))
                for (vec, mvidx) in ((2, MV_GA), (5, MV_GB)):
                    C.prod("scalar", "p0ga", 1, lambda e, vec=vec, c=c: e.activation(
                        gascr[:, 0:BPC], tcol(vec, c), AF.Abs))
                    C.prod("scalar", "p0ga", 1, lambda e, vec=vec, c=c: e.activation(
                        gascr[:, 2:2 + BPC], tcol(vec, c), AF.Sign))
                    C.wait("vector", "p0ga")
                    C.prodw("vector", "p0gb", 1, lambda e: e.tensor_scalar(
                        gascr[:, 0:BPC], gascr[:, 0:BPC], 0.25, None, ALU.max))
                    C.prod("vector", "p0gb", 1, lambda e, mvidx=mvidx, c=c: e.tensor_tensor(
                        modv[:, mvidx * 16 + c * BPC: mvidx * 16 + (c + 1) * BPC],
                        gascr[:, 0:BPC], gascr[:, 2:2 + BPC], ALU.mult))
                    C.wait("scalar", "p0gb")
            C.run_block("P0")

        with ExitStack() as esA:
            sbA = lambda n, s, dt=F32: esA.enter_context(nc.sbuf_tensor(n, s, dt))
            winT = [sbA(f"winT{c}", [128, 3 * D], BF16) for c in range(DC)]
            woutT = [sbA(f"woutT{c}", [128, D], BF16) for c in range(DC)]
            xfm = [sbA(f"xfm{i}", [128, DC * 512]) for i in range(2)]
            hbuf = [sbA(f"hbuf{i}", [128, DC * 512], BF16) for i in range(2)]
            QT = [sbA(f"QT{i}", [128, DC * 512], BF16) for i in range(2)]
            KT = [sbA(f"KT{i}", [128, DC * 512], BF16) for i in range(2)]
            VT = [sbA(f"VT{i}", [128, 1024], BF16) for i in range(6)]
            ctxT = [sbA(f"ctxT{i}", [128, DC * 128], BF16) for i in range(2)]
            xbf = [sbA(f"xbf{i}", [128, 1024], BF16) for i in range(2)]
            xsq = [sbA(f"xsq{i}", [128, 1024], BF16) for i in range(2)]
            ubuf = [sbA(f"ubuf{i}", [128, 1024]) for i in range(2)]
            u2buf = [sbA(f"u2buf{i}", [128, 1024]) for i in range(2)]
            bcsb = [sbA(f"bcsb{i}", [128, 256]) for i in range(2)]
            h2st = [sbA(f"h2st{i}", [128, 1024], BF16) for i in range(2)]
            expb = [sbA(f"expb{i}", [128, 384], BF16) for i in range(2)]
            mexp = [sbA(f"mexp{i}", [128, 384], BF16) for i in range(2)]
            atb = [sbA(f"atb{i}", [128, 384], BF16) for i in range(2)]
            Dt = [sbA(f"Dt{i}", [128, 128], BF16) for i in range(2)]
            nmx = sbA("nmx", [128, 4])
            sume = sbA("sume", [128, 4])

            st_ps = esA.enter_context(nc.psum_tensor("st_ps", [128, 256], F32))
            mm_ps = [esA.enter_context(nc.psum_tensor(f"mm_ps{i}", [128, 512], F32))
                     for i in range(2)]
            sc_ps = [esA.enter_context(nc.psum_tensor(f"sc_ps{i}", [128, 384], F32))
                     for i in range(2)]
            at_ps = [esA.enter_context(nc.psum_tensor(f"at_ps{i}", [128, 384], F32))
                     for i in range(2)]
            cx_ps = esA.enter_context(nc.psum_tensor("cx_ps", [128, 128], F32))

            for _eng in Ctx.ENGINES:
                C.wait(_eng, "p0gb")
                C.wait(_eng, "p0ld")
                C.wait(_eng, "p0v")

            for c in range(DC):
                C.prod("gpsimd", "wina", 16, lambda e, c=c: e.dma_start(
                    out=winT[c][:], in_=w_inT[c * 128:(c + 1) * 128, :]))
            for c in range(DC):
                C.prod("gpsimd", "wouta", 16, lambda e, c=c: e.dma_start(
                    out=woutT[c][:], in_=w_outT[c * 128:(c + 1) * 128, :]))

            tok0 = lambda s, t4: s * T + t4 * 512
            h2sp_marks = []
            C.counts.setdefault("mmuse", 0)

            def acquire_mm_bank():
                bank = C.counts["mmuse"] % 2
                C.wait("tensor", f"mmfree{bank}", C.val(f"mmfree{bank}"))
                C.counts["mmuse"] += 1
                return bank

            def emit_ln(s, b, second):
                t4, b4 = b // 4, b % 4
                slot, r2 = t4 % 2, b % 2
                xin = (xfm[slot].ap()
                       .rearrange("p (c w) -> p c w", w=512)[:, :, b4 * 128:(b4 + 1) * 128])
                C.prod("vector", "vxbf", 1, lambda e, xin=xin, r2=r2: e.tensor_copy(
                    xbf[r2][:].rearrange("p (c w) -> p c w", w=128), xin))
                C.prod("scalar", "sxsq", 1, lambda e, xin=xin, r2=r2: e.activation(
                    xsq[r2][:].rearrange("p (c w) -> p c w", w=128), xin, AF.Square))
                C.wait("tensor", "vbc", C.val("vbc"))
                C.wait("tensor", "vxbf")
                C.wait("tensor", "sxsq")
                for dc in range(DC):
                    C.prod("tensor", "tst", 1, lambda e, dc=dc, r2=r2: e.matmul(
                        st_ps[0:1, 0:128], invDc[:], xbf[r2][:, dc * 128:(dc + 1) * 128],
                        start=(dc == 0), stop=False))
                for dc in range(DC):
                    C.prod("tensor", "tst", 1, lambda e, dc=dc, r2=r2: e.matmul(
                        st_ps[0:1, 128:256], invDc[:], xsq[r2][:, dc * 128:(dc + 1) * 128],
                        start=False, stop=(dc == DC - 1)))
                C.wait("vector", "tst")
                C.prodw("vector", "vst", 1, lambda e: e.tensor_copy(rows[0:1, 0:256], st_ps[0:1, 0:256]))
                C.prodw("vector", "vst", 1, lambda e: e.tensor_tensor(
                    rows[0:1, 256:384], rows[0:1, 0:128], rows[0:1, 0:128], ALU.mult))
                C.prodw("vector", "vst", 1, lambda e: e.tensor_tensor(
                    rows[0:1, 256:384], rows[0:1, 128:256], rows[0:1, 256:384], ALU.subtract))
                C.prodw("vector", "vst", 1, lambda e: e.tensor_scalar(
                    rows[0:1, 256:384], rows[0:1, 256:384], 1e-5, None, ALU.add))
                C.wait("scalar", "vst")
                C.prodw("scalar", "sst", 1, lambda e: e.activation(
                    rows[0:1, 384:512], rows[0:1, 256:384], AF.Ln))
                C.prodw("scalar", "sst", 1, lambda e: e.activation(
                    rows[0:1, 384:512], rows[0:1, 384:512], AF.Exp, scale=-0.5))
                C.wait("vector", "sst")
                C.prodw("vector", "vst2", 1, lambda e: e.tensor_tensor(
                    rows[0:1, 256:384], rows[0:1, 0:128], rows[0:1, 384:512], ALU.mult))
                C.prodw("vector", "vst2", 1, lambda e: e.tensor_scalar(
                    rows[0:1, 256:384], rows[0:1, 256:384], -1.0, None, ALU.mult))
                C.wait("tensor", "vst2")
                C.prod("tensor", "tbc", 1, lambda e: e.matmul(
                    st_ps[:, 0:128], onesrf[:], rows[0:1, 384:512], start=True, stop=False))
                C.prod("tensor", "tbc", 1, lambda e: e.matmul(
                    st_ps[:, 128:256], onesrf[:], rows[0:1, 256:384], start=False, stop=True))
                C.wait("vector", "tbc")
                C.prod("vector", "vbc", 1, lambda e, r2=r2: e.tensor_copy(bcsb[r2][:], st_ps[:]))
                C.wait("gpsimd", "vbc")
                a_bc = bcsb[r2][:, 0:128].unsqueeze(1).broadcast_to([128, 8, 128])
                b_bc = bcsb[r2][:, 128:256].unsqueeze(1).broadcast_to([128, 8, 128])
                C.prod("gpsimd", "gu", 1, lambda e, xin=xin, a_bc=a_bc, r2=r2: e.tensor_tensor(
                    ubuf[r2][:].rearrange("p (c w) -> p c w", w=128), xin, a_bc, ALU.mult))
                C.prod("gpsimd", "gu", 1, lambda e, b_bc=b_bc, r2=r2: e.tensor_tensor(
                    u2buf[r2][:].rearrange("p (c w) -> p c w", w=128),
                    ubuf[r2][:].rearrange("p (c w) -> p c w", w=128), b_bc, ALU.add))
                C.wait("vector", "gu")
                if second:
                    C.wait("vector", f"h2sp{r2}", C.val(f"h2sp{r2}"))
                for c in range(DC):
                    u2c = u2buf[r2][:, c * 128:(c + 1) * 128]
                    if not second:
                        hdst = hbuf[slot][:, c * 512 + b4 * 128: c * 512 + b4 * 128 + 128]
                        C.prod("vector", "vh", 1, lambda e, u2c=u2c, hdst=hdst, c=c, s=s:
                               e.tensor_scalar(hdst, u2c, mv(MV_ONEPCA, c, s),
                                               mv(MV_SA, c, s), ALU.mult, ALU.add))
                    else:
                        hdst = h2st[r2][:, c * 128:(c + 1) * 128]
                        C.prod("vector", "vh2", 1, lambda e, u2c=u2c, hdst=hdst, c=c, s=s:
                               e.tensor_scalar(hdst, u2c, mv(MV_ONEPCB, c, s),
                                               mv(MV_SB, c, s), ALU.mult, ALU.add))
                if not second:
                    C.wait("vector", "vh")
                    for c in range(DC):
                        hdst = hbuf[slot][:, c * 512 + b4 * 128: c * 512 + b4 * 128 + 128]
                        xdst = xfm[slot][:, c * 512 + b4 * 128: c * 512 + b4 * 128 + 128]
                        C.prod("vector", "vx1", 1, lambda e, hdst=hdst, xdst=xdst, c=c, s=s:
                               e.scalar_tensor_tensor(xdst, hdst, mv(MV_GA, c, s), xdst,
                                                      ALU.mult, ALU.add))
                if second:
                    C.wait("vector", "vh2")
                    mcol = s * NBLK + b
                    C.prod("vector", "vh2m", 1, lambda e, r2=r2, mcol=mcol: e.tensor_reduce(
                        maxbuf[:, mcol:mcol + 1], h2st[r2][:], AX.X, ALU.max,
                        apply_absolute_value=True))
                    C.wait("gpsimd", "vh2", C.val("vh2"))
                    tk = s * T + b * 128
                    for c in range(DC):
                        C.prod("gpsimd", f"h2sp{r2}", 16, lambda e, c=c, tk=tk, r2=r2: e.dma_start(
                            out=h2_h[c, :, tk: tk + 128], in_=h2st[r2][:, c * 128:(c + 1) * 128]))

            def emit_qkv(s, t4):
                slot = t4 % 2
                C.wait("tensor", "wina", 16 * DC)
                C.wait("tensor", "vh", C.val("vh"))
                for ec in range(16):
                    bank = acquire_mm_bank()
                    for dc in range(DC):
                        C.prod("tensor", "tmm", 1, lambda e, ec=ec, dc=dc, bank=bank, slot=slot:
                               e.matmul(mm_ps[bank][:],
                                        winT[dc][:, ec * 128:(ec + 1) * 128],
                                        hbuf[slot][:, dc * 512:(dc + 1) * 512],
                                        start=(dc == 0), stop=(dc == DC - 1)))
                    C.wait("scalar", "tmm")
                    if ec < 8:
                        dst = QT[slot][:, ec * 512:(ec + 1) * 512]
                        bias, scale = bq_col[:, ec:ec + 1], 0.125
                    else:
                        dst = KT[slot][:, (ec - 8) * 512:(ec - 7) * 512]
                        bias, scale = bk_col[:, ec - 8:ec - 7], 1.0
                    C.prod("scalar", f"mmfree{bank}", 1, lambda e, dst=dst, bias=bias,
                           scale=scale, bank=bank: e.activation(dst, mm_ps[bank][:], AF.Identity,
                                                                bias=bias, scale=scale))
                for b4 in range(4):
                    b = t4 * 4 + b4
                    for g in range(2):
                        bank = acquire_mm_bank()
                        for dc in range(DC):
                            C.prod("tensor", "tmm", 1, lambda e, b4=b4, g=g, dc=dc, bank=bank,
                                   slot=slot: e.matmul(
                                       mm_ps[bank][:],
                                       hbuf[slot][:, dc * 512 + b4 * 128: dc * 512 + b4 * 128 + 128],
                                       winT[dc][:, 2 * D + g * 512: 2 * D + (g + 1) * 512],
                                       start=(dc == 0), stop=False))
                        C.prod("tensor", "tmm", 1, lambda e, g=g, bank=bank: e.matmul(
                            mm_ps[bank][:], onesrbf[0:1, 0:128],
                            bv_row[0:1, g * 512:(g + 1) * 512], start=False, stop=True))
                        C.wait("scalar", "tmm")
                        C.prod("scalar", f"mmfree{bank}", 1, lambda e, b=b, g=g, bank=bank:
                               e.activation(VT[b % 6][:, g * 512:(g + 1) * 512],
                                            mm_ps[bank][:], AF.Identity))

            attmk = {}

            def emit_attention(s, b):
                t4 = b // 4
                kb0 = min(max(b - 1, 0), NBLK - 3)
                mvar = 1 if b == 0 else (2 if b == NBLK - 1 else 0)
                C.wait("tensor", "mmfree0", C.val("mmfree0"))
                C.wait("tensor", "mmfree1", C.val("mmfree1"))
                G0 = C.ctr("hctr")

                def St(h):
                    G = G0 + h
                    sl = G % 2
                    hc, hr = h // 2, (h % 2) * 64
                    if ("se", G - 2) in attmk:
                        C.wait("tensor", "sexp", attmk[("se", G - 2)])
                    qap = QT[t4 % 2][hr:hr + 64,
                                     hc * 512 + (b % 4) * 128: hc * 512 + (b % 4) * 128 + 128]
                    for kk in range(3):
                        kb = kb0 + kk
                        kap = KT[(kb // 4) % 2][hr:hr + 64,
                                                hc * 512 + (kb % 4) * 128: hc * 512 + (kb % 4) * 128 + 128]
                        C.prod("tensor", "tsc", 1, lambda e, sl=sl, kk=kk, qap=qap, kap=kap:
                               e.matmul(sc_ps[sl][:, kk * 128:(kk + 1) * 128], qap, kap,
                                        start=(kk == 0), stop=(kk == 2)))
                    attmk[("st", G)] = C.val("tsc")

                def Vn(h):
                    G = G0 + h
                    sl = G % 2
                    C.wait("vector", "tsc", attmk[("st", G)])
                    C.prod("vector", "vnm", 1, lambda e, sl=sl: e.tensor_reduce(
                        nmx[:, sl:sl + 1], sc_ps[sl][:], AX.X, ALU.max, negate=True))
                    attmk[("vn", G)] = C.val("vnm")

                def Se(h):
                    G = G0 + h
                    sl = G % 2
                    C.wait("scalar", "vnm", attmk[("vn", G)])
                    if ("ta", G - 2) in attmk:
                        C.wait("scalar", "tat", attmk[("ta", G - 2)])
                    C.prod("scalar", "sexp", 1, lambda e, sl=sl: e.activation(
                        expb[sl][:], sc_ps[sl][:], AF.Exp, bias=nmx[:, sl:sl + 1]))
                    attmk[("se", G)] = C.val("sexp")

                def Vd(h):
                    G = G0 + h
                    sl = G % 2
                    C.wait("vector", "sexp", attmk[("se", G)])
                    if dbg and s == 0 and b == 0 and h == 0:
                        C.wait("gpsimd", "sexp", attmk[("se", G)])
                        C.prod("gpsimd", "tapd", 16, lambda e, sl=sl: e.dma_start(
                            out=tap_e, in_=expb[sl][:]))
                        C.wait("gpsimd", "tapd")
                        for _e2 in ("vector", "scalar", "tensor"):
                            C.wait(_e2, "tapd")
                    if ("ta", G - 2) in attmk:
                        C.wait("vector", "tat", attmk[("ta", G - 2)])
                    C.prod("vector", "vmk", 1, lambda e, sl=sl, mvar=mvar: e.scalar_tensor_tensor(
                        mexp[sl][:], expb[sl][:], 1.0, msk[:, mvar * 384:(mvar + 1) * 384],
                        ALU.mult, ALU.mult, accum_out=sume[:, sl:sl + 1]))
                    C.wait("vector", "vmk")
                    C.prodw("vector", "vrc", 1, lambda e, sl=sl: e.reciprocal(
                        sume[:, sl + 2:sl + 3], sume[:, sl:sl + 1]))
                    C.prod("vector", "vD", 1, lambda e, sl=sl: e.tensor_scalar(
                        Dt[sl][:], idbf[:], sume[:, sl + 2:sl + 3], None, ALU.mult))
                    attmk[("vd", G)] = C.val("vD")
                    if dbg and s == 0 and b == 0 and h == 0:
                        C.wait("gpsimd", "vD", C.val("vD"))
                        C.prod("gpsimd", "tapd", 16, lambda e: e.dma_start(
                            out=tap_sm, in_=sume[:]))
                        C.prod("gpsimd", "tapd", 16, lambda e, sl=sl: e.dma_start(
                            out=tap_at, in_=mexp[sl][:]))
                        C.prod("gpsimd", "tapd", 16, lambda e, sl=sl: e.dma_start(
                            out=tap_d, in_=Dt[sl][:]))
                        C.wait("gpsimd", "tapd")
                        for _e2 in ("vector", "scalar", "tensor"):
                            C.wait(_e2, "tapd")

                def Ta(h):
                    G = G0 + h
                    sl = G % 2
                    C.wait("tensor", "vD", attmk[("vd", G)])
                    if ("va", G - 2) in attmk:
                        C.wait("tensor", "vat", attmk[("va", G - 2)])
                    for kk in range(3):
                        C.prod("tensor", "tat", 1, lambda e, sl=sl, kk=kk: e.matmul(
                            at_ps[sl][:, kk * 128:(kk + 1) * 128],
                            mexp[sl][:, kk * 128:(kk + 1) * 128], Dt[sl][:],
                            start=(kk == 0), stop=(kk == 2)))
                    attmk[("ta", G)] = C.val("tat")

                def Va(h):
                    G = G0 + h
                    sl = G % 2
                    C.wait("vector", "tat", attmk[("ta", G)])
                    if ("tc", G - 2) in attmk:
                        C.wait("vector", "tcx", attmk[("tc", G - 2)])
                    C.prod("vector", "vat", 1, lambda e, sl=sl: e.tensor_copy(
                        atb[sl][:], at_ps[sl][:]))
                    attmk[("va", G)] = C.val("vat")

                def Tc(h):
                    G = G0 + h
                    sl = G % 2
                    hr = (h % 2) * 64
                    C.wait("tensor", "vat", attmk[("va", G)])
                    if h % 2 == 0 and ("vc", G - 1) in attmk:
                        C.wait("tensor", "vcx", attmk[("vc", G - 1)])
                    for kk in range(3):
                        C.prod("tensor", "tcx", 1, lambda e, kk=kk, h=h, hr=hr, kb0=kb0, sl=sl:
                               e.matmul(cx_ps[hr:hr + 64, :],
                                        VT[(kb0 + kk) % 6][:, h * 64:(h + 1) * 64],
                                        atb[sl][:, kk * 128:(kk + 1) * 128],
                                        start=(kk == 0), stop=(kk == 2)))
                    attmk[("tc", G)] = C.val("tcx")

                def Vc(h):
                    G = G0 + h
                    C.wait("vector", "tcx", attmk[("tc", G)])
                    C.prod("vector", "vcx", 1, lambda e, h=h, b=b: e.tensor_copy(
                        ctxT[b % 2][:, (h // 2) * 128:(h // 2 + 1) * 128], cx_ps[:]))
                    attmk[("vc", G)] = C.val("vcx")
                    attmk[("vc", G - 1)] = C.val("vcx")

                for i in range(NH + 2):
                    if i < NH:
                        St(i)
                    if 0 <= i - 1 < NH:
                        Vn(i - 1)
                        Se(i - 1)
                    if 0 <= i - 2 < NH:
                        h = i - 2
                        Vd(h)
                        Ta(h)
                        Va(h)
                        Tc(h)
                        if h % 2 == 1:
                            Vc(h)
                for h in range(NH):
                    C.ctr_inc("hctr")

            def emit_proj(s, b):
                t4, b4 = b // 4, b % 4
                slot = t4 % 2
                C.wait("tensor", "wouta", 16 * DC)
                C.wait("tensor", "vcx", C.val("vcx"))
                for dc in range(DC):
                    bank = acquire_mm_bank()
                    for ec in range(DC):
                        C.prod("tensor", "tmm", 1, lambda e, dc=dc, ec=ec, bank=bank, b=b:
                               e.matmul(mm_ps[bank][:, 0:128],
                                        woutT[ec][:, dc * 128:(dc + 1) * 128],
                                        ctxT[b % 2][:, ec * 128:(ec + 1) * 128],
                                        start=(ec == 0), stop=False))
                    C.prod("tensor", "tmm", 1, lambda e, dc=dc, bank=bank: e.matmul(
                        mm_ps[bank][:, 0:128], bo_row[0:1, dc * 128:(dc + 1) * 128],
                        onesrbf[0:1, 0:128], start=False, stop=True))
                    C.wait("vector", "tmm")
                    xdst = xfm[slot][:, dc * 512 + b4 * 128: dc * 512 + b4 * 128 + 128]
                    C.prod("vector", f"mmfree{bank}", 1, lambda e, dc=dc, xdst=xdst, bank=bank,
                           s=s: e.scalar_tensor_tensor(xdst, mm_ps[bank][:, 0:128],
                                                       mv(MV_GA, dc, s), xdst, ALU.mult, ALU.add))

            def spill_x1(s, t4):
                slot = t4 % 2
                C.wait("gpsimd", "mmfree0", C.val("mmfree0"))
                C.wait("gpsimd", "mmfree1", C.val("mmfree1"))
                for c in range(DC):
                    C.prod("gpsimd", f"x1sp{slot}", 16, lambda e, c=c, s=s, t4=t4, slot=slot: e.dma_start(
                        out=x1_h[c, :, tok0(s, t4): tok0(s, t4) + 512],
                        in_=xfm[slot][:, c * 512:(c + 1) * 512]))

            if dbg:
                def emit_tap(dst, src_ap, width):
                    for nm in ("vh", "vh2", "vcx", "vbc"):
                        C.wait("gpsimd", nm, C.val(nm))
                    for nm in ("mmfree0", "mmfree1"):
                        C.wait("gpsimd", nm, C.val(nm))
                    C.prod("gpsimd", "tapd", 16, lambda e, dst=dst, src_ap=src_ap, width=width:
                           e.dma_start(out=dst[:, 0:width], in_=src_ap))
                    C.wait("gpsimd", "tapd")
                    for _e2 in ("vector", "scalar", "tensor"):
                        C.wait(_e2, "tapd")

            for s in range(BPC):
                att_next = 0
                for t4 in range(NT):
                    slot = t4 % 2
                    C.wait("sync", f"x1sp{slot}", C.val(f"x1sp{slot}"))
                    C.wait("sync", "vh2m", C.val("vh2m"))
                    for c in range(DC):
                        C.prod("sync", f"qx{slot}", 16, lambda e, c=c, s=s, t4=t4, slot=slot:
                               e.dma_start(out=xfm[slot][:, c * 512:(c + 1) * 512],
                                           in_=x_fm[c, :, tok0(s, t4): tok0(s, t4) + 512]))
                    C.wait("vector", f"qx{slot}")
                    C.wait("scalar", f"qx{slot}")
                    C.wait("gpsimd", f"qx{slot}")
                    C.wait("vector", "mmfree", max(0, C.val("mmfree") - 0) if False else 0)
                    for b4 in range(4):
                        emit_ln(s, t4 * 4 + b4, second=False)
                    emit_qkv(s, t4)
                    if dbg and s == 0 and t4 == 0:
                        emit_tap(tap_h, hbuf[0][:], 4096)
                        emit_tap(tap_q, QT[0][:], 4096)
                        emit_tap(tap_k, KT[0][:], 4096)
                        emit_tap(tap_v, VT[0][:], 1024)
                        emit_tap(tap_ab, bcsb[0][:], 256)
                    hi = t4 * 4 + 2 if t4 < NT - 1 else NBLK - 1
                    while att_next <= hi:
                        b = att_next
                        emit_attention(s, b)
                        if dbg and s == 0 and b == 0:
                            emit_tap(tap_c, ctxT[0][:], 1024)
                        emit_proj(s, b)
                        emit_ln(s, b, second=True)
                        if b % 4 == 3:
                            spill_x1(s, b // 4)
                        att_next += 1
            C.run_block("A")

        with ExitStack() as esR:
            r_ps = esR.enter_context(nc.psum_tensor("r_ps", [1, 128], F32))
            rb_ps = esR.enter_context(nc.psum_tensor("rb_ps", [128, 16], F32))

            C.wait("vector", "vh2m", C.val("vh2m"))
            C.prodw("vector", "vh2m", 1, lambda e: e.tensor_reduce(
                runmax[:, 0:1], maxbuf[:, 0:BPC * NBLK], AX.X, ALU.max))
            C.wait("tensor", "vh2m", C.val("vh2m"))
            C.prod("tensor", "tarX", 1, lambda e: e.matmul(
                r_ps[:], runmax[:, 0:1], idf[:], start=True, stop=True))
            C.wait("vector", "tarX")
            C.prodw("vector", "varX", 1, lambda e: e.tensor_reduce(
                ccrow[0:1, 0:1], r_ps[:], AX.X, ALU.max))
            C.wait("gpsimd", "varX")
            C.prod("gpsimd", "ccwX", 16, lambda e: e.dma_start(out=cc_in.ap(), in_=ccrow[:]))
            C.wait("gpsimd", "ccwX")
            C.prod("gpsimd", "arQ", 1, lambda e: e.collective_compute(
                "AllReduce", ALU.max, replica_groups=[list(range(NCORE))],
                ins=[cc_in.ap()], outs=[cc_out.ap()]))
            C.wait("gpsimd", "arQ")
            C.prod("gpsimd", "ccrX", 16, lambda e: e.dma_start(out=ccrow[:], in_=cc_out.ap()))
            C.wait("vector", "ccrX")
            C.prodw("vector", "ardX", 1, lambda e: e.reciprocal(rows[0:1, 0:1], ccrow[0:1, 0:1]))
            C.prodw("vector", "ardX", 1, lambda e: e.tensor_scalar(
                rows[0:1, 0:1], rows[0:1, 0:1], 32767.0, None, ALU.mult))
            C.wait("tensor", "ardX")
            C.prod("tensor", "trbX", 1, lambda e: e.matmul(
                rb_ps[:, 0:1], onesrf[:], rows[0:1, 0:1], start=True, stop=True))
            C.wait("vector", "trbX")
            C.prod("vector", "vrbX", 1, lambda e: e.tensor_copy(scv(SC_INVX), rb_ps[:, 0:1]))
            if dbg:
                C.wait("gpsimd", "vrbX")
                C.prod("gpsimd", "dbgsc", 16, lambda e: e.dma_start(out=scal_h, in_=ccrow[:]))
            C.run_block("AR1")

        MG = MAGIC
        with ExitStack() as esF:
            sbF = lambda n, sh, dt=F32: esF.enter_context(nc.sbuf_tensor(n, sh, dt))
            wdT = [sbF(f"wdT{h}", [128, D], BF16) for h in range(HC)]

            with ExitStack() as esB:
                sbB = lambda n, sh, dt=F32: esB.enter_context(nc.sbuf_tensor(n, sh, dt))
                Rb = [sbB(f"Rb{c}", [128, TOK], BF16) for c in range(DC)]
                wraw = [sbB(f"wraw{i}", [128, D]) for i in range(2)]
                wtmp = sbB("wtmp", [128, D])
                wq32 = sbB("wq32", [128, D])
                blkmx = sbB("blkmx", [128, 32])
                binv = sbB("binv", [128, 32])
                bscl = sbB("bscl", [128, 32])
                wstr = [sbB(f"wstr{i}", [128, DC * 128], BF16) for i in range(2)]
                qtmp = [sbB(f"qtmp{i}", [128, 1024]) for i in range(2)]
                spst = [sbB(f"spst{i}", [128, TOK], BF16) for i in range(2)]
                mmB = [esB.enter_context(nc.psum_tensor(f"mmB{i}", [128, 512], F32))
                       for i in range(2)]
                wtps = esB.enter_context(nc.psum_tensor("wtps", [128, 1024], F32))

                for _eng in Ctx.ENGINES:
                    for nm in ("x1sp0", "x1sp1", "h2sp0", "h2sp1"):
                        C.wait(_eng, nm)
                    C.wait(_eng, "vrbX")

                h2in = [esB.enter_context(nc.sbuf_tensor(f"h2in{i}", [128, 1024], BF16))
                        for i in range(2)]
                for dc in range(DC):
                    for half in range(4):
                        hw = 1024
                        r2 = (dc * 4 + half) % 2
                        C.wait("sync", "b0q", max(0, C.val("b0q") - 1))
                        C.prod("sync", f"ldA{r2}", 16, lambda e, dc=dc, half=half, r2=r2:
                               e.dma_start(out=h2in[r2][:],
                                           in_=h2_h[dc, :, half * hw:(half + 1) * hw]))
                        C.wait("scalar", f"ldA{r2}")
                        C.wait("scalar", "b0q", max(0, C.val("b0q") - 1))
                        C.prod("scalar", "b0s", 1, lambda e, r2=r2: e.activation(
                            qtmp[r2][:], h2in[r2][:], AF.Identity, bias=mgc[:, 0:1], scale=scv(SC_INVX)))
                        C.wait("vector", "b0s")
                        C.prod("vector", "b0q", 1, lambda e, dc=dc, half=half, r2=r2:
                               e.tensor_scalar(Rb[dc][:, half * hw:(half + 1) * hw],
                                               qtmp[r2][:], -MG, None, ALU.add))
                C.wait("tensor", "b0q", C.val("b0q"))
                bqm_marks = []
                bdq_marks = []
                for wi, (wsrc, wdst, rmcol) in enumerate(
                        ((w_up, upr_h, 1), (w_gate, gtr_h, 2))):
                    for hc in range(HC):
                        wsl = (wi * HC + hc) % 2
                        C.wait("sync", "bdq", bdq_marks[-2] if len(bdq_marks) >= 2 else 0)
                        C.prod("sync", f"ldA{wsl}", 16, lambda e, wsrc=wsrc, hc=hc, wsl=wsl:
                               e.dma_start(out=wraw[wsl][:], in_=wsrc[hc * 128:(hc + 1) * 128, :]))
                        C.wait("vector", f"ldA{wsl}")
                        C.prodw("vector", "bdq", 1, lambda e, wsl=wsl: e.tensor_reduce(
                            blkmx[:], wraw[wsl][:].rearrange("p (a b) -> p a b", b=32),
                            AX.X, ALU.max, apply_absolute_value=True))
                        C.prodw("vector", "bdq", 1, lambda e: e.tensor_scalar(
                            bscl[:], blkmx[:], 1.0 / 7.0, 1e-12, ALU.mult, ALU.max))
                        C.prodw("vector", "bdq", 1, lambda e: e.reciprocal(binv[:], bscl[:]))
                        C.prod("vector", "bdq", 1, lambda e, wsl=wsl: e.tensor_tensor(
                            wtmp[:].rearrange("p (a b) -> p a b", b=32),
                            wraw[wsl][:].rearrange("p (a b) -> p a b", b=32),
                            binv[:].unsqueeze(2).broadcast_to([128, 32, 32]), ALU.mult))
                        bdq_marks.append(C.val("bdq"))
                        C.prod("vector", "bdq", 1, lambda e: e.tensor_scalar(
                            wtmp[:], wtmp[:], MG, -MG, ALU.add, ALU.add))
                        C.prod("vector", "bdq", 1, lambda e: e.tensor_tensor(
                            wq32[:].rearrange("p (a b) -> p a b", b=32),
                            wtmp[:].rearrange("p (a b) -> p a b", b=32),
                            bscl[:].unsqueeze(2).broadcast_to([128, 32, 32]), ALU.mult))
                        C.wait("tensor", "bdq")
                        C.wait("tensor", "bst", C.val("bst"))
                        for dc in range(DC):
                            C.prod("tensor", "btr", 1, lambda e, dc=dc: e.transpose(
                                wtps[:, dc * 128:(dc + 1) * 128], wq32[:, dc * 128:(dc + 1) * 128],
                                idf[:]))
                        C.wait("scalar", "btr")
                        C.wait("scalar", "bqm", bqm_marks[-2] if len(bqm_marks) >= 2 else 0)
                        C.prod("scalar", "bst", 1, lambda e, wsl=wsl: e.copy(
                            wstr[wsl][:], wtps[:]))
                        C.wait("tensor", "bst", C.val("bst"))
                        ssl = C.ctr("bse") % 2
                        C.ctr_inc("bse")
                        C.wait("scalar", f"sd{ssl}", C.val(f"sd{ssl}"))
                        for tt in range(NTT):
                            bank = C.counts["mmuse"] % 2
                            C.wait("tensor", f"mmfree{bank}", C.val(f"mmfree{bank}"))
                            C.counts["mmuse"] += 1
                            for dc in range(DC):
                                C.prod("tensor", "bqm", 1, lambda e, dc=dc, tt=tt, bank=bank,
                                       wsl=wsl: e.matmul(
                                           mmB[bank][:],
                                           wstr[wsl][:, dc * 128:(dc + 1) * 128],
                                           Rb[dc][:, tt * 512:(tt + 1) * 512],
                                           start=(dc == 0), stop=(dc == DC - 1)))
                            C.wait("scalar", "bqm")
                            C.prod("scalar", f"mmfree{bank}", 1, lambda e, bank=bank, ssl=ssl,
                                   tt=tt: e.copy(spst[ssl][:, tt * 512:(tt + 1) * 512],
                                                 mmB[bank][:]))
                            C.wait("vector", f"mmfree{bank}")
                            bcol = wi * 192 + hc * 8 + tt
                            C.prod("vector", "bsmx0", 1, lambda e, ssl=ssl, tt=tt, bcol=bcol:
                                   e.tensor_reduce(maxbuf[:, bcol:bcol + 1],
                                                   spst[ssl][:, tt * 512:(tt + 1) * 512],
                                                   AX.X, ALU.max, apply_absolute_value=True))
                        C.wait("gpsimd", "mmfree0", C.val("mmfree0"))
                        C.wait("gpsimd", "mmfree1", C.val(f"mmfree1"))
                        C.prod("gpsimd", f"sd{ssl}", 16, lambda e, wdst=wdst, hc=hc, ssl=ssl:
                               e.dma_start(out=wdst[hc, :, :], in_=spst[ssl][:]))
                        bqm_marks.append(C.val("bqm"))
            with ExitStack() as esR2:
                r_ps2 = esR2.enter_context(nc.psum_tensor("r_ps2", [1, 128], F32))
                rb_ps2 = esR2.enter_context(nc.psum_tensor("rb_ps2", [128, 16], F32))
                C.wait("vector", "bsmx0", C.val("bsmx0"))
                C.prod("vector", "bvmx", 1, lambda e: e.tensor_reduce(
                    runmax[:, 1:2], maxbuf[:, 0:192], AX.X, ALU.max))
                C.prodw("vector", "bvmx", 1, lambda e: e.tensor_reduce(
                    runmax[:, 2:3], maxbuf[:, 192:384], AX.X, ALU.max))
                C.prod("vector", "bvmx", 1, lambda e: e.memset(maxbuf[:], 0.0))
                C.wait("tensor", "bvmx", C.val("bvmx"))
                for j, col in enumerate((1, 2)):
                    C.wait("tensor", "varX", C.val("varX"))
                    C.prod("tensor", "tarX", 1, lambda e, col=col: e.matmul(
                        r_ps2[:], runmax[:, col:col + 1], idf[:], start=True, stop=True))
                    C.wait("vector", "tarX")
                    C.prodw("vector", "varX", 1, lambda e, j=j: e.tensor_reduce(
                        ccrow[0:1, j:j + 1], r_ps2[:], AX.X, ALU.max))
                C.wait("gpsimd", "varX")
                C.prod("gpsimd", "ccwX", 16, lambda e: e.dma_start(out=cc_in.ap(), in_=ccrow[:]))
                C.wait("gpsimd", "ccwX")
                C.prod("gpsimd", "arQ", 1, lambda e: e.collective_compute(
                    "AllReduce", ALU.max, replica_groups=[list(range(NCORE))],
                    ins=[cc_in.ap()], outs=[cc_out.ap()]))
                C.wait("gpsimd", "arQ")
                C.prod("gpsimd", "ccrX", 16, lambda e: e.dma_start(out=ccrow[:], in_=cc_out.ap()))
                C.wait("vector", "ccrX")
                C.prodw("vector", "ardX", 1, lambda e: e.reciprocal(rows[0:1, 21:22], scal[0:1, SC_INVX:SC_INVX + 1]))
                C.prodw("vector", "ardX", 1, lambda e: e.tensor_tensor(
                    rows[0:1, 16:17], ccrow[0:1, 0:1], rows[0:1, 21:22], ALU.mult))
                C.prodw("vector", "ardX", 1, lambda e: e.tensor_scalar(
                    rows[0:1, 16:17], rows[0:1, 16:17], 1.0 / 32767.0, None, ALU.mult))
                C.prodw("vector", "ardX", 1, lambda e: e.tensor_tensor(
                    rows[0:1, 17:18], ccrow[0:1, 1:2], rows[0:1, 21:22], ALU.mult))
                C.prodw("vector", "ardX", 1, lambda e: e.tensor_scalar(
                    rows[0:1, 17:18], rows[0:1, 17:18], 1.0 / 32767.0, None, ALU.mult))
                C.prodw("vector", "ardX", 1, lambda e: e.reciprocal(rows[0:1, 18:19], ccrow[0:1, 1:2]))
                C.prodw("vector", "ardX", 1, lambda e: e.tensor_scalar(
                    rows[0:1, 18:19], rows[0:1, 18:19], 32767.0, None, ALU.mult))
                C.prodw("vector", "ardX", 1, lambda e: e.reciprocal(rows[0:1, 19:20], ccrow[0:1, 0:1]))
                C.prodw("vector", "ardX", 1, lambda e: e.tensor_scalar(
                    rows[0:1, 19:20], rows[0:1, 19:20], 32767.0, None, ALU.mult))
                C.prodw("vector", "ardX", 1, lambda e: e.tensor_scalar(
                    rows[0:1, 20:21], rows[0:1, 17:18], -MG, None, ALU.mult))
                C.wait("tensor", "ardX")
                C.prod("tensor", "trbX", 1, lambda e: e.matmul(
                    rb_ps2[:, 0:4], onesrf[:], rows[0:1, 17:21], start=True, stop=True))
                C.wait("vector", "trbX")
                C.prod("vector", "vrbX", 1, lambda e: e.tensor_copy(scal[:, SC_SGT:SC_SGT + 1], rb_ps2[:, 0:1]))
                C.prod("vector", "vrbX", 1, lambda e: e.tensor_copy(scal[:, SC_KG:SC_KG + 1], rb_ps2[:, 1:2]))
                C.prod("vector", "vrbX", 1, lambda e: e.tensor_copy(scal[:, SC_KU:SC_KU + 1], rb_ps2[:, 2:3]))
                C.prodw("vector", "vrbX", 1, lambda e: e.tensor_copy(scal[:, SC_BGT:SC_BGT + 1], rb_ps2[:, 3:4]))
                C.run_block() if False else None

            with ExitStack() as esC:
                sbC = lambda n, sh, dt=F32: esC.enter_context(nc.sbuf_tensor(n, sh, dt))
                gg = [sbC(f"gg{i}", [128, 512]) for i in range(2)]
                sg = [sbC(f"sg{i}", [128, 512]) for i in range(2)]
                t2b = [sbC(f"t2b{i}", [128, 512]) for i in range(2)]
                s2c = [sbC(f"s2c{i}", [128, 512]) for i in range(2)]
                wrawC = [sbC(f"wrawC{i}", [128, HID]) for i in range(2)]
                wtmpC = sbC("wtmpC", [128, HID])
                wqC = sbC("wqC", [128, HID])
                bmxC = sbC("bmxC", [128, 96])
                bivC = sbC("bivC", [128, 96])
                bscC = sbC("bscC", [128, 96])
                wtpsC = esC.enter_context(nc.psum_tensor("wtpsC", [128, 1024], F32))

                for _eng in Ctx.ENGINES:
                    C.wait(_eng, "vrbX")
                    C.wait(_eng, "sd0"); C.wait(_eng, "sd1")
                for rc in range(DC):
                    wsl = rc % 2
                    C.wait("sync", "cdqv", max(0, C.val("cdqv") - 1))
                    C.prod("sync", f"ldA{wsl}", 16, lambda e, rc=rc, wsl=wsl: e.dma_start(
                        out=wrawC[wsl][:], in_=w_down[rc * 128:(rc + 1) * 128, :]))
                    C.wait("vector", f"ldA{wsl}")
                    C.prodw("vector", "cdq", 1, lambda e, wsl=wsl: e.tensor_reduce(
                        bmxC[:], wrawC[wsl][:].rearrange("p (a b) -> p a b", b=32),
                        AX.X, ALU.max, apply_absolute_value=True))
                    C.prodw("vector", "cdq", 1, lambda e: e.tensor_scalar(
                        bscC[:], bmxC[:], 1.0 / 7.0, 1e-12, ALU.mult, ALU.max))
                    C.prodw("vector", "cdq", 1, lambda e: e.reciprocal(bivC[:], bscC[:]))
                    C.prod("vector", "cdq", 1, lambda e, wsl=wsl: e.tensor_tensor(
                        wtmpC[:].rearrange("p (a b) -> p a b", b=32),
                        wrawC[wsl][:].rearrange("p (a b) -> p a b", b=32),
                        bivC[:].unsqueeze(2).broadcast_to([128, 96, 32]), ALU.mult))
                    C.prod("vector", "cdq", 1, lambda e: e.tensor_scalar(
                        wtmpC[:], wtmpC[:], MG, -MG, ALU.add, ALU.add))
                    C.prod("vector", "cdqv", 1, lambda e: e.tensor_tensor(
                        wqC[:].rearrange("p (a b) -> p a b", b=32),
                        wtmpC[:].rearrange("p (a b) -> p a b", b=32),
                        bscC[:].unsqueeze(2).broadcast_to([128, 96, 32]), ALU.mult))
                    C.wait("tensor", "cdqv")
                    for grp in range(HC // 8):
                        C.wait("tensor", "cst", C.val("cst"))
                        for bb in range(8):
                            hh = grp * 8 + bb
                            C.prod("tensor", "ctr", 1, lambda e, hh=hh, bb=bb: e.transpose(
                                wtpsC[:, bb * 128:(bb + 1) * 128],
                                wqC[:, hh * 128:(hh + 1) * 128], idf[:]))
                        C.wait("scalar", "ctr")
                        for bb in range(8):
                            hh = grp * 8 + bb
                            C.prod("scalar", "cst", 1, lambda e, hh=hh, bb=bb, rc=rc: e.copy(
                                wdT[hh][:, rc * 128:(rc + 1) * 128],
                                wtpsC[:, bb * 128:(bb + 1) * 128]))
                grow = [esC.enter_context(nc.sbuf_tensor(f"grow{i}", [128, TOK], BF16))
                        for i in range(2)]
                pprow = [esC.enter_context(nc.sbuf_tensor(f"pprow{i}", [128, TOK], BF16))
                        for i in range(2)]
                for hc in range(HC):
                    r2 = hc % 2
                    C.wait("sync", f"sd{r2}", C.val(f"sd{r2}"))
                    C.wait("sync", "cpp", max(0, C.val("cpp") - NTT))
                    C.prod("sync", f"ldA{r2}", 16, lambda e, hc=hc, r2=r2: e.dma_start(
                        out=grow[r2][:], in_=gtr_h[hc, :, :]))
                    C.wait("vector", f"ldA{r2}")
                    C.wait("vector", f"sd{r2}", C.val(f"sd{r2}"))
                    C.wait("scalar", f"ldA{r2}")
                    C.wait("gpsimd", f"ldA{r2}")
                    for tt in range(NTT):
                        t2_ = (hc * NTT + tt) % 2
                        gsl = grow[r2][:, tt * 512:(tt + 1) * 512]
                        psl = pprow[r2][:, tt * 512:(tt + 1) * 512]
                        C.wait("vector", "csg", max(0, C.val("csg") - 1))
                        C.prod("vector", "cgg", 1, lambda e, gsl=gsl, t2_=t2_: e.tensor_scalar(
                            gg[t2_][:], gsl, scv(SC_KG), MG, ALU.mult, ALU.add))
                        C.wait("scalar", "cgg")
                        C.wait("scalar", "cs2", max(0, C.val("cs2") - 1))
                        C.prod("scalar", "csg", 1, lambda e, t2_=t2_: e.activation(
                            sg[t2_][:], gg[t2_][:], AF.Sigmoid, bias=scv(SC_BGT), scale=scv(SC_SGT)))
                        C.prod("scalar", "ct2", 1, lambda e, t2_=t2_: e.activation(
                            t2b[t2_][:], sg[t2_][:], AF.Identity, bias=mgc[:, 0:1], scale=65536.0))
                        C.wait("vector", "ct2")
                        C.prod("vector", "cs2", 1, lambda e, t2_=t2_: e.tensor_scalar(
                            s2c[t2_][:], t2b[t2_][:], -MG, 65535.0, ALU.add, ALU.min))
                        C.prod("vector", "cpp0", 1, lambda e, t2_=t2_, psl=psl: e.scalar_tensor_tensor(
                            psl, gg[t2_][:], -MG, s2c[t2_][:], ALU.add, ALU.mult))
                        ccol = hc * 8 + tt
                        C.prod("vector", "cpp", 1, lambda e, psl=psl, ccol=ccol: e.tensor_reduce(
                            maxbuf[:, ccol:ccol + 1], psl, AX.X, ALU.max,
                            apply_absolute_value=True))
                    C.wait("gpsimd", "cpp0", C.val("cpp0"))
                    C.prod("gpsimd", f"sd{r2}", 16, lambda e, hc=hc, r2=r2: e.dma_start(
                        out=pp_h[hc, :, :], in_=pprow[r2][:]))

            with ExitStack() as esR3:
                r_ps3 = esR3.enter_context(nc.psum_tensor("r_ps3", [1, 128], F32))
                rb_ps3 = esR3.enter_context(nc.psum_tensor("rb_ps3", [128, 16], F32))
                C.wait("vector", "cpp", C.val("cpp"))
                C.prod("vector", "cpp", 1, lambda e: e.tensor_reduce(
                    runmax[:, 3:4], maxbuf[:, 0:192], AX.X, ALU.max))
                C.prodw("vector", "cpp", 1, lambda e: e.memset(maxbuf[:, 0:192], 0.0))
                C.wait("tensor", "cpp", C.val("cpp"))
                C.prod("tensor", "tarX", 1, lambda e: e.matmul(
                    r_ps3[:], runmax[:, 3:4], idf[:], start=True, stop=True))
                C.wait("vector", "tarX")
                C.prodw("vector", "varX", 1, lambda e: e.tensor_reduce(
                    ccrow[0:1, 0:1], r_ps3[:], AX.X, ALU.max))
                C.wait("gpsimd", "varX")
                C.prod("gpsimd", "ccwX", 16, lambda e: e.dma_start(out=cc_in.ap(), in_=ccrow[:]))
                C.wait("gpsimd", "ccwX")
                C.prod("gpsimd", "arQ", 1, lambda e: e.collective_compute(
                    "AllReduce", ALU.max, replica_groups=[list(range(NCORE))],
                    ins=[cc_in.ap()], outs=[cc_out.ap()]))
                C.wait("gpsimd", "arQ")
                C.prod("gpsimd", "ccrX", 16, lambda e: e.dma_start(out=ccrow[:], in_=cc_out.ap()))
                C.wait("vector", "ccrX")
                C.prodw("vector", "ardX", 1, lambda e: e.reciprocal(rows[0:1, 23:24], ccrow[0:1, 0:1]))
                C.prodw("vector", "ardX", 1, lambda e: e.tensor_scalar(
                    rows[0:1, 23:24], rows[0:1, 23:24], 32767.0, None, ALU.mult))
                C.prodw("vector", "ardX", 1, lambda e: e.tensor_tensor(
                    rows[0:1, 22:23], ccrow[0:1, 0:1], rows[0:1, 17:18], ALU.mult))
                C.prodw("vector", "ardX", 1, lambda e: e.tensor_scalar(
                    rows[0:1, 22:23], rows[0:1, 22:23], 1.0 / (65536.0 * 32767.0), None, ALU.mult))
                C.wait("tensor", "ardX")
                C.prod("tensor", "trbX", 1, lambda e: e.matmul(
                    rb_ps3[:, 0:1], onesrf[:], rows[0:1, 23:24], start=True, stop=True))
                C.wait("vector", "trbX")
                C.prodw("vector", "vrbX", 1, lambda e: e.tensor_copy(scv(SC_K1), rb_ps3[:, 0:1]))

            with ExitStack() as esD:
                sbD = lambda n, sh, dt=F32: esD.enter_context(nc.sbuf_tensor(n, sh, dt))
                t3b = [sbD(f"t3b{i}", [128, 512]) for i in range(2)]
                tub = [sbD(f"tub{i}", [128, 512]) for i in range(2)]
                ugb = [sbD(f"ugb{i}", [128, 512]) for i in range(2)]
                for _eng in Ctx.ENGINES:
                    C.wait(_eng, "vrbX")
                    C.wait(_eng, "sd0"); C.wait(_eng, "sd1")
                prow = [esD.enter_context(nc.sbuf_tensor(f"prow{i}", [128, TOK], BF16))
                        for i in range(2)]
                urow = [esD.enter_context(nc.sbuf_tensor(f"urow{i}", [128, TOK], BF16))
                        for i in range(2)]
                oorow = [esD.enter_context(nc.sbuf_tensor(f"oorow{i}", [128, TOK], BF16))
                         for i in range(2)]
                for hc in range(HC):
                    r2 = hc % 2
                    C.wait("sync", f"sd{r2}", C.val(f"sd{r2}"))
                    C.wait("sync", "doo", max(0, C.val("doo") - NTT))
                    C.prod("sync", f"ldA{r2}", 16, lambda e, hc=hc, r2=r2: e.dma_start(
                        out=prow[r2][:], in_=pp_h[hc, :, :]))
                    C.prod("sync", f"ldA{r2}", 16, lambda e, hc=hc, r2=r2: e.dma_start(
                        out=urow[r2][:], in_=upr_h[hc, :, :]))
                    C.wait("scalar", f"ldA{r2}")
                    C.wait("gpsimd", f"ldA{r2}")
                    C.wait("vector", f"sd{r2}", C.val(f"sd{r2}"))
                    for tt in range(NTT):
                        t2_ = (hc * NTT + tt) % 2
                        psl = prow[r2][:, tt * 512:(tt + 1) * 512]
                        usl = urow[r2][:, tt * 512:(tt + 1) * 512]
                        osl_ = oorow[r2][:, tt * 512:(tt + 1) * 512]
                        C.wait("scalar", "doo0", max(0, C.val("doo0") - 1))
                        C.prod("scalar", "dt3", 1, lambda e, t2_=t2_, psl=psl: e.activation(
                            t3b[t2_][:], psl, AF.Identity, bias=mgc[:, 0:1], scale=scv(SC_K1)))
                        C.prod("scalar", "dtu", 1, lambda e, t2_=t2_, usl=usl: e.activation(
                            tub[t2_][:], usl, AF.Identity, bias=mgc[:, 0:1], scale=scv(SC_KU)))
                        C.prod("scalar", "dug", 1, lambda e, t2_=t2_: e.activation(
                            ugb[t2_][:], tub[t2_][:], AF.Identity, bias=mgc[:, 1:2]))
                        C.wait("vector", "dug")
                        C.prod("vector", "doo0", 1, lambda e, t2_=t2_, osl_=osl_: e.scalar_tensor_tensor(
                            osl_, t3b[t2_][:], -MG, ugb[t2_][:], ALU.add, ALU.mult))
                        dcol = hc * 8 + tt
                        C.prod("vector", "doo", 1, lambda e, osl_=osl_, dcol=dcol: e.tensor_reduce(
                            maxbuf[:, dcol:dcol + 1], osl_, AX.X, ALU.max,
                            apply_absolute_value=True))
                    C.wait("gpsimd", "doo0", C.val("doo0"))
                    C.prod("gpsimd", f"sd{r2}", 16, lambda e, hc=hc, r2=r2: e.dma_start(
                        out=oo_h[hc, :, :], in_=oorow[r2][:]))

            with ExitStack() as esR4:
                r_ps4 = esR4.enter_context(nc.psum_tensor("r_ps4", [1, 128], F32))
                rb_ps4 = esR4.enter_context(nc.psum_tensor("rb_ps4", [128, 16], F32))
                C.wait("vector", "doo", C.val("doo"))
                C.prodw("vector", "doo", 1, lambda e: e.tensor_reduce(
                    runmax[:, 4:5], maxbuf[:, 0:192], AX.X, ALU.max))
                C.wait("tensor", "doo", C.val("doo"))
                C.prod("tensor", "tarX", 1, lambda e: e.matmul(
                    r_ps4[:], runmax[:, 4:5], idf[:], start=True, stop=True))
                C.wait("vector", "tarX")
                C.prodw("vector", "varX", 1, lambda e: e.tensor_reduce(
                    ccrow[0:1, 0:1], r_ps4[:], AX.X, ALU.max))
                C.wait("gpsimd", "varX")
                C.prod("gpsimd", "ccwX", 16, lambda e: e.dma_start(out=cc_in.ap(), in_=ccrow[:]))
                C.wait("gpsimd", "ccwX")
                C.prod("gpsimd", "arQ", 1, lambda e: e.collective_compute(
                    "AllReduce", ALU.max, replica_groups=[list(range(NCORE))],
                    ins=[cc_in.ap()], outs=[cc_out.ap()]))
                C.wait("gpsimd", "arQ")
                C.prod("gpsimd", "ccrX", 16, lambda e: e.dma_start(out=ccrow[:], in_=cc_out.ap()))
                C.wait("vector", "ccrX")
                C.prodw("vector", "ardX", 1, lambda e: e.reciprocal(rows[0:1, 24:25], ccrow[0:1, 0:1]))
                C.prodw("vector", "ardX", 1, lambda e: e.tensor_scalar(
                    rows[0:1, 24:25], rows[0:1, 24:25], 32767.0, None, ALU.mult))
                C.prodw("vector", "ardX", 1, lambda e: e.tensor_tensor(
                    rows[0:1, 25:26], ccrow[0:1, 0:1], rows[0:1, 22:23], ALU.mult))
                C.prodw("vector", "ardX", 1, lambda e: e.tensor_tensor(
                    rows[0:1, 25:26], rows[0:1, 25:26], rows[0:1, 16:17], ALU.mult))
                C.prodw("vector", "ardX", 1, lambda e: e.tensor_scalar(
                    rows[0:1, 25:26], rows[0:1, 25:26], 1.0 / 32767.0, None, ALU.mult))
                C.wait("tensor", "ardX")
                C.prod("tensor", "trbX", 1, lambda e: e.matmul(
                    rb_ps4[:, 0:2], onesrf[:], rows[0:1, 24:26], start=True, stop=True))
                C.wait("vector", "trbX")
                C.prod("vector", "vrbX", 1, lambda e: e.tensor_copy(scv(SC_K2), rb_ps4[:, 0:1]))
                C.prodw("vector", "vrbX", 1, lambda e: e.tensor_copy(scv(SC_SO), rb_ps4[:, 1:2]))
                C.prodw("vector", "vrbX", 1, lambda e: e.tensor_scalar(
                    modv[:, MV_GB * 16: MV_GB * 16 + 16],
                    modv[:, MV_GB * 16: MV_GB * 16 + 16], scv(SC_SO), None, ALU.mult))

            with ExitStack() as esE:
                sbE = lambda n, sh, dt=F32: esE.enter_context(nc.sbuf_tensor(n, sh, dt))
                oin = [sbE(f"oin{i}", [128, 512], BF16) for i in range(2)]
                t4b = [sbE(f"t4b{i}", [128, 512]) for i in range(2)]
                Ob = [sbE(f"Ob{i}", [128, 512], BF16) for i in range(HC)]
                x1t = [sbE(f"x1t{i}", [128, DC * 512]) for i in range(2)]
                outb = [sbE(f"outb{i}", [128, 512]) for i in range(4)]
                mmE = [esE.enter_context(nc.psum_tensor(f"mmE{i}", [128, 512], F32))
                       for i in range(2)]
                for _eng in Ctx.ENGINES:
                    C.wait(_eng, "vrbX")
                    C.wait(_eng, "sd0"); C.wait(_eng, "sd1")
                for tt in range(NTT):
                    s = tt // NT
                    xs = tt % 2
                    C.wait("sync", "mmfree0", C.val("mmfree0"))
                    C.wait("sync", "mmfree1", C.val("mmfree1"))
                    for dc in range(DC):
                        C.prod("sync", f"ex1{xs}", 16, lambda e, dc=dc, tt=tt, xs=xs: e.dma_start(
                            out=x1t[xs][:, dc * 512:(dc + 1) * 512],
                            in_=x1_h[dc, :, tt * 512:(tt + 1) * 512]))
                    for hc in range(HC):
                        r2 = (tt * HC + hc) % 2
                        C.wait("sync", "eO", max(0, C.val("eO") - 1))
                        C.prod("sync", f"ldA{r2}", 16, lambda e, hc=hc, tt=tt, r2=r2: e.dma_start(
                            out=oin[r2][:], in_=oo_h[hc, :, tt * 512:(tt + 1) * 512]))
                        C.wait("scalar", f"ldA{r2}")
                        C.wait("scalar", "eO", max(0, C.val("eO") - 1))
                        C.wait("scalar", "emm", C.val("emm"))
                        C.prod("scalar", "et4", 1, lambda e, r2=r2: e.activation(
                            t4b[r2][:], oin[r2][:], AF.Identity, bias=mgc[:, 0:1], scale=scv(SC_K2)))
                        C.prod("scalar", "eO", 1, lambda e, hc=hc, r2=r2: e.activation(
                            Ob[hc][:], t4b[r2][:], AF.Identity, bias=mgc[:, 1:2]))
                    C.wait("tensor", "eO", C.val("eO"))
                    C.wait("tensor", "cst", C.val("cst"))
                    C.wait("tensor", f"ex1{xs}", C.val(f"ex1{xs}"))
                    for dc in range(DC):
                        bank = C.counts["mmuse"] % 2
                        C.wait("tensor", f"mmfree{bank}", C.val(f"mmfree{bank}"))
                        C.counts["mmuse"] += 1
                        for hc in range(HC):
                            C.prod("tensor", "emm", 1, lambda e, dc=dc, hc=hc, bank=bank:
                                   e.matmul(mmE[bank][:], wdT[hc][:, dc * 128:(dc + 1) * 128],
                                            Ob[hc][:], start=(hc == 0), stop=(hc == HC - 1)))
                        osl = C.ctr("eev") % 4
                        C.wait("vector", "emm")
                        C.wait("vector", f"sd{osl}", C.val(f"sd{osl}"))
                        C.wait("vector", f"ex1{xs}", C.val(f"ex1{xs}"))
                        C.prod("vector", f"mmfree{bank}", 1, lambda e, dc=dc, s=s, xs=xs, osl=osl,
                               bank=bank: e.scalar_tensor_tensor(
                                   outb[osl][:], mmE[bank][:], mv(MV_GB, dc, s),
                                   x1t[xs][:, dc * 512:(dc + 1) * 512], ALU.mult, ALU.add))
                        C.ctr_inc("eev")
                        C.wait("gpsimd", f"mmfree{bank}", C.val(f"mmfree{bank}"))
                        C.prod("gpsimd", f"sd{osl}", 16, lambda e, dc=dc, tt=tt, osl=osl:
                               e.dma_start(out=out_fm[dc, :, tt * 512:(tt + 1) * 512],
                                           in_=outb[osl][:]))
                for i in range(4):
                    C.wait("gpsimd", f"sd{i}", C.val(f"sd{i}"))
                C.run_block("BE")

    return nc



def _prep_inputs(inputs):
    f32 = np.float32
    bf16 = ml_dtypes.bfloat16
    x = np.asarray(inputs["L0_attn_x"], f32)
    b_in = np.asarray(inputs["L0_attn_b_in"], f32)
    b_out = np.asarray(inputs["L0_attn_b_out"], f32)
    w_inT = np.ascontiguousarray(np.asarray(inputs["L0_attn_w_in"], f32).T)
    w_outT = np.ascontiguousarray(np.asarray(inputs["L0_attn_w_out"], f32).T)
    w_gate = np.asarray(inputs["L0_ffn_w_gate"], f32)
    w_up = np.asarray(inputs["L0_ffn_w_up"], f32)
    w_down = np.asarray(inputs["L0_ffn_w_down"], f32)
    c = np.asarray(inputs["L0_c"], f32)
    wmod = np.asarray(inputs["L0_wmod"], f32)

    ident_f = np.eye(128, dtype=f32)
    ident_bf = ident_f.astype(bf16)
    ones_r_f = np.ones((1, 128), f32)
    ones_r_bf = np.ones((1, 512), f32).astype(bf16)
    invD_c = np.full((128, 1), 1.0 / D, f32).astype(bf16)

    q = np.arange(128)[:, None]
    j = np.arange(384)[None, :]
    m_int = np.where((j >= q) & (j <= q + 256), 1.0, 0.0).astype(f32)
    m_first = np.where(np.abs(q - j) <= 128, 1.0, 0.0).astype(f32)
    m_last = np.where(j >= q + 128, 1.0, 0.0).astype(f32)
    masks = np.concatenate([m_int, m_first, m_last], axis=1).astype(bf16)

    in_maps = []
    for core in range(NCORE):
        sl = slice(core * BPC, (core + 1) * BPC)
        x_fmc = np.ascontiguousarray(
            x[sl].reshape(BPC * T, DC, 128).transpose(1, 2, 0))
        c_fmc = np.ascontiguousarray(c[sl].T.reshape(DC, 128, BPC))
        in_maps.append({
            "x_fm": x_fmc, "w_inT": w_inT, "b_in": b_in, "w_outT": w_outT,
            "b_out": b_out, "w_gate": w_gate, "w_up": w_up, "w_down": w_down,
            "c_fm": c_fmc, "wmod": wmod,
            "ident_bf": ident_bf, "ident_f": ident_f, "masks": masks,
            "ones_r_f": ones_r_f, "ones_r_bf": ones_r_bf, "invD_c_bf": invD_c,
        })
    return in_maps


def kernel(**inputs):
    in_maps = _prep_inputs(inputs)
    nc = build_nc(False)
    res = run_bass_kernel_spmd(nc, in_maps, core_ids=list(range(NCORE)))
    out = np.empty((B, T, D), np.float32)
    for core in range(NCORE):
        o = res.results[core]["out_fm"]
        out[core * BPC:(core + 1) * BPC] = o.transpose(2, 0, 1).reshape(BPC, T, D)
    return out


# version 12
# speedup vs baseline: 1.5237x; 1.0353x over previous
import numpy as np
import ml_dtypes
from contextlib import ExitStack

import concourse.bass as bass
import concourse.mybir as mybir
from concourse.bass_utils import run_bass_kernel_spmd

F32 = mybir.dt.float32
BF16 = mybir.dt.bfloat16
AF = mybir.ActivationFunctionType
ALU = mybir.AluOpType
AX = mybir.AxisListType

MAGIC = 12582912.0
NCORE = 8
B, T, D, HID = 16, 2048, 1024, 3072
NH, HDIM = 16, 64
BPC = B // NCORE
NBLK = T // 128
NT = T // 512
DC = D // 128
HC = HID // 128
TOK = BPC * T
NTT = TOK // 512


class Ctx:
    ENGINES = ("sync", "gpsimd", "vector", "scalar", "tensor")

    def __init__(self, nc, es):
        self.nc, self.es = nc, es
        self.sems, self.counts = {}, {}
        self.plan = {e: [] for e in self.ENGINES}

    def sem(self, name):
        if name not in self.sems:
            self.sems[name] = self.es.enter_context(self.nc.semaphore(name))
            self.counts[name] = 0
        return self.sems[name]

    def val(self, name):
        self.sem(name)
        return self.counts[name]

    def ctr(self, name):
        return self.counts.setdefault("#" + name, 0)

    def ctr_inc(self, name):
        self.counts["#" + name] = self.ctr(name) + 1

    def emit(self, eng, fn):
        self.plan[eng].append(fn)

    def wait(self, eng, name, value=None):
        h = self.sem(name)
        v = self.counts[name] if value is None else value
        if v > 0:
            self.emit(eng, lambda e, h=h, v=v: e.wait_ge(h, v))

    def prod(self, eng, name, k, fn):
        h = self.sem(name)
        self.counts[name] += k
        self.emit(eng, lambda e, h=h, k=k, fn=fn: fn(e).then_inc(h, k))

    def prodw(self, eng, name, k, fn):
        self.prod(eng, name, k, fn)
        self.wait(eng, name)

    def run_block(self, label=None):
        plan, self.plan = self.plan, {e: [] for e in self.ENGINES}
        import contextlib
        scope = self.nc.named_scope(label) if label else contextlib.nullcontext()
        with scope, self.nc.Block() as blk:
            blk.sync(lambda e: [f(e) for f in plan["sync"]])
            blk.gpsimd(lambda e: [f(e) for f in plan["gpsimd"]])
            blk.vector(lambda e: [f(e) for f in plan["vector"]])
            blk.scalar(lambda e: [f(e) for f in plan["scalar"]])
            blk.tensor(lambda e: [f(e) for f in plan["tensor"]])


def build_nc(dbg=False):
    nc = bass.Bass("TRN2", target_bir_lowering=False, detect_race_conditions=False)

    x_fm = nc.dram_tensor("x_fm", [DC, 128, TOK], F32, kind="ExternalInput").ap()
    w_inT = nc.dram_tensor("w_inT", [D, 3 * D], F32, kind="ExternalInput").ap()
    b_in = nc.dram_tensor("b_in", [3 * D], F32, kind="ExternalInput").ap()
    w_outT = nc.dram_tensor("w_outT", [D, D], F32, kind="ExternalInput").ap()
    b_out = nc.dram_tensor("b_out", [D], F32, kind="ExternalInput").ap()
    w_gate = nc.dram_tensor("w_gate", [HID, D], F32, kind="ExternalInput").ap()
    w_up = nc.dram_tensor("w_up", [HID, D], F32, kind="ExternalInput").ap()
    w_down = nc.dram_tensor("w_down", [D, HID], F32, kind="ExternalInput").ap()
    c_fm = nc.dram_tensor("c_fm", [DC, 128, BPC], F32, kind="ExternalInput").ap()
    wmod = nc.dram_tensor("wmod", [D, 6 * D], F32, kind="ExternalInput").ap()
    ident_bf = nc.dram_tensor("ident_bf", [128, 128], BF16, kind="ExternalInput").ap()
    ident_f = nc.dram_tensor("ident_f", [128, 128], F32, kind="ExternalInput").ap()
    masks = nc.dram_tensor("masks", [128, 3 * 384], BF16, kind="ExternalInput").ap()
    ones_r_f = nc.dram_tensor("ones_r_f", [1, 128], F32, kind="ExternalInput").ap()
    ones_r_bf = nc.dram_tensor("ones_r_bf", [1, 512], BF16, kind="ExternalInput").ap()
    invD_c_bf = nc.dram_tensor("invD_c_bf", [128, 1], BF16, kind="ExternalInput").ap()

    out_fm = nc.dram_tensor("out_fm", [DC, 128, TOK], F32, kind="ExternalOutput").ap()

    skind = "ExternalOutput" if dbg else "Internal"
    x1_h = nc.dram_tensor("x1_h", [DC, 128, TOK], F32, kind=skind).ap()
    h2_h = nc.dram_tensor("h2_h", [DC, 128, TOK], BF16, kind=skind).ap()
    upr_h = nc.dram_tensor("upr_h", [HC, 128, TOK], BF16, kind=skind).ap()
    gtr_h = nc.dram_tensor("gtr_h", [HC, 128, TOK], BF16, kind=skind).ap()
    pp_h = nc.dram_tensor("pp_h", [HC, 128, TOK], BF16, kind=skind).ap()
    oo_h = nc.dram_tensor("oo_h", [HC, 128, TOK], BF16, kind=skind).ap()
    scal_h = (nc.dram_tensor("scal_h", [1, 16], F32, kind="ExternalOutput").ap()
              if dbg else None)
    if dbg:
        tap_h = nc.dram_tensor("tap_h", [128, 4096], BF16, kind="ExternalOutput").ap()
        tap_q = nc.dram_tensor("tap_q", [128, 4096], BF16, kind="ExternalOutput").ap()
        tap_k = nc.dram_tensor("tap_k", [128, 4096], BF16, kind="ExternalOutput").ap()
        tap_v = nc.dram_tensor("tap_v", [128, 1024], BF16, kind="ExternalOutput").ap()
        tap_c = nc.dram_tensor("tap_c", [128, 1024], BF16, kind="ExternalOutput").ap()
        tap_ab = nc.dram_tensor("tap_ab", [128, 256], F32, kind="ExternalOutput").ap()
        tap_e = nc.dram_tensor("tap_e", [128, 384], BF16, kind="ExternalOutput").ap()
        tap_at = nc.dram_tensor("tap_at", [128, 384], BF16, kind="ExternalOutput").ap()
        tap_d = nc.dram_tensor("tap_d", [128, 128], BF16, kind="ExternalOutput").ap()
        tap_sm = nc.dram_tensor("tap_sm", [128, 4], F32, kind="ExternalOutput").ap()
    cc_in = nc.dram_tensor("cc_in", [1, 16], F32, kind="Internal")
    cc_out = nc.dram_tensor("cc_out", [1, 16], F32, kind="Internal")

    es = ExitStack()
    with es:
        C = Ctx(nc, es)
        sbuf = lambda n, s, dt=F32: es.enter_context(nc.sbuf_tensor(n, s, dt))

        idbf = sbuf("idbf", [128, 128], BF16)
        idf = sbuf("idf", [128, 128], F32)
        msk = sbuf("msk", [128, 3 * 384], BF16)
        onesrf = sbuf("onesrf", [1, 128], F32)
        onesrbf = sbuf("onesrbf", [1, 512], BF16)
        invDc = sbuf("invDc", [128, 1], BF16)
        bq_col = sbuf("bq_col", [128, 8], F32)
        bk_col = sbuf("bk_col", [128, 8], F32)
        bv_row = sbuf("bv_row", [1, 1024], BF16)
        bo_row = sbuf("bo_row", [1, 1024], BF16)
        modv = sbuf("modv", [128, 6 * 16], F32)
        scal = sbuf("scal", [128, 16], F32)
        runmax = sbuf("runmax", [128, 8], F32)
        rows = sbuf("rows", [1, 512], F32)
        ccrow = sbuf("ccrow", [1, 16], F32)
        gascr = sbuf("gascr", [128, 4], F32)
        mgc = sbuf("mgc", [128, 2], F32)
        maxbuf = sbuf("maxbuf", [128, 384], F32)

        MV_ONEPCA, MV_SA, MV_GA, MV_ONEPCB, MV_SB, MV_GB = range(6)

        def mv(which, c, s):
            j = which * 16 + c * BPC + s
            return modv[:, j: j + 1]

        SC_INVX, SC_KG, SC_SGT, SC_BGT, SC_KU, SC_K1, SC_K2, SC_SO = range(8)
        scv = lambda j: scal[:, j: j + 1]

        with ExitStack() as es0:
            sb0 = lambda n, s, dt=F32: es0.enter_context(nc.sbuf_tensor(n, s, dt))
            cfm = sb0("cfm", [128, DC * BPC])
            silc = sb0("silc", [128, DC * BPC])
            wmt = sb0("wmt", [128, 2 * 512])
            modr = sb0("modr", [BPC, 6 * D])
            bcol_raw = sb0("bcol_raw", [128, 16])
            bvr_f = sb0("bvr_f", [1, 1024])
            bor_f = sb0("bor_f", [1, 1024])
            mps = es0.enter_context(nc.psum_tensor("mps", [BPC, 512], F32))
            tps = es0.enter_context(nc.psum_tensor("tps", [128, 96], F32))

            loads = [(idbf, ident_bf), (idf, ident_f), (msk, masks),
                     (onesrf, ones_r_f), (onesrbf, ones_r_bf), (invDc, invD_c_bf),
                     (bvr_f, b_in[2 * D: 3 * D].unsqueeze(0)),
                     (bor_f, b_out.unsqueeze(0))]
            for dst, src in loads:
                C.prod("sync", "p0ld", 16, lambda e, d=dst, s=src: e.dma_start(out=d[:], in_=s))
            C.prod("sync", "p0ld", 16, lambda e: e.dma_start(
                out=cfm[:].rearrange("p (c s) -> p c s", s=BPC),
                in_=c_fm.rearrange("c p s -> p c s"),
                allow_slow_non_contiguous=True))
            C.prod("sync", "p0ld", 16, lambda e: e.dma_start(
                out=bcol_raw[:], in_=b_in[0: 2 * D].rearrange("(c p) -> p c", p=128),
                allow_slow_non_contiguous=True))

            C.wait("vector", "p0ld")
            C.prod("vector", "p0v", 1, lambda e: e.tensor_copy(bv_row[:], bvr_f[:]))
            C.prod("vector", "p0v", 1, lambda e: e.tensor_copy(bo_row[:], bor_f[:]))
            C.prod("vector", "p0v", 1, lambda e: e.tensor_scalar(
                bq_col[:], bcol_raw[:, 0:8], 0.125, None, ALU.mult))
            C.prod("vector", "p0v", 1, lambda e: e.tensor_copy(bk_col[:], bcol_raw[:, 8:16]))
            C.prod("vector", "p0v", 1, lambda e: e.memset(runmax[:], 0.0))
            C.prod("vector", "p0v", 1, lambda e: e.memset(ccrow[:], 0.0))
            C.prod("vector", "p0v", 1, lambda e: e.memset(mgc[:, 0:1], MAGIC))
            C.prod("vector", "p0v", 1, lambda e: e.memset(mgc[:, 1:2], -MAGIC))
            C.prod("vector", "p0v", 1, lambda e: e.memset(maxbuf[:], 0.0))
            C.wait("scalar", "p0ld")
            C.prod("scalar", "p0s", 1, lambda e: e.activation(silc[:], cfm[:], AF.Silu))

            C.wait("tensor", "p0s")
            for j in range(12):
                C.wait("tensor", "p0ev", j)
                for dc in range(DC):
                    i = j * DC + dc
                    slot = i % 2
                    C.wait("sync", "p0mm", max(0, i - 1))
                    C.prod("sync", f"p0wld{slot}", 16, lambda e, j=j, dc=dc, slot=slot: e.dma_start(
                        out=wmt[:, slot * 512: slot * 512 + 512],
                        in_=wmod[dc * 128:(dc + 1) * 128, j * 512:(j + 1) * 512]))
                    C.wait("tensor", f"p0wld{slot}")
                    C.prod("tensor", "p0mm", 1, lambda e, dc=dc, slot=slot: e.matmul(
                        mps[:, 0:512], silc[:, dc * BPC:(dc + 1) * BPC],
                        wmt[:, slot * 512: slot * 512 + 512],
                        start=(dc == 0), stop=(dc == DC - 1)))
                C.wait("vector", "p0mm")
                C.prod("vector", "p0ev", 1, lambda e, j=j: e.tensor_copy(
                    modr[:, j * 512:(j + 1) * 512], mps[:, 0:512]))

            C.wait("tensor", "p0ev")
            for k in range(48):
                C.prod("tensor", "p0tr", 1, lambda e, k=k: e.transpose(
                    tps[:, k * BPC:(k + 1) * BPC],
                    modr[:, k * 128:(k + 1) * 128], idf[0:BPC, 0:BPC]))
            C.wait("vector", "p0tr")
            C.wait("scalar", "p0tr")

            def tcol(v, c):
                k = v * 8 + c
                return tps[:, k * BPC:(k + 1) * BPC]

            for c in range(DC):
                C.prod("vector", "p0d", 1, lambda e, c=c: e.tensor_copy(
                    modv[:, MV_SA * 16 + c * BPC: MV_SA * 16 + (c + 1) * BPC], tcol(0, c)))
                C.prod("vector", "p0d", 1, lambda e, c=c: e.tensor_scalar(
                    modv[:, MV_ONEPCA * 16 + c * BPC: MV_ONEPCA * 16 + (c + 1) * BPC],
                    tcol(1, c), 1.0, None, ALU.add))
                C.prod("vector", "p0d", 1, lambda e, c=c: e.tensor_copy(
                    modv[:, MV_SB * 16 + c * BPC: MV_SB * 16 + (c + 1) * BPC], tcol(3, c)))
                C.prod("vector", "p0d", 1, lambda e, c=c: e.tensor_scalar(
                    modv[:, MV_ONEPCB * 16 + c * BPC: MV_ONEPCB * 16 + (c + 1) * BPC],
                    tcol(4, c), 1.0, None, ALU.add))
                for (vec, mvidx) in ((2, MV_GA), (5, MV_GB)):
                    C.prod("scalar", "p0ga", 1, lambda e, vec=vec, c=c: e.activation(
                        gascr[:, 0:BPC], tcol(vec, c), AF.Abs))
                    C.prod("scalar", "p0ga", 1, lambda e, vec=vec, c=c: e.activation(
                        gascr[:, 2:2 + BPC], tcol(vec, c), AF.Sign))
                    C.wait("vector", "p0ga")
                    C.prodw("vector", "p0gb", 1, lambda e: e.tensor_scalar(
                        gascr[:, 0:BPC], gascr[:, 0:BPC], 0.25, None, ALU.max))
                    C.prod("vector", "p0gb", 1, lambda e, mvidx=mvidx, c=c: e.tensor_tensor(
                        modv[:, mvidx * 16 + c * BPC: mvidx * 16 + (c + 1) * BPC],
                        gascr[:, 0:BPC], gascr[:, 2:2 + BPC], ALU.mult))
                    C.wait("scalar", "p0gb")
            C.run_block("P0")

        with ExitStack() as esA:
            sbA = lambda n, sh, dt=F32: esA.enter_context(nc.sbuf_tensor(n, sh, dt))
            winT = [sbA(f"winT{c}", [128, 3 * D], BF16) for c in range(DC)]
            woutT = [sbA(f"woutT{c}", [128, D], BF16) for c in range(DC)]
            xfm = [sbA(f"xfm{i}", [128, DC * 512]) for i in range(2)]
            hbuf = [sbA(f"hbuf{i}", [128, DC * 512], BF16) for i in range(2)]
            QT = [sbA(f"QT{i}", [128, DC * 512], BF16) for i in range(2)]
            KT = [sbA(f"KT{i}", [128, DC * 512], BF16) for i in range(3)]
            VT = [sbA(f"VT{i}", [128, 1024], BF16) for i in range(9)]
            ctxT = [sbA(f"ctxT{i}", [128, DC * 128], BF16) for i in range(2)]
            xbf = sbA("xbf", [128, 1024], BF16)
            xsq = sbA("xsq", [128, 1024], BF16)
            ubuf = sbA("ubuf", [128, 1024])
            bcsb = sbA("bcsb", [128, 256])
            h2st = [sbA(f"h2st{i}", [128, 1024], BF16) for i in range(2)]
            expb = [sbA(f"expb{i}", [128, 384], BF16) for i in range(2)]
            mexp = [sbA(f"mexp{i}", [128, 384], BF16) for i in range(2)]
            atb = [sbA(f"atb{i}", [128, 384], BF16) for i in range(2)]
            Dt = [sbA(f"Dt{i}", [128, 128], BF16) for i in range(2)]
            nmx = sbA("nmx", [128, 4])
            sume = sbA("sume", [128, 4])

            st_ps = esA.enter_context(nc.psum_tensor("st_ps", [128, 256], F32))
            mm_ps = [esA.enter_context(nc.psum_tensor(f"mm_ps{i}", [128, 512], F32))
                     for i in range(2)]
            sc_ps = [esA.enter_context(nc.psum_tensor(f"sc_ps{i}", [128, 384], F32))
                     for i in range(2)]
            at_ps = [esA.enter_context(nc.psum_tensor(f"at_ps{i}", [128, 384], F32))
                     for i in range(2)]
            cx_ps = esA.enter_context(nc.psum_tensor("cx_ps", [128, 128], F32))

            for _eng in Ctx.ENGINES:
                C.wait(_eng, "p0gb")
                C.wait(_eng, "p0ld")
                C.wait(_eng, "p0v")

            for c in range(DC):
                C.prod("gpsimd", "wina", 16, lambda e, c=c: e.dma_start(
                    out=winT[c][:], in_=w_inT[c * 128:(c + 1) * 128, :]))
            for c in range(DC):
                C.prod("gpsimd", "wouta", 16, lambda e, c=c: e.dma_start(
                    out=woutT[c][:], in_=w_outT[c * 128:(c + 1) * 128, :]))

            tok0 = lambda s, t4: s * T + t4 * 512
            C.counts.setdefault("mmuse", 0)
            qkv_marks = {}

            def acquire_mm_bank():
                bank = C.counts["mmuse"] % 2
                C.wait("tensor", f"mmfree{bank}", C.val(f"mmfree{bank}"))
                C.counts["mmuse"] += 1
                return bank

            def emit_ln(s, b, second):
                t4, b4 = b // 4, b % 4
                gt4 = s * NT + t4
                slot, r2 = gt4 % 2, b % 2
                xin = (xfm[slot].ap()
                       .rearrange("p (c w) -> p c w", w=512)[:, :, b4 * 128:(b4 + 1) * 128])
                C.wait("vector", "tst", C.val("tst"))
                C.prod("vector", "vxbf", 1, lambda e, xin=xin: e.tensor_copy(
                    xbf[:].rearrange("p (c w) -> p c w", w=128), xin))
                C.wait("scalar", "tst", C.val("tst"))
                C.prod("scalar", "sxsq", 1, lambda e, xin=xin: e.activation(
                    xsq[:].rearrange("p (c w) -> p c w", w=128), xin, AF.Square))
                C.wait("tensor", "vbc", C.val("vbc"))
                C.wait("tensor", "vxbf")
                C.wait("tensor", "sxsq")
                for dc in range(DC):
                    C.prod("tensor", "tst", 1, lambda e, dc=dc: e.matmul(
                        st_ps[0:1, 0:128], invDc[:], xbf[:, dc * 128:(dc + 1) * 128],
                        start=(dc == 0), stop=False))
                for dc in range(DC):
                    C.prod("tensor", "tst", 1, lambda e, dc=dc: e.matmul(
                        st_ps[0:1, 128:256], invDc[:], xsq[:, dc * 128:(dc + 1) * 128],
                        start=False, stop=(dc == DC - 1)))
                C.wait("vector", "tst")
                C.prodw("vector", "vst", 1, lambda e: e.tensor_copy(rows[0:1, 0:256], st_ps[0:1, 0:256]))
                C.prodw("vector", "vst", 1, lambda e: e.tensor_tensor(
                    rows[0:1, 256:384], rows[0:1, 0:128], rows[0:1, 0:128], ALU.mult))
                C.prodw("vector", "vst", 1, lambda e: e.tensor_tensor(
                    rows[0:1, 256:384], rows[0:1, 128:256], rows[0:1, 256:384], ALU.subtract))
                C.prodw("vector", "vst", 1, lambda e: e.tensor_scalar(
                    rows[0:1, 256:384], rows[0:1, 256:384], 1e-5, None, ALU.add))
                C.wait("scalar", "vst")
                C.prodw("scalar", "sst", 1, lambda e: e.activation(
                    rows[0:1, 384:512], rows[0:1, 256:384], AF.Ln))
                C.prodw("scalar", "sst", 1, lambda e: e.activation(
                    rows[0:1, 384:512], rows[0:1, 384:512], AF.Exp, scale=-0.5))
                C.wait("vector", "sst")
                C.prodw("vector", "vst2", 1, lambda e: e.tensor_tensor(
                    rows[0:1, 256:384], rows[0:1, 0:128], rows[0:1, 384:512], ALU.mult))
                C.prodw("vector", "vst2", 1, lambda e: e.tensor_scalar(
                    rows[0:1, 256:384], rows[0:1, 256:384], -1.0, None, ALU.mult))
                C.wait("tensor", "vst2")
                C.prod("tensor", "tbc", 1, lambda e: e.matmul(
                    st_ps[:, 0:128], onesrf[:], rows[0:1, 384:512], start=True, stop=False))
                C.prod("tensor", "tbc", 1, lambda e: e.matmul(
                    st_ps[:, 128:256], onesrf[:], rows[0:1, 256:384], start=False, stop=True))
                C.wait("vector", "tbc")
                C.wait("vector", "gu", C.val("gu"))
                C.prod("vector", "vbc", 1, lambda e: e.tensor_copy(bcsb[:], st_ps[:]))
                C.wait("gpsimd", "vbc")
                C.wait("gpsimd", "vh", C.val("vh"))
                C.wait("gpsimd", "vh2", C.val("vh2"))
                a_bc = bcsb[:, 0:128].unsqueeze(1).broadcast_to([128, 8, 128])
                b_bc = bcsb[:, 128:256].unsqueeze(1).broadcast_to([128, 8, 128])
                C.prod("gpsimd", "gu", 1, lambda e, xin=xin, a_bc=a_bc: e.tensor_tensor(
                    ubuf[:].rearrange("p (c w) -> p c w", w=128), xin, a_bc, ALU.mult))
                C.prod("gpsimd", "gu", 1, lambda e, b_bc=b_bc: e.tensor_tensor(
                    ubuf[:].rearrange("p (c w) -> p c w", w=128),
                    ubuf[:].rearrange("p (c w) -> p c w", w=128), b_bc, ALU.add))
                C.wait("vector", "gu")
                if second:
                    C.wait("vector", f"h2sp{r2}", C.val(f"h2sp{r2}"))
                for c in range(DC):
                    u2c = ubuf[:, c * 128:(c + 1) * 128]
                    if not second:
                        hdst = hbuf[slot][:, c * 512 + b4 * 128: c * 512 + b4 * 128 + 128]
                        C.prod("vector", "vh", 1, lambda e, u2c=u2c, hdst=hdst, c=c, s=s:
                               e.tensor_scalar(hdst, u2c, mv(MV_ONEPCA, c, s),
                                               mv(MV_SA, c, s), ALU.mult, ALU.add))
                    else:
                        hdst = h2st[r2][:, c * 128:(c + 1) * 128]
                        C.prod("vector", "vh2", 1, lambda e, u2c=u2c, hdst=hdst, c=c, s=s:
                               e.tensor_scalar(hdst, u2c, mv(MV_ONEPCB, c, s),
                                               mv(MV_SB, c, s), ALU.mult, ALU.add))
                if not second:
                    C.wait("vector", "vh")
                    for c in range(DC):
                        hdst = hbuf[slot][:, c * 512 + b4 * 128: c * 512 + b4 * 128 + 128]
                        xdst = xfm[slot][:, c * 512 + b4 * 128: c * 512 + b4 * 128 + 128]
                        C.prod("vector", "vx1", 1, lambda e, hdst=hdst, xdst=xdst, c=c, s=s:
                               e.scalar_tensor_tensor(xdst, hdst, mv(MV_GA, c, s), xdst,
                                                      ALU.mult, ALU.add))
                else:
                    C.wait("vector", "vh2")
                    mcol = s * NBLK + b
                    C.prod("vector", "vh2m", 1, lambda e, r2=r2, mcol=mcol: e.tensor_reduce(
                        maxbuf[:, mcol:mcol + 1], h2st[r2][:], AX.X, ALU.max,
                        apply_absolute_value=True))
                    C.wait("gpsimd", "vh2", C.val("vh2"))
                    tk = s * T + b * 128
                    for c in range(DC):
                        C.prod("gpsimd", f"h2sp{r2}", 16, lambda e, c=c, tk=tk, r2=r2: e.dma_start(
                            out=h2_h[c, :, tk: tk + 128], in_=h2st[r2][:, c * 128:(c + 1) * 128]))

            def qkv_groups(s, t4):
                gt4 = s * NT + t4
                slot = gt4 % 2

                def qk_group(ec):
                    def go():
                        bank = acquire_mm_bank()
                        for dc in range(DC):
                            C.prod("tensor", "tmm", 1, lambda e, ec=ec, dc=dc, bank=bank,
                                   slot=slot: e.matmul(
                                       mm_ps[bank][:],
                                       winT[dc][:, ec * 128:(ec + 1) * 128],
                                       hbuf[slot][:, dc * 512:(dc + 1) * 512],
                                       start=(dc == 0), stop=(dc == DC - 1)))
                        C.wait("scalar", "tmm")
                        if ec < 8:
                            dst = QT[slot][:, ec * 512:(ec + 1) * 512]
                            bias, scale = bq_col[:, ec:ec + 1], 0.125
                        else:
                            dst = KT[gt4 % 3][:, (ec - 8) * 512:(ec - 7) * 512]
                            bias, scale = bk_col[:, ec - 8:ec - 7], 1.0
                        C.prod("scalar", f"mmfree{bank}", 1, lambda e, dst=dst, bias=bias,
                               scale=scale, bank=bank: e.activation(
                                   dst, mm_ps[bank][:], AF.Identity, bias=bias, scale=scale))
                    return go

                def v_group(b4, g):
                    def go():
                        b = t4 * 4 + b4
                        gb = s * NBLK + b
                        bank = acquire_mm_bank()
                        for dc in range(DC):
                            C.prod("tensor", "tmm", 1, lambda e, b4=b4, g=g, dc=dc, bank=bank,
                                   slot=slot: e.matmul(
                                       mm_ps[bank][:],
                                       hbuf[slot][:, dc * 512 + b4 * 128: dc * 512 + b4 * 128 + 128],
                                       winT[dc][:, 2 * D + g * 512: 2 * D + (g + 1) * 512],
                                       start=(dc == 0), stop=False))
                        C.prod("tensor", "tmm", 1, lambda e, g=g, bank=bank: e.matmul(
                            mm_ps[bank][:], onesrbf[0:1, 0:128],
                            bv_row[0:1, g * 512:(g + 1) * 512], start=False, stop=True))
                        C.wait("scalar", "tmm")
                        C.prod("scalar", f"mmfree{bank}", 1, lambda e, gb=gb, g=g, bank=bank:
                               e.activation(VT[gb % 9][:, g * 512:(g + 1) * 512],
                                            mm_ps[bank][:], AF.Identity))
                    return go

                C.wait("tensor", "wina", 16 * DC)
                C.wait("tensor", "vh", C.val("vh"))
                for ec in range(8, 16):
                    yield qk_group(ec)
                for g in range(2):
                    yield v_group(0, g)
                for ec in range(8):
                    yield qk_group(ec)
                for b4 in range(1, 4):
                    for g in range(2):
                        yield v_group(b4, g)

            attmk = {}

            def emit_attention(s, b, filler):
                t4 = b // 4
                gt4 = s * NT + t4
                kb0 = min(max(b - 1, 0), NBLK - 3)
                mvar = 1 if b == 0 else (2 if b == NBLK - 1 else 0)
                C.wait("tensor", "mmfree0", C.val("mmfree0"))
                C.wait("tensor", "mmfree1", C.val("mmfree1"))
                G0 = C.ctr("hctr")

                def St(h):
                    G = G0 + h
                    sl = G % 2
                    hc, hr = h // 2, (h % 2) * 64
                    if ("se", G - 2) in attmk:
                        C.wait("tensor", "sexp", attmk[("se", G - 2)])
                    qap = QT[gt4 % 2][hr:hr + 64,
                                      hc * 512 + (b % 4) * 128: hc * 512 + (b % 4) * 128 + 128]
                    for kk in range(3):
                        kb = kb0 + kk
                        kap = KT[(s * NT + kb // 4) % 3][hr:hr + 64,
                                                        hc * 512 + (kb % 4) * 128: hc * 512 + (kb % 4) * 128 + 128]
                        C.prod("tensor", "tsc", 1, lambda e, sl=sl, kk=kk, qap=qap, kap=kap:
                               e.matmul(sc_ps[sl][:, kk * 128:(kk + 1) * 128], qap, kap,
                                        start=(kk == 0), stop=(kk == 2)))
                    attmk[("st", G)] = C.val("tsc")

                def Vn(h):
                    G = G0 + h
                    sl = G % 2
                    C.wait("vector", "tsc", attmk[("st", G)])
                    C.prod("vector", "vnm", 1, lambda e, sl=sl: e.tensor_reduce(
                        nmx[:, sl:sl + 1], sc_ps[sl][:], AX.X, ALU.max, negate=True))
                    attmk[("vn", G)] = C.val("vnm")

                def Se(h):
                    G = G0 + h
                    sl = G % 2
                    C.wait("scalar", "vnm", attmk[("vn", G)])
                    if ("ta", G - 2) in attmk:
                        C.wait("scalar", "tat", attmk[("ta", G - 2)])
                    C.prod("scalar", "sexp", 1, lambda e, sl=sl: e.activation(
                        expb[sl][:], sc_ps[sl][:], AF.Exp, bias=nmx[:, sl:sl + 1]))
                    attmk[("se", G)] = C.val("sexp")

                def Vd(h):
                    G = G0 + h
                    sl = G % 2
                    C.wait("vector", "sexp", attmk[("se", G)])
                    if ("ta", G - 2) in attmk:
                        C.wait("vector", "tat", attmk[("ta", G - 2)])
                    C.prod("vector", "vmk", 1, lambda e, sl=sl, mvar=mvar: e.scalar_tensor_tensor(
                        mexp[sl][:], expb[sl][:], 1.0, msk[:, mvar * 384:(mvar + 1) * 384],
                        ALU.mult, ALU.mult, accum_out=sume[:, sl:sl + 1]))
                    C.wait("vector", "vmk")
                    C.prodw("vector", "vrc", 1, lambda e, sl=sl: e.reciprocal(
                        sume[:, sl + 2:sl + 3], sume[:, sl:sl + 1]))
                    C.prod("vector", "vD", 1, lambda e, sl=sl: e.tensor_scalar(
                        Dt[sl][:], idbf[:], sume[:, sl + 2:sl + 3], None, ALU.mult))
                    attmk[("vd", G)] = C.val("vD")

                def Ta(h):
                    G = G0 + h
                    sl = G % 2
                    C.wait("tensor", "vD", attmk[("vd", G)])
                    if ("va", G - 2) in attmk:
                        C.wait("tensor", "vat", attmk[("va", G - 2)])
                    for kk in range(3):
                        C.prod("tensor", "tat", 1, lambda e, sl=sl, kk=kk: e.matmul(
                            at_ps[sl][:, kk * 128:(kk + 1) * 128],
                            mexp[sl][:, kk * 128:(kk + 1) * 128], Dt[sl][:],
                            start=(kk == 0), stop=(kk == 2)))
                    attmk[("ta", G)] = C.val("tat")

                def Va(h):
                    G = G0 + h
                    sl = G % 2
                    C.wait("vector", "tat", attmk[("ta", G)])
                    if ("tc", G - 2) in attmk:
                        C.wait("vector", "tcx", attmk[("tc", G - 2)])
                    C.prod("vector", "vat", 1, lambda e, sl=sl: e.tensor_copy(
                        atb[sl][:], at_ps[sl][:]))
                    attmk[("va", G)] = C.val("vat")

                def Tc(h):
                    G = G0 + h
                    sl = G % 2
                    hr = (h % 2) * 64
                    C.wait("tensor", "vat", attmk[("va", G)])
                    if h % 2 == 0 and ("vc", G - 1) in attmk:
                        C.wait("tensor", "vcx", attmk[("vc", G - 1)])
                    for kk in range(3):
                        C.prod("tensor", "tcx", 1, lambda e, kk=kk, h=h, hr=hr, kb0=kb0, sl=sl,
                               s=s: e.matmul(
                                   cx_ps[hr:hr + 64, :],
                                   VT[(s * NBLK + kb0 + kk) % 9][:, h * 64:(h + 1) * 64],
                                   atb[sl][:, kk * 128:(kk + 1) * 128],
                                   start=(kk == 0), stop=(kk == 2)))
                    attmk[("tc", G)] = C.val("tcx")

                def Vc(h):
                    G = G0 + h
                    C.wait("vector", "tcx", attmk[("tc", G)])
                    C.prod("vector", "vcx", 1, lambda e, h=h, b=b: e.tensor_copy(
                        ctxT[b % 2][:, (h // 2) * 128:(h // 2 + 1) * 128], cx_ps[:]))
                    attmk[("vc", G)] = C.val("vcx")
                    attmk[("vc", G - 1)] = C.val("vcx")

                for i in range(NH + 2):
                    if i < NH:
                        St(i)
                    if 0 <= i - 1 < NH:
                        Vn(i - 1)
                        Se(i - 1)
                    if 0 <= i - 2 < NH:
                        h = i - 2
                        Vd(h)
                        Ta(h)
                        Va(h)
                        Tc(h)
                        if h % 2 == 1:
                            Vc(h)
                    filler()
                for h in range(NH):
                    C.ctr_inc("hctr")

            def emit_proj(s, b):
                t4, b4 = b // 4, b % 4
                slot = (s * NT + t4) % 2
                C.wait("tensor", "wouta", 16 * DC)
                C.wait("tensor", "vcx", C.val("vcx"))
                for dc in range(DC):
                    bank = acquire_mm_bank()
                    for ec in range(DC):
                        C.prod("tensor", "tmm", 1, lambda e, dc=dc, ec=ec, bank=bank, b=b:
                               e.matmul(mm_ps[bank][:, 0:128],
                                        woutT[ec][:, dc * 128:(dc + 1) * 128],
                                        ctxT[b % 2][:, ec * 128:(ec + 1) * 128],
                                        start=(ec == 0), stop=False))
                    C.prod("tensor", "tmm", 1, lambda e, dc=dc, bank=bank: e.matmul(
                        mm_ps[bank][:, 0:128], bo_row[0:1, dc * 128:(dc + 1) * 128],
                        onesrbf[0:1, 0:128], start=False, stop=True))
                    C.wait("vector", "tmm")
                    xdst = xfm[slot][:, dc * 512 + b4 * 128: dc * 512 + b4 * 128 + 128]
                    C.prod("vector", f"mmfree{bank}", 1, lambda e, dc=dc, xdst=xdst, bank=bank,
                           s=s: e.scalar_tensor_tensor(xdst, mm_ps[bank][:, 0:128],
                                                       mv(MV_GA, dc, s), xdst, ALU.mult, ALU.add))

            def spill_x1(s, t4):
                slot = (s * NT + t4) % 2
                C.wait("gpsimd", "mmfree0", C.val("mmfree0"))
                C.wait("gpsimd", "mmfree1", C.val("mmfree1"))
                for c in range(DC):
                    C.prod("gpsimd", f"x1sp{slot}", 16, lambda e, c=c, s=s, t4=t4, slot=slot:
                           e.dma_start(out=x1_h[c, :, tok0(s, t4): tok0(s, t4) + 512],
                                       in_=xfm[slot][:, c * 512:(c + 1) * 512]))

            def process_block(s, b, filler):
                emit_attention(s, b, filler)
                emit_proj(s, b)
                emit_ln(s, b, second=True)
                if b % 4 == 3:
                    spill_x1(s, b // 4)

            pending = []
            for s in range(BPC):
                for t4 in range(NT):
                    gt4 = s * NT + t4
                    slot = gt4 % 2
                    C.wait("sync", f"x1sp{slot}", C.val(f"x1sp{slot}"))
                    C.wait("sync", "vh2m", C.val("vh2m"))
                    for c in range(DC):
                        C.prod("sync", f"qx{slot}", 16, lambda e, c=c, s=s, t4=t4, slot=slot:
                               e.dma_start(out=xfm[slot][:, c * 512:(c + 1) * 512],
                                           in_=x_fm[c, :, tok0(s, t4): tok0(s, t4) + 512]))
                    C.wait("vector", f"qx{slot}")
                    C.wait("scalar", f"qx{slot}")
                    C.wait("gpsimd", f"qx{slot}")
                    for b4 in range(4):
                        emit_ln(s, t4 * 4 + b4, second=False)
                    work = qkv_groups(s, t4)

                    def filler():
                        g = next(work, None)
                        if g is not None:
                            g()

                    for (ps, pb) in pending:
                        process_block(ps, pb, filler)
                    for g in work:
                        g()
                    pending = [(s, t4 * 4 + b4) for b4 in range(4)]
            for (ps, pb) in pending:
                process_block(ps, pb, lambda: None)
            C.run_block("A")

        with ExitStack() as esR:
            r_ps = esR.enter_context(nc.psum_tensor("r_ps", [1, 128], F32))
            rb_ps = esR.enter_context(nc.psum_tensor("rb_ps", [128, 16], F32))

            C.wait("vector", "vh2m", C.val("vh2m"))
            C.prodw("vector", "vh2m", 1, lambda e: e.tensor_reduce(
                runmax[:, 0:1], maxbuf[:, 0:BPC * NBLK], AX.X, ALU.max))
            C.wait("tensor", "vh2m", C.val("vh2m"))
            C.prod("tensor", "tarX", 1, lambda e: e.matmul(
                r_ps[:], runmax[:, 0:1], idf[:], start=True, stop=True))
            C.wait("vector", "tarX")
            C.prodw("vector", "varX", 1, lambda e: e.tensor_reduce(
                ccrow[0:1, 0:1], r_ps[:], AX.X, ALU.max))
            C.wait("gpsimd", "varX")
            C.prod("gpsimd", "ccwX", 16, lambda e: e.dma_start(out=cc_in.ap(), in_=ccrow[:]))
            C.wait("gpsimd", "ccwX")
            C.prod("gpsimd", "arQ", 1, lambda e: e.collective_compute(
                "AllReduce", ALU.max, replica_groups=[list(range(NCORE))],
                ins=[cc_in.ap()], outs=[cc_out.ap()]))
            C.wait("gpsimd", "arQ")
            C.prod("gpsimd", "ccrX", 16, lambda e: e.dma_start(out=ccrow[:], in_=cc_out.ap()))
            C.wait("vector", "ccrX")
            C.prodw("vector", "ardX", 1, lambda e: e.reciprocal(rows[0:1, 0:1], ccrow[0:1, 0:1]))
            C.prodw("vector", "ardX", 1, lambda e: e.tensor_scalar(
                rows[0:1, 0:1], rows[0:1, 0:1], 32767.0, None, ALU.mult))
            C.wait("tensor", "ardX")
            C.prod("tensor", "trbX", 1, lambda e: e.matmul(
                rb_ps[:, 0:1], onesrf[:], rows[0:1, 0:1], start=True, stop=True))
            C.wait("vector", "trbX")
            C.prod("vector", "vrbX", 1, lambda e: e.tensor_copy(scv(SC_INVX), rb_ps[:, 0:1]))
            if dbg:
                C.wait("gpsimd", "vrbX")
                C.prod("gpsimd", "dbgsc", 16, lambda e: e.dma_start(out=scal_h, in_=ccrow[:]))
            C.run_block("AR1")

        MG = MAGIC
        with ExitStack() as esF:
            sbF = lambda n, sh, dt=F32: esF.enter_context(nc.sbuf_tensor(n, sh, dt))
            wdT = [sbF(f"wdT{h}", [128, D], BF16) for h in range(HC)]

            with ExitStack() as esB:
                sbB = lambda n, sh, dt=F32: esB.enter_context(nc.sbuf_tensor(n, sh, dt))
                Rb = [sbB(f"Rb{c}", [128, TOK], BF16) for c in range(DC)]
                wraw = [sbB(f"wraw{i}", [128, D]) for i in range(2)]
                wtmp = sbB("wtmp", [128, D])
                wq32 = sbB("wq32", [128, D])
                blkmx = sbB("blkmx", [128, 32])
                binv = sbB("binv", [128, 32])
                bscl = sbB("bscl", [128, 32])
                wstr = [sbB(f"wstr{i}", [128, DC * 128], BF16) for i in range(2)]
                qtmp = [sbB(f"qtmp{i}", [128, 1024]) for i in range(2)]
                spst = [sbB(f"spst{i}", [128, TOK], BF16) for i in range(2)]
                mmB = [esB.enter_context(nc.psum_tensor(f"mmB{i}", [128, 512], F32))
                       for i in range(2)]
                wtps = esB.enter_context(nc.psum_tensor("wtps", [128, 1024], F32))

                for _eng in Ctx.ENGINES:
                    for nm in ("x1sp0", "x1sp1", "h2sp0", "h2sp1"):
                        C.wait(_eng, nm)
                    C.wait(_eng, "vrbX")

                h2in = [esB.enter_context(nc.sbuf_tensor(f"h2in{i}", [128, 1024], BF16))
                        for i in range(2)]
                for dc in range(DC):
                    for half in range(4):
                        hw = 1024
                        r2 = (dc * 4 + half) % 2
                        C.wait("sync", "b0q", max(0, C.val("b0q") - 1))
                        C.prod("sync", f"ldA{r2}", 16, lambda e, dc=dc, half=half, r2=r2:
                               e.dma_start(out=h2in[r2][:],
                                           in_=h2_h[dc, :, half * hw:(half + 1) * hw]))
                        C.wait("scalar", f"ldA{r2}")
                        C.wait("scalar", "b0q", max(0, C.val("b0q") - 1))
                        C.prod("scalar", "b0s", 1, lambda e, r2=r2: e.activation(
                            qtmp[r2][:], h2in[r2][:], AF.Identity, bias=mgc[:, 0:1], scale=scv(SC_INVX)))
                        C.wait("vector", "b0s")
                        C.prod("vector", "b0q", 1, lambda e, dc=dc, half=half, r2=r2:
                               e.tensor_scalar(Rb[dc][:, half * hw:(half + 1) * hw],
                                               qtmp[r2][:], -MG, None, ALU.add))
                C.wait("tensor", "b0q", C.val("b0q"))
                bqm_marks = []
                bdq_marks = []
                for wi, (wsrc, wdst, rmcol) in enumerate(
                        ((w_up, upr_h, 1), (w_gate, gtr_h, 2))):
                    for hc in range(HC):
                        wsl = (wi * HC + hc) % 2
                        C.wait("sync", "bdq", bdq_marks[-2] if len(bdq_marks) >= 2 else 0)
                        C.prod("sync", f"ldA{wsl}", 16, lambda e, wsrc=wsrc, hc=hc, wsl=wsl:
                               e.dma_start(out=wraw[wsl][:], in_=wsrc[hc * 128:(hc + 1) * 128, :]))
                        C.wait("vector", f"ldA{wsl}")
                        C.prodw("vector", "bdq", 1, lambda e, wsl=wsl: e.tensor_reduce(
                            blkmx[:], wraw[wsl][:].rearrange("p (a b) -> p a b", b=32),
                            AX.X, ALU.max, apply_absolute_value=True))
                        C.prodw("vector", "bdq", 1, lambda e: e.tensor_scalar(
                            bscl[:], blkmx[:], 1.0 / 7.0, 1e-12, ALU.mult, ALU.max))
                        C.prodw("vector", "bdq", 1, lambda e: e.reciprocal(binv[:], bscl[:]))
                        C.prod("vector", "bdq", 1, lambda e, wsl=wsl: e.tensor_tensor(
                            wtmp[:].rearrange("p (a b) -> p a b", b=32),
                            wraw[wsl][:].rearrange("p (a b) -> p a b", b=32),
                            binv[:].unsqueeze(2).broadcast_to([128, 32, 32]), ALU.mult))
                        bdq_marks.append(C.val("bdq"))
                        C.prod("vector", "bdq", 1, lambda e: e.tensor_scalar(
                            wtmp[:], wtmp[:], MG, -MG, ALU.add, ALU.add))
                        C.prod("vector", "bdq", 1, lambda e: e.tensor_tensor(
                            wq32[:].rearrange("p (a b) -> p a b", b=32),
                            wtmp[:].rearrange("p (a b) -> p a b", b=32),
                            bscl[:].unsqueeze(2).broadcast_to([128, 32, 32]), ALU.mult))
                        C.wait("tensor", "bdq")
                        C.wait("tensor", "bst", C.val("bst"))
                        for dc in range(DC):
                            C.prod("tensor", "btr", 1, lambda e, dc=dc: e.transpose(
                                wtps[:, dc * 128:(dc + 1) * 128], wq32[:, dc * 128:(dc + 1) * 128],
                                idf[:]))
                        C.wait("scalar", "btr")
                        C.wait("scalar", "bqm", bqm_marks[-2] if len(bqm_marks) >= 2 else 0)
                        C.prod("scalar", "bst", 1, lambda e, wsl=wsl: e.copy(
                            wstr[wsl][:], wtps[:]))
                        C.wait("tensor", "bst", C.val("bst"))
                        ssl = C.ctr("bse") % 2
                        C.ctr_inc("bse")
                        C.wait("scalar", f"sd{ssl}", C.val(f"sd{ssl}"))
                        for tt in range(NTT):
                            bank = C.counts["mmuse"] % 2
                            C.wait("tensor", f"mmfree{bank}", C.val(f"mmfree{bank}"))
                            C.counts["mmuse"] += 1
                            for dc in range(DC):
                                C.prod("tensor", "bqm", 1, lambda e, dc=dc, tt=tt, bank=bank,
                                       wsl=wsl: e.matmul(
                                           mmB[bank][:],
                                           wstr[wsl][:, dc * 128:(dc + 1) * 128],
                                           Rb[dc][:, tt * 512:(tt + 1) * 512],
                                           start=(dc == 0), stop=(dc == DC - 1)))
                            C.wait("scalar", "bqm")
                            C.prod("scalar", f"mmfree{bank}", 1, lambda e, bank=bank, ssl=ssl,
                                   tt=tt: e.copy(spst[ssl][:, tt * 512:(tt + 1) * 512],
                                                 mmB[bank][:]))
                            C.wait("vector", f"mmfree{bank}")
                            bcol = wi * 192 + hc * 8 + tt
                            C.prod("vector", "bsmx0", 1, lambda e, ssl=ssl, tt=tt, bcol=bcol:
                                   e.tensor_reduce(maxbuf[:, bcol:bcol + 1],
                                                   spst[ssl][:, tt * 512:(tt + 1) * 512],
                                                   AX.X, ALU.max, apply_absolute_value=True))
                        C.wait("gpsimd", "mmfree0", C.val("mmfree0"))
                        C.wait("gpsimd", "mmfree1", C.val(f"mmfree1"))
                        C.prod("gpsimd", f"sd{ssl}", 16, lambda e, wdst=wdst, hc=hc, ssl=ssl:
                               e.dma_start(out=wdst[hc, :, :], in_=spst[ssl][:]))
                        bqm_marks.append(C.val("bqm"))
            with ExitStack() as esR2:
                r_ps2 = esR2.enter_context(nc.psum_tensor("r_ps2", [1, 128], F32))
                rb_ps2 = esR2.enter_context(nc.psum_tensor("rb_ps2", [128, 16], F32))
                C.wait("vector", "bsmx0", C.val("bsmx0"))
                C.prod("vector", "bvmx", 1, lambda e: e.tensor_reduce(
                    runmax[:, 1:2], maxbuf[:, 0:192], AX.X, ALU.max))
                C.prodw("vector", "bvmx", 1, lambda e: e.tensor_reduce(
                    runmax[:, 2:3], maxbuf[:, 192:384], AX.X, ALU.max))
                C.prod("vector", "bvmx", 1, lambda e: e.memset(maxbuf[:], 0.0))
                C.wait("tensor", "bvmx", C.val("bvmx"))
                for j, col in enumerate((1, 2)):
                    C.wait("tensor", "varX", C.val("varX"))
                    C.prod("tensor", "tarX", 1, lambda e, col=col: e.matmul(
                        r_ps2[:], runmax[:, col:col + 1], idf[:], start=True, stop=True))
                    C.wait("vector", "tarX")
                    C.prodw("vector", "varX", 1, lambda e, j=j: e.tensor_reduce(
                        ccrow[0:1, j:j + 1], r_ps2[:], AX.X, ALU.max))
                C.wait("gpsimd", "varX")
                C.prod("gpsimd", "ccwX", 16, lambda e: e.dma_start(out=cc_in.ap(), in_=ccrow[:]))
                C.wait("gpsimd", "ccwX")
                C.prod("gpsimd", "arQ", 1, lambda e: e.collective_compute(
                    "AllReduce", ALU.max, replica_groups=[list(range(NCORE))],
                    ins=[cc_in.ap()], outs=[cc_out.ap()]))
                C.wait("gpsimd", "arQ")
                C.prod("gpsimd", "ccrX", 16, lambda e: e.dma_start(out=ccrow[:], in_=cc_out.ap()))
                C.wait("vector", "ccrX")
                C.prodw("vector", "ardX", 1, lambda e: e.reciprocal(rows[0:1, 21:22], scal[0:1, SC_INVX:SC_INVX + 1]))
                C.prodw("vector", "ardX", 1, lambda e: e.tensor_tensor(
                    rows[0:1, 16:17], ccrow[0:1, 0:1], rows[0:1, 21:22], ALU.mult))
                C.prodw("vector", "ardX", 1, lambda e: e.tensor_scalar(
                    rows[0:1, 16:17], rows[0:1, 16:17], 1.0 / 32767.0, None, ALU.mult))
                C.prodw("vector", "ardX", 1, lambda e: e.tensor_tensor(
                    rows[0:1, 17:18], ccrow[0:1, 1:2], rows[0:1, 21:22], ALU.mult))
                C.prodw("vector", "ardX", 1, lambda e: e.tensor_scalar(
                    rows[0:1, 17:18], rows[0:1, 17:18], 1.0 / 32767.0, None, ALU.mult))
                C.prodw("vector", "ardX", 1, lambda e: e.reciprocal(rows[0:1, 18:19], ccrow[0:1, 1:2]))
                C.prodw("vector", "ardX", 1, lambda e: e.tensor_scalar(
                    rows[0:1, 18:19], rows[0:1, 18:19], 32767.0, None, ALU.mult))
                C.prodw("vector", "ardX", 1, lambda e: e.reciprocal(rows[0:1, 19:20], ccrow[0:1, 0:1]))
                C.prodw("vector", "ardX", 1, lambda e: e.tensor_scalar(
                    rows[0:1, 19:20], rows[0:1, 19:20], 32767.0, None, ALU.mult))
                C.prodw("vector", "ardX", 1, lambda e: e.tensor_scalar(
                    rows[0:1, 20:21], rows[0:1, 17:18], -MG, None, ALU.mult))
                C.wait("tensor", "ardX")
                C.prod("tensor", "trbX", 1, lambda e: e.matmul(
                    rb_ps2[:, 0:4], onesrf[:], rows[0:1, 17:21], start=True, stop=True))
                C.wait("vector", "trbX")
                C.prod("vector", "vrbX", 1, lambda e: e.tensor_copy(scal[:, SC_SGT:SC_SGT + 1], rb_ps2[:, 0:1]))
                C.prod("vector", "vrbX", 1, lambda e: e.tensor_copy(scal[:, SC_KG:SC_KG + 1], rb_ps2[:, 1:2]))
                C.prod("vector", "vrbX", 1, lambda e: e.tensor_copy(scal[:, SC_KU:SC_KU + 1], rb_ps2[:, 2:3]))
                C.prodw("vector", "vrbX", 1, lambda e: e.tensor_copy(scal[:, SC_BGT:SC_BGT + 1], rb_ps2[:, 3:4]))
                C.run_block() if False else None

            with ExitStack() as esC:
                sbC = lambda n, sh, dt=F32: esC.enter_context(nc.sbuf_tensor(n, sh, dt))
                gg = [sbC(f"gg{i}", [128, 512]) for i in range(2)]
                sg = [sbC(f"sg{i}", [128, 512]) for i in range(2)]
                t2b = [sbC(f"t2b{i}", [128, 512]) for i in range(2)]
                s2c = [sbC(f"s2c{i}", [128, 512]) for i in range(2)]
                wrawC = [sbC(f"wrawC{i}", [128, HID]) for i in range(2)]
                wtmpC = sbC("wtmpC", [128, HID])
                wqC = sbC("wqC", [128, HID])
                bmxC = sbC("bmxC", [128, 96])
                bivC = sbC("bivC", [128, 96])
                bscC = sbC("bscC", [128, 96])
                wtpsC = esC.enter_context(nc.psum_tensor("wtpsC", [128, 1024], F32))

                for _eng in Ctx.ENGINES:
                    C.wait(_eng, "vrbX")
                    C.wait(_eng, "sd0"); C.wait(_eng, "sd1")
                for rc in range(DC):
                    wsl = rc % 2
                    C.wait("sync", "cdqv", max(0, C.val("cdqv") - 1))
                    C.prod("sync", f"ldA{wsl}", 16, lambda e, rc=rc, wsl=wsl: e.dma_start(
                        out=wrawC[wsl][:], in_=w_down[rc * 128:(rc + 1) * 128, :]))
                    C.wait("vector", f"ldA{wsl}")
                    C.prodw("vector", "cdq", 1, lambda e, wsl=wsl: e.tensor_reduce(
                        bmxC[:], wrawC[wsl][:].rearrange("p (a b) -> p a b", b=32),
                        AX.X, ALU.max, apply_absolute_value=True))
                    C.prodw("vector", "cdq", 1, lambda e: e.tensor_scalar(
                        bscC[:], bmxC[:], 1.0 / 7.0, 1e-12, ALU.mult, ALU.max))
                    C.prodw("vector", "cdq", 1, lambda e: e.reciprocal(bivC[:], bscC[:]))
                    C.prod("vector", "cdq", 1, lambda e, wsl=wsl: e.tensor_tensor(
                        wtmpC[:].rearrange("p (a b) -> p a b", b=32),
                        wrawC[wsl][:].rearrange("p (a b) -> p a b", b=32),
                        bivC[:].unsqueeze(2).broadcast_to([128, 96, 32]), ALU.mult))
                    C.prod("vector", "cdq", 1, lambda e: e.tensor_scalar(
                        wtmpC[:], wtmpC[:], MG, -MG, ALU.add, ALU.add))
                    C.prod("vector", "cdqv", 1, lambda e: e.tensor_tensor(
                        wqC[:].rearrange("p (a b) -> p a b", b=32),
                        wtmpC[:].rearrange("p (a b) -> p a b", b=32),
                        bscC[:].unsqueeze(2).broadcast_to([128, 96, 32]), ALU.mult))
                    C.wait("tensor", "cdqv")
                    for grp in range(HC // 8):
                        C.wait("tensor", "cst", C.val("cst"))
                        for bb in range(8):
                            hh = grp * 8 + bb
                            C.prod("tensor", "ctr", 1, lambda e, hh=hh, bb=bb: e.transpose(
                                wtpsC[:, bb * 128:(bb + 1) * 128],
                                wqC[:, hh * 128:(hh + 1) * 128], idf[:]))
                        C.wait("scalar", "ctr")
                        for bb in range(8):
                            hh = grp * 8 + bb
                            C.prod("scalar", "cst", 1, lambda e, hh=hh, bb=bb, rc=rc: e.copy(
                                wdT[hh][:, rc * 128:(rc + 1) * 128],
                                wtpsC[:, bb * 128:(bb + 1) * 128]))
                grow = [esC.enter_context(nc.sbuf_tensor(f"grow{i}", [128, TOK], BF16))
                        for i in range(2)]
                pprow = [esC.enter_context(nc.sbuf_tensor(f"pprow{i}", [128, TOK], BF16))
                        for i in range(2)]
                for hc in range(HC):
                    r2 = hc % 2
                    C.wait("sync", f"sd{r2}", C.val(f"sd{r2}"))
                    C.wait("sync", "cpp", max(0, C.val("cpp") - NTT))
                    C.prod("sync", f"ldA{r2}", 16, lambda e, hc=hc, r2=r2: e.dma_start(
                        out=grow[r2][:], in_=gtr_h[hc, :, :]))
                    C.wait("vector", f"ldA{r2}")
                    C.wait("vector", f"sd{r2}", C.val(f"sd{r2}"))
                    C.wait("scalar", f"ldA{r2}")
                    C.wait("gpsimd", f"ldA{r2}")
                    for tt in range(NTT):
                        t2_ = (hc * NTT + tt) % 2
                        gsl = grow[r2][:, tt * 512:(tt + 1) * 512]
                        psl = pprow[r2][:, tt * 512:(tt + 1) * 512]
                        C.wait("vector", "csg", max(0, C.val("csg") - 1))
                        C.prod("vector", "cgg", 1, lambda e, gsl=gsl, t2_=t2_: e.tensor_scalar(
                            gg[t2_][:], gsl, scv(SC_KG), MG, ALU.mult, ALU.add))
                        C.wait("scalar", "cgg")
                        C.wait("scalar", "cs2", max(0, C.val("cs2") - 1))
                        C.prod("scalar", "csg", 1, lambda e, t2_=t2_: e.activation(
                            sg[t2_][:], gg[t2_][:], AF.Sigmoid, bias=scv(SC_BGT), scale=scv(SC_SGT)))
                        C.prod("scalar", "ct2", 1, lambda e, t2_=t2_: e.activation(
                            t2b[t2_][:], sg[t2_][:], AF.Identity, bias=mgc[:, 0:1], scale=65536.0))
                        C.wait("vector", "ct2")
                        C.prod("vector", "cs2", 1, lambda e, t2_=t2_: e.tensor_scalar(
                            s2c[t2_][:], t2b[t2_][:], -MG, 65535.0, ALU.add, ALU.min))
                        C.prod("vector", "cpp0", 1, lambda e, t2_=t2_, psl=psl: e.scalar_tensor_tensor(
                            psl, gg[t2_][:], -MG, s2c[t2_][:], ALU.add, ALU.mult))
                        ccol = hc * 8 + tt
                        C.prod("vector", "cpp", 1, lambda e, psl=psl, ccol=ccol: e.tensor_reduce(
                            maxbuf[:, ccol:ccol + 1], psl, AX.X, ALU.max,
                            apply_absolute_value=True))
                    C.wait("gpsimd", "cpp0", C.val("cpp0"))
                    C.prod("gpsimd", f"sd{r2}", 16, lambda e, hc=hc, r2=r2: e.dma_start(
                        out=pp_h[hc, :, :], in_=pprow[r2][:]))

            with ExitStack() as esR3:
                r_ps3 = esR3.enter_context(nc.psum_tensor("r_ps3", [1, 128], F32))
                rb_ps3 = esR3.enter_context(nc.psum_tensor("rb_ps3", [128, 16], F32))
                C.wait("vector", "cpp", C.val("cpp"))
                C.prod("vector", "cpp", 1, lambda e: e.tensor_reduce(
                    runmax[:, 3:4], maxbuf[:, 0:192], AX.X, ALU.max))
                C.prodw("vector", "cpp", 1, lambda e: e.memset(maxbuf[:, 0:192], 0.0))
                C.wait("tensor", "cpp", C.val("cpp"))
                C.prod("tensor", "tarX", 1, lambda e: e.matmul(
                    r_ps3[:], runmax[:, 3:4], idf[:], start=True, stop=True))
                C.wait("vector", "tarX")
                C.prodw("vector", "varX", 1, lambda e: e.tensor_reduce(
                    ccrow[0:1, 0:1], r_ps3[:], AX.X, ALU.max))
                C.wait("gpsimd", "varX")
                C.prod("gpsimd", "ccwX", 16, lambda e: e.dma_start(out=cc_in.ap(), in_=ccrow[:]))
                C.wait("gpsimd", "ccwX")
                C.prod("gpsimd", "arQ", 1, lambda e: e.collective_compute(
                    "AllReduce", ALU.max, replica_groups=[list(range(NCORE))],
                    ins=[cc_in.ap()], outs=[cc_out.ap()]))
                C.wait("gpsimd", "arQ")
                C.prod("gpsimd", "ccrX", 16, lambda e: e.dma_start(out=ccrow[:], in_=cc_out.ap()))
                C.wait("vector", "ccrX")
                C.prodw("vector", "ardX", 1, lambda e: e.reciprocal(rows[0:1, 23:24], ccrow[0:1, 0:1]))
                C.prodw("vector", "ardX", 1, lambda e: e.tensor_scalar(
                    rows[0:1, 23:24], rows[0:1, 23:24], 32767.0, None, ALU.mult))
                C.prodw("vector", "ardX", 1, lambda e: e.tensor_tensor(
                    rows[0:1, 22:23], ccrow[0:1, 0:1], rows[0:1, 17:18], ALU.mult))
                C.prodw("vector", "ardX", 1, lambda e: e.tensor_scalar(
                    rows[0:1, 22:23], rows[0:1, 22:23], 1.0 / (65536.0 * 32767.0), None, ALU.mult))
                C.wait("tensor", "ardX")
                C.prod("tensor", "trbX", 1, lambda e: e.matmul(
                    rb_ps3[:, 0:1], onesrf[:], rows[0:1, 23:24], start=True, stop=True))
                C.wait("vector", "trbX")
                C.prodw("vector", "vrbX", 1, lambda e: e.tensor_copy(scv(SC_K1), rb_ps3[:, 0:1]))

            with ExitStack() as esD:
                sbD = lambda n, sh, dt=F32: esD.enter_context(nc.sbuf_tensor(n, sh, dt))
                t3b = [sbD(f"t3b{i}", [128, 512]) for i in range(2)]
                tub = [sbD(f"tub{i}", [128, 512]) for i in range(2)]
                ugb = [sbD(f"ugb{i}", [128, 512]) for i in range(2)]
                for _eng in Ctx.ENGINES:
                    C.wait(_eng, "vrbX")
                    C.wait(_eng, "sd0"); C.wait(_eng, "sd1")
                prow = [esD.enter_context(nc.sbuf_tensor(f"prow{i}", [128, TOK], BF16))
                        for i in range(2)]
                urow = [esD.enter_context(nc.sbuf_tensor(f"urow{i}", [128, TOK], BF16))
                        for i in range(2)]
                oorow = [esD.enter_context(nc.sbuf_tensor(f"oorow{i}", [128, TOK], BF16))
                         for i in range(2)]
                for hc in range(HC):
                    r2 = hc % 2
                    C.wait("sync", f"sd{r2}", C.val(f"sd{r2}"))
                    C.wait("sync", "doo", max(0, C.val("doo") - NTT))
                    C.prod("sync", f"ldA{r2}", 16, lambda e, hc=hc, r2=r2: e.dma_start(
                        out=prow[r2][:], in_=pp_h[hc, :, :]))
                    C.prod("sync", f"ldA{r2}", 16, lambda e, hc=hc, r2=r2: e.dma_start(
                        out=urow[r2][:], in_=upr_h[hc, :, :]))
                    C.wait("scalar", f"ldA{r2}")
                    C.wait("gpsimd", f"ldA{r2}")
                    C.wait("vector", f"sd{r2}", C.val(f"sd{r2}"))
                    for tt in range(NTT):
                        t2_ = (hc * NTT + tt) % 2
                        psl = prow[r2][:, tt * 512:(tt + 1) * 512]
                        usl = urow[r2][:, tt * 512:(tt + 1) * 512]
                        osl_ = oorow[r2][:, tt * 512:(tt + 1) * 512]
                        C.wait("scalar", "doo0", max(0, C.val("doo0") - 1))
                        C.prod("scalar", "dt3", 1, lambda e, t2_=t2_, psl=psl: e.activation(
                            t3b[t2_][:], psl, AF.Identity, bias=mgc[:, 0:1], scale=scv(SC_K1)))
                        C.prod("scalar", "dtu", 1, lambda e, t2_=t2_, usl=usl: e.activation(
                            tub[t2_][:], usl, AF.Identity, bias=mgc[:, 0:1], scale=scv(SC_KU)))
                        C.prod("scalar", "dug", 1, lambda e, t2_=t2_: e.activation(
                            ugb[t2_][:], tub[t2_][:], AF.Identity, bias=mgc[:, 1:2]))
                        C.wait("vector", "dug")
                        C.prod("vector", "doo0", 1, lambda e, t2_=t2_, osl_=osl_: e.scalar_tensor_tensor(
                            osl_, t3b[t2_][:], -MG, ugb[t2_][:], ALU.add, ALU.mult))
                        dcol = hc * 8 + tt
                        C.prod("vector", "doo", 1, lambda e, osl_=osl_, dcol=dcol: e.tensor_reduce(
                            maxbuf[:, dcol:dcol + 1], osl_, AX.X, ALU.max,
                            apply_absolute_value=True))
                    C.wait("gpsimd", "doo0", C.val("doo0"))
                    C.prod("gpsimd", f"sd{r2}", 16, lambda e, hc=hc, r2=r2: e.dma_start(
                        out=oo_h[hc, :, :], in_=oorow[r2][:]))

            with ExitStack() as esR4:
                r_ps4 = esR4.enter_context(nc.psum_tensor("r_ps4", [1, 128], F32))
                rb_ps4 = esR4.enter_context(nc.psum_tensor("rb_ps4", [128, 16], F32))
                C.wait("vector", "doo", C.val("doo"))
                C.prodw("vector", "doo", 1, lambda e: e.tensor_reduce(
                    runmax[:, 4:5], maxbuf[:, 0:192], AX.X, ALU.max))
                C.wait("tensor", "doo", C.val("doo"))
                C.prod("tensor", "tarX", 1, lambda e: e.matmul(
                    r_ps4[:], runmax[:, 4:5], idf[:], start=True, stop=True))
                C.wait("vector", "tarX")
                C.prodw("vector", "varX", 1, lambda e: e.tensor_reduce(
                    ccrow[0:1, 0:1], r_ps4[:], AX.X, ALU.max))
                C.wait("gpsimd", "varX")
                C.prod("gpsimd", "ccwX", 16, lambda e: e.dma_start(out=cc_in.ap(), in_=ccrow[:]))
                C.wait("gpsimd", "ccwX")
                C.prod("gpsimd", "arQ", 1, lambda e: e.collective_compute(
                    "AllReduce", ALU.max, replica_groups=[list(range(NCORE))],
                    ins=[cc_in.ap()], outs=[cc_out.ap()]))
                C.wait("gpsimd", "arQ")
                C.prod("gpsimd", "ccrX", 16, lambda e: e.dma_start(out=ccrow[:], in_=cc_out.ap()))
                C.wait("vector", "ccrX")
                C.prodw("vector", "ardX", 1, lambda e: e.reciprocal(rows[0:1, 24:25], ccrow[0:1, 0:1]))
                C.prodw("vector", "ardX", 1, lambda e: e.tensor_scalar(
                    rows[0:1, 24:25], rows[0:1, 24:25], 32767.0, None, ALU.mult))
                C.prodw("vector", "ardX", 1, lambda e: e.tensor_tensor(
                    rows[0:1, 25:26], ccrow[0:1, 0:1], rows[0:1, 22:23], ALU.mult))
                C.prodw("vector", "ardX", 1, lambda e: e.tensor_tensor(
                    rows[0:1, 25:26], rows[0:1, 25:26], rows[0:1, 16:17], ALU.mult))
                C.prodw("vector", "ardX", 1, lambda e: e.tensor_scalar(
                    rows[0:1, 25:26], rows[0:1, 25:26], 1.0 / 32767.0, None, ALU.mult))
                C.wait("tensor", "ardX")
                C.prod("tensor", "trbX", 1, lambda e: e.matmul(
                    rb_ps4[:, 0:2], onesrf[:], rows[0:1, 24:26], start=True, stop=True))
                C.wait("vector", "trbX")
                C.prod("vector", "vrbX", 1, lambda e: e.tensor_copy(scv(SC_K2), rb_ps4[:, 0:1]))
                C.prodw("vector", "vrbX", 1, lambda e: e.tensor_copy(scv(SC_SO), rb_ps4[:, 1:2]))
                C.prodw("vector", "vrbX", 1, lambda e: e.tensor_scalar(
                    modv[:, MV_GB * 16: MV_GB * 16 + 16],
                    modv[:, MV_GB * 16: MV_GB * 16 + 16], scv(SC_SO), None, ALU.mult))

            with ExitStack() as esE:
                sbE = lambda n, sh, dt=F32: esE.enter_context(nc.sbuf_tensor(n, sh, dt))
                oin = [sbE(f"oin{i}", [128, 512], BF16) for i in range(2)]
                t4b = [sbE(f"t4b{i}", [128, 512]) for i in range(2)]
                Ob = [sbE(f"Ob{i}", [128, 512], BF16) for i in range(HC)]
                x1t = [sbE(f"x1t{i}", [128, DC * 512]) for i in range(2)]
                outb = [sbE(f"outb{i}", [128, 512]) for i in range(4)]
                mmE = [esE.enter_context(nc.psum_tensor(f"mmE{i}", [128, 512], F32))
                       for i in range(2)]
                for _eng in Ctx.ENGINES:
                    C.wait(_eng, "vrbX")
                    C.wait(_eng, "sd0"); C.wait(_eng, "sd1")
                for tt in range(NTT):
                    s = tt // NT
                    xs = tt % 2
                    C.wait("sync", "mmfree0", C.val("mmfree0"))
                    C.wait("sync", "mmfree1", C.val("mmfree1"))
                    for dc in range(DC):
                        C.prod("sync", f"ex1{xs}", 16, lambda e, dc=dc, tt=tt, xs=xs: e.dma_start(
                            out=x1t[xs][:, dc * 512:(dc + 1) * 512],
                            in_=x1_h[dc, :, tt * 512:(tt + 1) * 512]))
                    for hc in range(HC):
                        r2 = (tt * HC + hc) % 2
                        C.wait("sync", "eO", max(0, C.val("eO") - 1))
                        C.prod("sync", f"ldA{r2}", 16, lambda e, hc=hc, tt=tt, r2=r2: e.dma_start(
                            out=oin[r2][:], in_=oo_h[hc, :, tt * 512:(tt + 1) * 512]))
                        C.wait("scalar", f"ldA{r2}")
                        C.wait("scalar", "eO", max(0, C.val("eO") - 1))
                        C.wait("scalar", "emm", C.val("emm"))
                        C.prod("scalar", "et4", 1, lambda e, r2=r2: e.activation(
                            t4b[r2][:], oin[r2][:], AF.Identity, bias=mgc[:, 0:1], scale=scv(SC_K2)))
                        C.prod("scalar", "eO", 1, lambda e, hc=hc, r2=r2: e.activation(
                            Ob[hc][:], t4b[r2][:], AF.Identity, bias=mgc[:, 1:2]))
                    C.wait("tensor", "eO", C.val("eO"))
                    C.wait("tensor", "cst", C.val("cst"))
                    C.wait("tensor", f"ex1{xs}", C.val(f"ex1{xs}"))
                    for dc in range(DC):
                        bank = C.counts["mmuse"] % 2
                        C.wait("tensor", f"mmfree{bank}", C.val(f"mmfree{bank}"))
                        C.counts["mmuse"] += 1
                        for hc in range(HC):
                            C.prod("tensor", "emm", 1, lambda e, dc=dc, hc=hc, bank=bank:
                                   e.matmul(mmE[bank][:], wdT[hc][:, dc * 128:(dc + 1) * 128],
                                            Ob[hc][:], start=(hc == 0), stop=(hc == HC - 1)))
                        osl = C.ctr("eev") % 4
                        C.wait("vector", "emm")
                        C.wait("vector", f"sd{osl}", C.val(f"sd{osl}"))
                        C.wait("vector", f"ex1{xs}", C.val(f"ex1{xs}"))
                        C.prod("vector", f"mmfree{bank}", 1, lambda e, dc=dc, s=s, xs=xs, osl=osl,
                               bank=bank: e.scalar_tensor_tensor(
                                   outb[osl][:], mmE[bank][:], mv(MV_GB, dc, s),
                                   x1t[xs][:, dc * 512:(dc + 1) * 512], ALU.mult, ALU.add))
                        C.ctr_inc("eev")
                        C.wait("gpsimd", f"mmfree{bank}", C.val(f"mmfree{bank}"))
                        C.prod("gpsimd", f"sd{osl}", 16, lambda e, dc=dc, tt=tt, osl=osl:
                               e.dma_start(out=out_fm[dc, :, tt * 512:(tt + 1) * 512],
                                           in_=outb[osl][:]))
                for i in range(4):
                    C.wait("gpsimd", f"sd{i}", C.val(f"sd{i}"))
                C.run_block("BE")

    return nc



def _prep_inputs(inputs):
    f32 = np.float32
    bf16 = ml_dtypes.bfloat16
    x = np.asarray(inputs["L0_attn_x"], f32)
    b_in = np.asarray(inputs["L0_attn_b_in"], f32)
    b_out = np.asarray(inputs["L0_attn_b_out"], f32)
    w_inT = np.ascontiguousarray(np.asarray(inputs["L0_attn_w_in"], f32).T)
    w_outT = np.ascontiguousarray(np.asarray(inputs["L0_attn_w_out"], f32).T)
    w_gate = np.asarray(inputs["L0_ffn_w_gate"], f32)
    w_up = np.asarray(inputs["L0_ffn_w_up"], f32)
    w_down = np.asarray(inputs["L0_ffn_w_down"], f32)
    c = np.asarray(inputs["L0_c"], f32)
    wmod = np.asarray(inputs["L0_wmod"], f32)

    ident_f = np.eye(128, dtype=f32)
    ident_bf = ident_f.astype(bf16)
    ones_r_f = np.ones((1, 128), f32)
    ones_r_bf = np.ones((1, 512), f32).astype(bf16)
    invD_c = np.full((128, 1), 1.0 / D, f32).astype(bf16)

    q = np.arange(128)[:, None]
    j = np.arange(384)[None, :]
    m_int = np.where((j >= q) & (j <= q + 256), 1.0, 0.0).astype(f32)
    m_first = np.where(np.abs(q - j) <= 128, 1.0, 0.0).astype(f32)
    m_last = np.where(j >= q + 128, 1.0, 0.0).astype(f32)
    masks = np.concatenate([m_int, m_first, m_last], axis=1).astype(bf16)

    in_maps = []
    for core in range(NCORE):
        sl = slice(core * BPC, (core + 1) * BPC)
        x_fmc = np.ascontiguousarray(
            x[sl].reshape(BPC * T, DC, 128).transpose(1, 2, 0))
        c_fmc = np.ascontiguousarray(c[sl].T.reshape(DC, 128, BPC))
        in_maps.append({
            "x_fm": x_fmc, "w_inT": w_inT, "b_in": b_in, "w_outT": w_outT,
            "b_out": b_out, "w_gate": w_gate, "w_up": w_up, "w_down": w_down,
            "c_fm": c_fmc, "wmod": wmod,
            "ident_bf": ident_bf, "ident_f": ident_f, "masks": masks,
            "ones_r_f": ones_r_f, "ones_r_bf": ones_r_bf, "invD_c_bf": invD_c,
        })
    return in_maps


def kernel(**inputs):
    in_maps = _prep_inputs(inputs)
    nc = build_nc(False)
    res = run_bass_kernel_spmd(nc, in_maps, core_ids=list(range(NCORE)))
    out = np.empty((B, T, D), np.float32)
    for core in range(NCORE):
        o = res.results[core]["out_fm"]
        out[core * BPC:(core + 1) * BPC] = o.transpose(2, 0, 1).reshape(BPC, T, D)
    return out


# version 13
# speedup vs baseline: 1.5292x; 1.0036x over previous
import numpy as np
import ml_dtypes
from contextlib import ExitStack

import concourse.bass as bass
import concourse.mybir as mybir
from concourse.bass_utils import run_bass_kernel_spmd

F32 = mybir.dt.float32
BF16 = mybir.dt.bfloat16
AF = mybir.ActivationFunctionType
ALU = mybir.AluOpType
AX = mybir.AxisListType

MAGIC = 12582912.0
NCORE = 8
B, T, D, HID = 16, 2048, 1024, 3072
NH, HDIM = 16, 64
BPC = B // NCORE
NBLK = T // 128
NT = T // 512
DC = D // 128
HC = HID // 128
TOK = BPC * T
NTT = TOK // 512


class Ctx:
    ENGINES = ("sync", "gpsimd", "vector", "scalar", "tensor")

    def __init__(self, nc, es):
        self.nc, self.es = nc, es
        self.sems, self.counts = {}, {}
        self.plan = {e: [] for e in self.ENGINES}

    def sem(self, name):
        if name not in self.sems:
            self.sems[name] = self.es.enter_context(self.nc.semaphore(name))
            self.counts[name] = 0
        return self.sems[name]

    def val(self, name):
        self.sem(name)
        return self.counts[name]

    def ctr(self, name):
        return self.counts.setdefault("#" + name, 0)

    def ctr_inc(self, name):
        self.counts["#" + name] = self.ctr(name) + 1

    def emit(self, eng, fn):
        self.plan[eng].append(fn)

    def wait(self, eng, name, value=None):
        h = self.sem(name)
        v = self.counts[name] if value is None else value
        if v > 0:
            self.emit(eng, lambda e, h=h, v=v: e.wait_ge(h, v))

    def prod(self, eng, name, k, fn):
        h = self.sem(name)
        self.counts[name] += k
        self.emit(eng, lambda e, h=h, k=k, fn=fn: fn(e).then_inc(h, k))

    def prodw(self, eng, name, k, fn):
        self.prod(eng, name, k, fn)
        self.wait(eng, name)

    def run_block(self, label=None):
        plan, self.plan = self.plan, {e: [] for e in self.ENGINES}
        import contextlib
        scope = self.nc.named_scope(label) if label else contextlib.nullcontext()
        with scope, self.nc.Block() as blk:
            blk.sync(lambda e: [f(e) for f in plan["sync"]])
            blk.gpsimd(lambda e: [f(e) for f in plan["gpsimd"]])
            blk.vector(lambda e: [f(e) for f in plan["vector"]])
            blk.scalar(lambda e: [f(e) for f in plan["scalar"]])
            blk.tensor(lambda e: [f(e) for f in plan["tensor"]])


def build_nc(dbg=False):
    nc = bass.Bass("TRN2", target_bir_lowering=False, detect_race_conditions=False)

    x_fm = nc.dram_tensor("x_fm", [DC, 128, TOK], F32, kind="ExternalInput").ap()
    w_inT = nc.dram_tensor("w_inT", [D, 3 * D], F32, kind="ExternalInput").ap()
    b_in = nc.dram_tensor("b_in", [3 * D], F32, kind="ExternalInput").ap()
    w_outT = nc.dram_tensor("w_outT", [D, D], F32, kind="ExternalInput").ap()
    b_out = nc.dram_tensor("b_out", [D], F32, kind="ExternalInput").ap()
    w_gate = nc.dram_tensor("w_gate", [HID, D], F32, kind="ExternalInput").ap()
    w_up = nc.dram_tensor("w_up", [HID, D], F32, kind="ExternalInput").ap()
    w_down = nc.dram_tensor("w_down", [D, HID], F32, kind="ExternalInput").ap()
    c_fm = nc.dram_tensor("c_fm", [DC, 128, BPC], F32, kind="ExternalInput").ap()
    wmod = nc.dram_tensor("wmod", [D, 6 * D], F32, kind="ExternalInput").ap()
    ident_bf = nc.dram_tensor("ident_bf", [128, 128], BF16, kind="ExternalInput").ap()
    ident_f = nc.dram_tensor("ident_f", [128, 128], F32, kind="ExternalInput").ap()
    masks = nc.dram_tensor("masks", [128, 3 * 384], BF16, kind="ExternalInput").ap()
    ones_r_f = nc.dram_tensor("ones_r_f", [1, 128], F32, kind="ExternalInput").ap()
    ones_r_bf = nc.dram_tensor("ones_r_bf", [1, 512], BF16, kind="ExternalInput").ap()
    invD_c_bf = nc.dram_tensor("invD_c_bf", [128, 1], BF16, kind="ExternalInput").ap()

    out_fm = nc.dram_tensor("out_fm", [DC, 128, TOK], F32, kind="ExternalOutput").ap()

    skind = "ExternalOutput" if dbg else "Internal"
    x1_h = nc.dram_tensor("x1_h", [DC, 128, TOK], F32, kind=skind).ap()
    h2_h = nc.dram_tensor("h2_h", [DC, 128, TOK], BF16, kind=skind).ap()
    upr_h = nc.dram_tensor("upr_h", [HC, 128, TOK], BF16, kind=skind).ap()
    gtr_h = nc.dram_tensor("gtr_h", [HC, 128, TOK], BF16, kind=skind).ap()
    pp_h = nc.dram_tensor("pp_h", [HC, 128, TOK], BF16, kind=skind).ap()
    oo_h = nc.dram_tensor("oo_h", [HC, 128, TOK], BF16, kind=skind).ap()
    scal_h = (nc.dram_tensor("scal_h", [1, 16], F32, kind="ExternalOutput").ap()
              if dbg else None)
    if dbg:
        tap_h = nc.dram_tensor("tap_h", [128, 4096], BF16, kind="ExternalOutput").ap()
        tap_q = nc.dram_tensor("tap_q", [128, 4096], BF16, kind="ExternalOutput").ap()
        tap_k = nc.dram_tensor("tap_k", [128, 4096], BF16, kind="ExternalOutput").ap()
        tap_v = nc.dram_tensor("tap_v", [128, 1024], BF16, kind="ExternalOutput").ap()
        tap_c = nc.dram_tensor("tap_c", [128, 1024], BF16, kind="ExternalOutput").ap()
        tap_ab = nc.dram_tensor("tap_ab", [128, 256], F32, kind="ExternalOutput").ap()
        tap_e = nc.dram_tensor("tap_e", [128, 384], BF16, kind="ExternalOutput").ap()
        tap_at = nc.dram_tensor("tap_at", [128, 384], BF16, kind="ExternalOutput").ap()
        tap_d = nc.dram_tensor("tap_d", [128, 128], BF16, kind="ExternalOutput").ap()
        tap_sm = nc.dram_tensor("tap_sm", [128, 4], F32, kind="ExternalOutput").ap()
    cc_in = nc.dram_tensor("cc_in", [1, 16], F32, kind="Internal")
    cc_out = nc.dram_tensor("cc_out", [1, 16], F32, kind="Internal")

    es = ExitStack()
    with es:
        C = Ctx(nc, es)
        sbuf = lambda n, s, dt=F32: es.enter_context(nc.sbuf_tensor(n, s, dt))

        idbf = sbuf("idbf", [128, 128], BF16)
        idf = sbuf("idf", [128, 128], F32)
        msk = sbuf("msk", [128, 3 * 384], BF16)
        onesrf = sbuf("onesrf", [1, 128], F32)
        onesrbf = sbuf("onesrbf", [1, 512], BF16)
        invDc = sbuf("invDc", [128, 1], BF16)
        bq_col = sbuf("bq_col", [128, 8], F32)
        bk_col = sbuf("bk_col", [128, 8], F32)
        bv_row = sbuf("bv_row", [1, 1024], BF16)
        bo_row = sbuf("bo_row", [1, 1024], BF16)
        modv = sbuf("modv", [128, 6 * 16], F32)
        scal = sbuf("scal", [128, 16], F32)
        runmax = sbuf("runmax", [128, 8], F32)
        rows = sbuf("rows", [1, 512], F32)
        ccrow = sbuf("ccrow", [1, 16], F32)
        gascr = sbuf("gascr", [128, 4], F32)
        mgc = sbuf("mgc", [128, 2], F32)
        maxbuf = sbuf("maxbuf", [128, 384], F32)

        MV_ONEPCA, MV_SA, MV_GA, MV_ONEPCB, MV_SB, MV_GB = range(6)

        def mv(which, c, s):
            j = which * 16 + c * BPC + s
            return modv[:, j: j + 1]

        SC_INVX, SC_KG, SC_SGT, SC_BGT, SC_KU, SC_K1, SC_K2, SC_SO = range(8)
        scv = lambda j: scal[:, j: j + 1]

        with ExitStack() as es0:
            sb0 = lambda n, s, dt=F32: es0.enter_context(nc.sbuf_tensor(n, s, dt))
            cfm = sb0("cfm", [128, DC * BPC])
            silc = sb0("silc", [128, DC * BPC])
            wmt = sb0("wmt", [128, 2 * 512])
            modr = sb0("modr", [BPC, 6 * D])
            bcol_raw = sb0("bcol_raw", [128, 16])
            bvr_f = sb0("bvr_f", [1, 1024])
            bor_f = sb0("bor_f", [1, 1024])
            mps = es0.enter_context(nc.psum_tensor("mps", [BPC, 512], F32))
            tps = es0.enter_context(nc.psum_tensor("tps", [128, 96], F32))

            loads = [(idbf, ident_bf), (idf, ident_f), (msk, masks),
                     (onesrf, ones_r_f), (onesrbf, ones_r_bf), (invDc, invD_c_bf),
                     (bvr_f, b_in[2 * D: 3 * D].unsqueeze(0)),
                     (bor_f, b_out.unsqueeze(0))]
            for dst, src in loads:
                C.prod("sync", "p0ld", 16, lambda e, d=dst, s=src: e.dma_start(out=d[:], in_=s))
            C.prod("sync", "p0ld", 16, lambda e: e.dma_start(
                out=cfm[:].rearrange("p (c s) -> p c s", s=BPC),
                in_=c_fm.rearrange("c p s -> p c s"),
                allow_slow_non_contiguous=True))
            C.prod("sync", "p0ld", 16, lambda e: e.dma_start(
                out=bcol_raw[:], in_=b_in[0: 2 * D].rearrange("(c p) -> p c", p=128),
                allow_slow_non_contiguous=True))

            C.wait("vector", "p0ld")
            C.prod("vector", "p0v", 1, lambda e: e.tensor_copy(bv_row[:], bvr_f[:]))
            C.prod("vector", "p0v", 1, lambda e: e.tensor_copy(bo_row[:], bor_f[:]))
            C.prod("vector", "p0v", 1, lambda e: e.tensor_scalar(
                bq_col[:], bcol_raw[:, 0:8], 0.125, None, ALU.mult))
            C.prod("vector", "p0v", 1, lambda e: e.tensor_copy(bk_col[:], bcol_raw[:, 8:16]))
            C.prod("vector", "p0v", 1, lambda e: e.memset(runmax[:], 0.0))
            C.prod("vector", "p0v", 1, lambda e: e.memset(ccrow[:], 0.0))
            C.prod("vector", "p0v", 1, lambda e: e.memset(mgc[:, 0:1], MAGIC))
            C.prod("vector", "p0v", 1, lambda e: e.memset(mgc[:, 1:2], -MAGIC))
            C.prod("vector", "p0v", 1, lambda e: e.memset(maxbuf[:], 0.0))
            C.wait("scalar", "p0ld")
            C.prod("scalar", "p0s", 1, lambda e: e.activation(silc[:], cfm[:], AF.Silu))

            C.wait("tensor", "p0s")
            for j in range(12):
                C.wait("tensor", "p0ev", j)
                for dc in range(DC):
                    i = j * DC + dc
                    slot = i % 2
                    C.wait("sync", "p0mm", max(0, i - 1))
                    C.prod("sync", f"p0wld{slot}", 16, lambda e, j=j, dc=dc, slot=slot: e.dma_start(
                        out=wmt[:, slot * 512: slot * 512 + 512],
                        in_=wmod[dc * 128:(dc + 1) * 128, j * 512:(j + 1) * 512]))
                    C.wait("tensor", f"p0wld{slot}")
                    C.prod("tensor", "p0mm", 1, lambda e, dc=dc, slot=slot: e.matmul(
                        mps[:, 0:512], silc[:, dc * BPC:(dc + 1) * BPC],
                        wmt[:, slot * 512: slot * 512 + 512],
                        start=(dc == 0), stop=(dc == DC - 1)))
                C.wait("vector", "p0mm")
                C.prod("vector", "p0ev", 1, lambda e, j=j: e.tensor_copy(
                    modr[:, j * 512:(j + 1) * 512], mps[:, 0:512]))

            C.wait("tensor", "p0ev")
            for k in range(48):
                C.prod("tensor", "p0tr", 1, lambda e, k=k: e.transpose(
                    tps[:, k * BPC:(k + 1) * BPC],
                    modr[:, k * 128:(k + 1) * 128], idf[0:BPC, 0:BPC]))
            C.wait("vector", "p0tr")
            C.wait("scalar", "p0tr")

            def tcol(v, c):
                k = v * 8 + c
                return tps[:, k * BPC:(k + 1) * BPC]

            for c in range(DC):
                C.prod("vector", "p0d", 1, lambda e, c=c: e.tensor_copy(
                    modv[:, MV_SA * 16 + c * BPC: MV_SA * 16 + (c + 1) * BPC], tcol(0, c)))
                C.prod("vector", "p0d", 1, lambda e, c=c: e.tensor_scalar(
                    modv[:, MV_ONEPCA * 16 + c * BPC: MV_ONEPCA * 16 + (c + 1) * BPC],
                    tcol(1, c), 1.0, None, ALU.add))
                C.prod("vector", "p0d", 1, lambda e, c=c: e.tensor_copy(
                    modv[:, MV_SB * 16 + c * BPC: MV_SB * 16 + (c + 1) * BPC], tcol(3, c)))
                C.prod("vector", "p0d", 1, lambda e, c=c: e.tensor_scalar(
                    modv[:, MV_ONEPCB * 16 + c * BPC: MV_ONEPCB * 16 + (c + 1) * BPC],
                    tcol(4, c), 1.0, None, ALU.add))
                for (vec, mvidx) in ((2, MV_GA), (5, MV_GB)):
                    C.prod("scalar", "p0ga", 1, lambda e, vec=vec, c=c: e.activation(
                        gascr[:, 0:BPC], tcol(vec, c), AF.Abs))
                    C.prod("scalar", "p0ga", 1, lambda e, vec=vec, c=c: e.activation(
                        gascr[:, 2:2 + BPC], tcol(vec, c), AF.Sign))
                    C.wait("vector", "p0ga")
                    C.prodw("vector", "p0gb", 1, lambda e: e.tensor_scalar(
                        gascr[:, 0:BPC], gascr[:, 0:BPC], 0.25, None, ALU.max))
                    C.prod("vector", "p0gb", 1, lambda e, mvidx=mvidx, c=c: e.tensor_tensor(
                        modv[:, mvidx * 16 + c * BPC: mvidx * 16 + (c + 1) * BPC],
                        gascr[:, 0:BPC], gascr[:, 2:2 + BPC], ALU.mult))
                    C.wait("scalar", "p0gb")
            C.run_block("P0")

        with ExitStack() as esA:
            sbA = lambda n, sh, dt=F32: esA.enter_context(nc.sbuf_tensor(n, sh, dt))
            winT = [sbA(f"winT{c}", [128, 3 * D], BF16) for c in range(DC)]
            woutT = [sbA(f"woutT{c}", [128, D], BF16) for c in range(DC)]
            xfm = [sbA(f"xfm{i}", [128, DC * 512]) for i in range(2)]
            hbuf = [sbA(f"hbuf{i}", [128, DC * 512], BF16) for i in range(2)]
            QT = [sbA(f"QT{i}", [128, DC * 512], BF16) for i in range(2)]
            KT = [sbA(f"KT{i}", [128, DC * 512], BF16) for i in range(3)]
            VT = [sbA(f"VT{i}", [128, 1024], BF16) for i in range(9)]
            ctxT = [sbA(f"ctxT{i}", [128, DC * 128], BF16) for i in range(2)]
            xbf = sbA("xbf", [128, 1024], BF16)
            xsq = sbA("xsq", [128, 1024], BF16)
            ubuf = sbA("ubuf", [128, 1024])
            bcsb = sbA("bcsb", [128, 256])
            h2st = [sbA(f"h2st{i}", [128, 1024], BF16) for i in range(2)]
            expb = [sbA(f"expb{i}", [128, 384], BF16) for i in range(2)]
            mexp = [sbA(f"mexp{i}", [128, 384], BF16) for i in range(2)]
            atb = [sbA(f"atb{i}", [128, 384], BF16) for i in range(2)]
            Dt = [sbA(f"Dt{i}", [128, 128], BF16) for i in range(2)]
            nmx = sbA("nmx", [128, 4])
            sume = sbA("sume", [128, 8])

            st_ps = esA.enter_context(nc.psum_tensor("st_ps", [128, 256], F32))
            mm_ps = [esA.enter_context(nc.psum_tensor(f"mm_ps{i}", [128, 512], F32))
                     for i in range(2)]
            sc_ps = [esA.enter_context(nc.psum_tensor(f"sc_ps{i}", [128, 384], F32))
                     for i in range(2)]
            at_ps = [esA.enter_context(nc.psum_tensor(f"at_ps{i}", [128, 384], F32))
                     for i in range(2)]
            cx_ps = esA.enter_context(nc.psum_tensor("cx_ps", [128, 128], F32))

            for _eng in Ctx.ENGINES:
                C.wait(_eng, "p0gb")
                C.wait(_eng, "p0ld")
                C.wait(_eng, "p0v")

            for c in range(DC):
                C.prod("gpsimd", "wina", 16, lambda e, c=c: e.dma_start(
                    out=winT[c][:], in_=w_inT[c * 128:(c + 1) * 128, :]))
            for c in range(DC):
                C.prod("gpsimd", "wouta", 16, lambda e, c=c: e.dma_start(
                    out=woutT[c][:], in_=w_outT[c * 128:(c + 1) * 128, :]))

            tok0 = lambda s, t4: s * T + t4 * 512
            C.counts.setdefault("mmuse", 0)
            qkv_marks = {}

            def acquire_mm_bank():
                bank = C.counts["mmuse"] % 2
                C.wait("tensor", f"mmfree{bank}", C.val(f"mmfree{bank}"))
                C.counts["mmuse"] += 1
                return bank

            def emit_ln(s, b, second):
                t4, b4 = b // 4, b % 4
                gt4 = s * NT + t4
                slot, r2 = gt4 % 2, b % 2
                xin = (xfm[slot].ap()
                       .rearrange("p (c w) -> p c w", w=512)[:, :, b4 * 128:(b4 + 1) * 128])
                C.wait("vector", "tst", C.val("tst"))
                C.prod("vector", "vxbf", 1, lambda e, xin=xin: e.tensor_copy(
                    xbf[:].rearrange("p (c w) -> p c w", w=128), xin))
                C.wait("scalar", "tst", C.val("tst"))
                C.prod("scalar", "sxsq", 1, lambda e, xin=xin: e.activation(
                    xsq[:].rearrange("p (c w) -> p c w", w=128), xin, AF.Square))
                C.wait("tensor", "vbc", C.val("vbc"))
                C.wait("tensor", "vxbf")
                C.wait("tensor", "sxsq")
                for dc in range(DC):
                    C.prod("tensor", "tst", 1, lambda e, dc=dc: e.matmul(
                        st_ps[0:1, 0:128], invDc[:], xbf[:, dc * 128:(dc + 1) * 128],
                        start=(dc == 0), stop=False))
                for dc in range(DC):
                    C.prod("tensor", "tst", 1, lambda e, dc=dc: e.matmul(
                        st_ps[0:1, 128:256], invDc[:], xsq[:, dc * 128:(dc + 1) * 128],
                        start=False, stop=(dc == DC - 1)))
                C.wait("vector", "tst")
                C.prodw("vector", "vst", 1, lambda e: e.tensor_copy(rows[0:1, 0:256], st_ps[0:1, 0:256]))
                C.prodw("vector", "vst", 1, lambda e: e.tensor_tensor(
                    rows[0:1, 256:384], rows[0:1, 0:128], rows[0:1, 0:128], ALU.mult))
                C.prodw("vector", "vst", 1, lambda e: e.tensor_tensor(
                    rows[0:1, 256:384], rows[0:1, 128:256], rows[0:1, 256:384], ALU.subtract))
                C.prodw("vector", "vst", 1, lambda e: e.tensor_scalar(
                    rows[0:1, 256:384], rows[0:1, 256:384], 1e-5, None, ALU.add))
                C.wait("scalar", "vst")
                C.prodw("scalar", "sst", 1, lambda e: e.activation(
                    rows[0:1, 384:512], rows[0:1, 256:384], AF.Ln))
                C.prodw("scalar", "sst", 1, lambda e: e.activation(
                    rows[0:1, 384:512], rows[0:1, 384:512], AF.Exp, scale=-0.5))
                C.wait("vector", "sst")
                C.prodw("vector", "vst2", 1, lambda e: e.tensor_tensor(
                    rows[0:1, 256:384], rows[0:1, 0:128], rows[0:1, 384:512], ALU.mult))
                C.prodw("vector", "vst2", 1, lambda e: e.tensor_scalar(
                    rows[0:1, 256:384], rows[0:1, 256:384], -1.0, None, ALU.mult))
                C.wait("tensor", "vst2")
                C.prod("tensor", "tbc", 1, lambda e: e.matmul(
                    st_ps[:, 0:128], onesrf[:], rows[0:1, 384:512], start=True, stop=False))
                C.prod("tensor", "tbc", 1, lambda e: e.matmul(
                    st_ps[:, 128:256], onesrf[:], rows[0:1, 256:384], start=False, stop=True))
                C.wait("vector", "tbc")
                C.wait("vector", "gu", C.val("gu"))
                C.prod("vector", "vbc", 1, lambda e: e.tensor_copy(bcsb[:], st_ps[:]))
                C.wait("gpsimd", "vbc")
                C.wait("gpsimd", "vh", C.val("vh"))
                C.wait("gpsimd", "vh2", C.val("vh2"))
                a_bc = bcsb[:, 0:128].unsqueeze(1).broadcast_to([128, 8, 128])
                b_bc = bcsb[:, 128:256].unsqueeze(1).broadcast_to([128, 8, 128])
                C.prod("gpsimd", "gu", 1, lambda e, xin=xin, a_bc=a_bc: e.tensor_tensor(
                    ubuf[:].rearrange("p (c w) -> p c w", w=128), xin, a_bc, ALU.mult))
                C.prod("gpsimd", "gu", 1, lambda e, b_bc=b_bc: e.tensor_tensor(
                    ubuf[:].rearrange("p (c w) -> p c w", w=128),
                    ubuf[:].rearrange("p (c w) -> p c w", w=128), b_bc, ALU.add))
                C.wait("vector", "gu")
                if second:
                    C.wait("vector", f"h2sp{r2}", C.val(f"h2sp{r2}"))
                for c in range(DC):
                    u2c = ubuf[:, c * 128:(c + 1) * 128]
                    if not second:
                        hdst = hbuf[slot][:, c * 512 + b4 * 128: c * 512 + b4 * 128 + 128]
                        C.prod("vector", "vh", 1, lambda e, u2c=u2c, hdst=hdst, c=c, s=s:
                               e.tensor_scalar(hdst, u2c, mv(MV_ONEPCA, c, s),
                                               mv(MV_SA, c, s), ALU.mult, ALU.add))
                    else:
                        hdst = h2st[r2][:, c * 128:(c + 1) * 128]
                        C.prod("vector", "vh2", 1, lambda e, u2c=u2c, hdst=hdst, c=c, s=s:
                               e.tensor_scalar(hdst, u2c, mv(MV_ONEPCB, c, s),
                                               mv(MV_SB, c, s), ALU.mult, ALU.add))
                if not second:
                    C.wait("vector", "vh")
                    for c in range(DC):
                        hdst = hbuf[slot][:, c * 512 + b4 * 128: c * 512 + b4 * 128 + 128]
                        xdst = xfm[slot][:, c * 512 + b4 * 128: c * 512 + b4 * 128 + 128]
                        C.prod("vector", "vx1", 1, lambda e, hdst=hdst, xdst=xdst, c=c, s=s:
                               e.scalar_tensor_tensor(xdst, hdst, mv(MV_GA, c, s), xdst,
                                                      ALU.mult, ALU.add))
                else:
                    C.wait("vector", "vh2")
                    mcol = s * NBLK + b
                    C.prod("vector", "vh2m", 1, lambda e, r2=r2, mcol=mcol: e.tensor_reduce(
                        maxbuf[:, mcol:mcol + 1], h2st[r2][:], AX.X, ALU.max,
                        apply_absolute_value=True))
                    C.wait("gpsimd", "vh2", C.val("vh2"))
                    tk = s * T + b * 128
                    for c in range(DC):
                        C.prod("gpsimd", f"h2sp{r2}", 16, lambda e, c=c, tk=tk, r2=r2: e.dma_start(
                            out=h2_h[c, :, tk: tk + 128], in_=h2st[r2][:, c * 128:(c + 1) * 128]))

            def qkv_groups(s, t4):
                gt4 = s * NT + t4
                slot = gt4 % 2

                def qk_group(ec):
                    def go():
                        bank = acquire_mm_bank()
                        for dc in range(DC):
                            C.prod("tensor", "tmm", 1, lambda e, ec=ec, dc=dc, bank=bank,
                                   slot=slot: e.matmul(
                                       mm_ps[bank][:],
                                       winT[dc][:, ec * 128:(ec + 1) * 128],
                                       hbuf[slot][:, dc * 512:(dc + 1) * 512],
                                       start=(dc == 0), stop=(dc == DC - 1)))
                        C.wait("scalar", "tmm")
                        if ec < 8:
                            dst = QT[slot][:, ec * 512:(ec + 1) * 512]
                            bias, scale = bq_col[:, ec:ec + 1], 0.125
                        else:
                            dst = KT[gt4 % 3][:, (ec - 8) * 512:(ec - 7) * 512]
                            bias, scale = bk_col[:, ec - 8:ec - 7], 1.0
                        C.prod("scalar", f"mmfree{bank}", 1, lambda e, dst=dst, bias=bias,
                               scale=scale, bank=bank: e.activation(
                                   dst, mm_ps[bank][:], AF.Identity, bias=bias, scale=scale))
                    return go

                def v_group(b4, g):
                    def go():
                        b = t4 * 4 + b4
                        gb = s * NBLK + b
                        bank = acquire_mm_bank()
                        for dc in range(DC):
                            C.prod("tensor", "tmm", 1, lambda e, b4=b4, g=g, dc=dc, bank=bank,
                                   slot=slot: e.matmul(
                                       mm_ps[bank][:],
                                       hbuf[slot][:, dc * 512 + b4 * 128: dc * 512 + b4 * 128 + 128],
                                       winT[dc][:, 2 * D + g * 512: 2 * D + (g + 1) * 512],
                                       start=(dc == 0), stop=False))
                        C.prod("tensor", "tmm", 1, lambda e, g=g, bank=bank: e.matmul(
                            mm_ps[bank][:], onesrbf[0:1, 0:128],
                            bv_row[0:1, g * 512:(g + 1) * 512], start=False, stop=True))
                        C.wait("scalar", "tmm")
                        C.prod("scalar", f"mmfree{bank}", 1, lambda e, gb=gb, g=g, bank=bank:
                               e.activation(VT[gb % 9][:, g * 512:(g + 1) * 512],
                                            mm_ps[bank][:], AF.Identity))
                    return go

                C.wait("tensor", "wina", 16 * DC)
                C.wait("tensor", "vh", C.val("vh"))
                for ec in range(8, 16):
                    yield qk_group(ec)
                for g in range(2):
                    yield v_group(0, g)
                for ec in range(8):
                    yield qk_group(ec)
                for b4 in range(1, 4):
                    for g in range(2):
                        yield v_group(b4, g)

            attmk = {}

            def emit_attention(s, b, filler):
                t4 = b // 4
                gt4 = s * NT + t4
                kb0 = min(max(b - 1, 0), NBLK - 3)
                mvar = 1 if b == 0 else (2 if b == NBLK - 1 else 0)
                C.wait("tensor", "mmfree0", C.val("mmfree0"))
                C.wait("tensor", "mmfree1", C.val("mmfree1"))
                G0 = C.ctr("sctr")
                NHP = NH // 2

                def St(j):
                    G = G0 + j
                    if ("se", G - 1) in attmk:
                        C.wait("tensor", "sexp", attmk[("se", G - 1)])
                    for p in range(2):
                        hr = p * 64
                        qap = QT[gt4 % 2][hr:hr + 64,
                                          j * 512 + (b % 4) * 128: j * 512 + (b % 4) * 128 + 128]
                        for kk in range(3):
                            kb = kb0 + kk
                            kap = KT[(s * NT + kb // 4) % 3][hr:hr + 64,
                                                            j * 512 + (kb % 4) * 128: j * 512 + (kb % 4) * 128 + 128]
                            C.prod("tensor", "tsc", 1, lambda e, p=p, kk=kk, qap=qap, kap=kap:
                                   e.matmul(sc_ps[p][:, kk * 128:(kk + 1) * 128], qap, kap,
                                            start=(kk == 0), stop=(kk == 2)))
                    attmk[("st", G)] = C.val("tsc")

                def Vn(j):
                    G = G0 + j
                    r = (j % 2) * 2
                    C.wait("vector", "tsc", attmk[("st", G)])
                    for p in range(2):
                        C.prod("vector", "vnm", 1, lambda e, p=p, r=r: e.tensor_reduce(
                            nmx[:, r + p:r + p + 1], sc_ps[p][:], AX.X, ALU.max, negate=True))
                    attmk[("vn", G)] = C.val("vnm")

                def Se(j):
                    G = G0 + j
                    r = (j % 2) * 2
                    C.wait("scalar", "vnm", attmk[("vn", G)])
                    if ("vm", G - 1) in attmk:
                        C.wait("scalar", "vmk", attmk[("vm", G - 1)])
                    for p in range(2):
                        C.prod("scalar", "sexp", 1, lambda e, p=p, r=r: e.activation(
                            expb[p][:], sc_ps[p][:], AF.Exp, bias=nmx[:, r + p:r + p + 1]))
                    attmk[("se", G)] = C.val("sexp")

                def Vd(j):
                    G = G0 + j
                    r = (j % 2) * 4
                    C.wait("vector", "sexp", attmk[("se", G)])
                    if ("ta", G - 1) in attmk:
                        C.wait("vector", "tat", attmk[("ta", G - 1)])
                    for p in range(2):
                        C.prod("vector", "vmk", 1, lambda e, p=p, r=r, mvar=mvar:
                               e.scalar_tensor_tensor(
                                   mexp[p][:], expb[p][:], 1.0,
                                   msk[:, mvar * 384:(mvar + 1) * 384],
                                   ALU.mult, ALU.mult, accum_out=sume[:, r + p:r + p + 1]))
                    attmk[("vm", G)] = C.val("vmk")
                    C.wait("vector", "vmk")
                    C.prodw("vector", "vrc", 1, lambda e, r=r: e.reciprocal(
                        sume[:, r + 2:r + 4], sume[:, r:r + 2]))
                    for p in range(2):
                        C.prod("vector", "vD", 1, lambda e, p=p, r=r: e.tensor_scalar(
                            Dt[p][:], idbf[:], sume[:, r + 2 + p:r + 3 + p], None, ALU.mult))
                    attmk[("vd", G)] = C.val("vD")

                def Ta(j):
                    G = G0 + j
                    C.wait("tensor", "vD", attmk[("vd", G)])
                    if ("va", G - 1) in attmk:
                        C.wait("tensor", "vat", attmk[("va", G - 1)])
                    for p in range(2):
                        for kk in range(3):
                            C.prod("tensor", "tat", 1, lambda e, p=p, kk=kk: e.matmul(
                                at_ps[p][:, kk * 128:(kk + 1) * 128],
                                mexp[p][:, kk * 128:(kk + 1) * 128], Dt[p][:],
                                start=(kk == 0), stop=(kk == 2)))
                    attmk[("ta", G)] = C.val("tat")

                def Va(j):
                    G = G0 + j
                    C.wait("vector", "tat", attmk[("ta", G)])
                    if ("tc", G - 1) in attmk:
                        C.wait("vector", "tcx", attmk[("tc", G - 1)])
                    for p in range(2):
                        C.prod("vector", "vat", 1, lambda e, p=p: e.tensor_copy(
                            atb[p][:], at_ps[p][:]))
                    attmk[("va", G)] = C.val("vat")

                def Tc(j):
                    G = G0 + j
                    C.wait("tensor", "vat", attmk[("va", G)])
                    if ("vc", G - 1) in attmk:
                        C.wait("tensor", "vcx", attmk[("vc", G - 1)])
                    for p in range(2):
                        hr = p * 64
                        h = 2 * j + p
                        for kk in range(3):
                            C.prod("tensor", "tcx", 1, lambda e, kk=kk, h=h, hr=hr, kb0=kb0,
                                   p=p, s=s: e.matmul(
                                       cx_ps[hr:hr + 64, :],
                                       VT[(s * NBLK + kb0 + kk) % 9][:, h * 64:(h + 1) * 64],
                                       atb[p][:, kk * 128:(kk + 1) * 128],
                                       start=(kk == 0), stop=(kk == 2)))
                    attmk[("tc", G)] = C.val("tcx")

                def Vc(j):
                    G = G0 + j
                    C.wait("vector", "tcx", attmk[("tc", G)])
                    C.prod("vector", "vcx", 1, lambda e, j=j, b=b: e.tensor_copy(
                        ctxT[b % 2][:, j * 128:(j + 1) * 128], cx_ps[:]))
                    attmk[("vc", G)] = C.val("vcx")

                for i in range(NHP + 2):
                    if 0 <= i - 1 < NHP:
                        Vn(i - 1)
                    if 0 <= i - 2 < NHP:
                        Vd(i - 2)
                    if 0 <= i - 1 < NHP:
                        Se(i - 1)
                    if i < NHP:
                        St(i)
                    if 0 <= i - 2 < NHP:
                        Ta(i - 2)
                        Va(i - 2)
                        Tc(i - 2)
                        Vc(i - 2)
                    filler()
                for j in range(NHP):
                    C.ctr_inc("sctr")

            def emit_proj(s, b):
                t4, b4 = b // 4, b % 4
                slot = (s * NT + t4) % 2
                C.wait("tensor", "wouta", 16 * DC)
                C.wait("tensor", "vcx", C.val("vcx"))
                for dc in range(DC):
                    bank = acquire_mm_bank()
                    for ec in range(DC):
                        C.prod("tensor", "tmm", 1, lambda e, dc=dc, ec=ec, bank=bank, b=b:
                               e.matmul(mm_ps[bank][:, 0:128],
                                        woutT[ec][:, dc * 128:(dc + 1) * 128],
                                        ctxT[b % 2][:, ec * 128:(ec + 1) * 128],
                                        start=(ec == 0), stop=False))
                    C.prod("tensor", "tmm", 1, lambda e, dc=dc, bank=bank: e.matmul(
                        mm_ps[bank][:, 0:128], bo_row[0:1, dc * 128:(dc + 1) * 128],
                        onesrbf[0:1, 0:128], start=False, stop=True))
                    C.wait("vector", "tmm")
                    xdst = xfm[slot][:, dc * 512 + b4 * 128: dc * 512 + b4 * 128 + 128]
                    C.prod("vector", f"mmfree{bank}", 1, lambda e, dc=dc, xdst=xdst, bank=bank,
                           s=s: e.scalar_tensor_tensor(xdst, mm_ps[bank][:, 0:128],
                                                       mv(MV_GA, dc, s), xdst, ALU.mult, ALU.add))

            def spill_x1(s, t4):
                slot = (s * NT + t4) % 2
                C.wait("gpsimd", "mmfree0", C.val("mmfree0"))
                C.wait("gpsimd", "mmfree1", C.val("mmfree1"))
                for c in range(DC):
                    C.prod("gpsimd", f"x1sp{slot}", 16, lambda e, c=c, s=s, t4=t4, slot=slot:
                           e.dma_start(out=x1_h[c, :, tok0(s, t4): tok0(s, t4) + 512],
                                       in_=xfm[slot][:, c * 512:(c + 1) * 512]))

            def process_block(s, b, filler):
                emit_attention(s, b, filler)
                emit_proj(s, b)
                emit_ln(s, b, second=True)
                if b % 4 == 3:
                    spill_x1(s, b // 4)

            pending = []
            for s in range(BPC):
                for t4 in range(NT):
                    gt4 = s * NT + t4
                    slot = gt4 % 2
                    C.wait("sync", f"x1sp{slot}", C.val(f"x1sp{slot}"))
                    C.wait("sync", "vh2m", C.val("vh2m"))
                    for c in range(DC):
                        C.prod("sync", f"qx{slot}", 16, lambda e, c=c, s=s, t4=t4, slot=slot:
                               e.dma_start(out=xfm[slot][:, c * 512:(c + 1) * 512],
                                           in_=x_fm[c, :, tok0(s, t4): tok0(s, t4) + 512]))
                    C.wait("vector", f"qx{slot}")
                    C.wait("scalar", f"qx{slot}")
                    C.wait("gpsimd", f"qx{slot}")
                    for b4 in range(4):
                        emit_ln(s, t4 * 4 + b4, second=False)
                    work = qkv_groups(s, t4)

                    def filler():
                        g = next(work, None)
                        if g is not None:
                            g()

                    for (ps, pb) in pending:
                        process_block(ps, pb, filler)
                    for g in work:
                        g()
                    pending = [(s, t4 * 4 + b4) for b4 in range(4)]
            for (ps, pb) in pending:
                process_block(ps, pb, lambda: None)
            C.run_block("A")

        with ExitStack() as esR:
            r_ps = esR.enter_context(nc.psum_tensor("r_ps", [1, 128], F32))
            rb_ps = esR.enter_context(nc.psum_tensor("rb_ps", [128, 16], F32))

            C.wait("vector", "vh2m", C.val("vh2m"))
            C.prodw("vector", "vh2m", 1, lambda e: e.tensor_reduce(
                runmax[:, 0:1], maxbuf[:, 0:BPC * NBLK], AX.X, ALU.max))
            C.wait("tensor", "vh2m", C.val("vh2m"))
            C.prod("tensor", "tarX", 1, lambda e: e.matmul(
                r_ps[:], runmax[:, 0:1], idf[:], start=True, stop=True))
            C.wait("vector", "tarX")
            C.prodw("vector", "varX", 1, lambda e: e.tensor_reduce(
                ccrow[0:1, 0:1], r_ps[:], AX.X, ALU.max))
            C.wait("gpsimd", "varX")
            C.prod("gpsimd", "ccwX", 16, lambda e: e.dma_start(out=cc_in.ap(), in_=ccrow[:]))
            C.wait("gpsimd", "ccwX")
            C.prod("gpsimd", "arQ", 1, lambda e: e.collective_compute(
                "AllReduce", ALU.max, replica_groups=[list(range(NCORE))],
                ins=[cc_in.ap()], outs=[cc_out.ap()]))
            C.wait("gpsimd", "arQ")
            C.prod("gpsimd", "ccrX", 16, lambda e: e.dma_start(out=ccrow[:], in_=cc_out.ap()))
            C.wait("vector", "ccrX")
            C.prodw("vector", "ardX", 1, lambda e: e.reciprocal(rows[0:1, 0:1], ccrow[0:1, 0:1]))
            C.prodw("vector", "ardX", 1, lambda e: e.tensor_scalar(
                rows[0:1, 0:1], rows[0:1, 0:1], 32767.0, None, ALU.mult))
            C.wait("tensor", "ardX")
            C.prod("tensor", "trbX", 1, lambda e: e.matmul(
                rb_ps[:, 0:1], onesrf[:], rows[0:1, 0:1], start=True, stop=True))
            C.wait("vector", "trbX")
            C.prod("vector", "vrbX", 1, lambda e: e.tensor_copy(scv(SC_INVX), rb_ps[:, 0:1]))
            if dbg:
                C.wait("gpsimd", "vrbX")
                C.prod("gpsimd", "dbgsc", 16, lambda e: e.dma_start(out=scal_h, in_=ccrow[:]))
            C.run_block("AR1")

        MG = MAGIC
        with ExitStack() as esF:
            sbF = lambda n, sh, dt=F32: esF.enter_context(nc.sbuf_tensor(n, sh, dt))
            wdT = [sbF(f"wdT{h}", [128, D], BF16) for h in range(HC)]

            with ExitStack() as esB:
                sbB = lambda n, sh, dt=F32: esB.enter_context(nc.sbuf_tensor(n, sh, dt))
                Rb = [sbB(f"Rb{c}", [128, TOK], BF16) for c in range(DC)]
                wraw = [sbB(f"wraw{i}", [128, D]) for i in range(2)]
                wtmp = sbB("wtmp", [128, D])
                wq32 = sbB("wq32", [128, D])
                blkmx = sbB("blkmx", [128, 32])
                binv = sbB("binv", [128, 32])
                bscl = sbB("bscl", [128, 32])
                wstr = [sbB(f"wstr{i}", [128, DC * 128], BF16) for i in range(2)]
                qtmp = [sbB(f"qtmp{i}", [128, 1024]) for i in range(2)]
                spst = [sbB(f"spst{i}", [128, TOK], BF16) for i in range(2)]
                mmB = [esB.enter_context(nc.psum_tensor(f"mmB{i}", [128, 512], F32))
                       for i in range(2)]
                wtps = esB.enter_context(nc.psum_tensor("wtps", [128, 1024], F32))

                for _eng in Ctx.ENGINES:
                    for nm in ("x1sp0", "x1sp1", "h2sp0", "h2sp1"):
                        C.wait(_eng, nm)
                    C.wait(_eng, "vrbX")

                h2in = [esB.enter_context(nc.sbuf_tensor(f"h2in{i}", [128, 1024], BF16))
                        for i in range(2)]
                for dc in range(DC):
                    for half in range(4):
                        hw = 1024
                        r2 = (dc * 4 + half) % 2
                        C.wait("sync", "b0q", max(0, C.val("b0q") - 1))
                        C.prod("sync", f"ldA{r2}", 16, lambda e, dc=dc, half=half, r2=r2:
                               e.dma_start(out=h2in[r2][:],
                                           in_=h2_h[dc, :, half * hw:(half + 1) * hw]))
                        C.wait("scalar", f"ldA{r2}")
                        C.wait("scalar", "b0q", max(0, C.val("b0q") - 1))
                        C.prod("scalar", "b0s", 1, lambda e, r2=r2: e.activation(
                            qtmp[r2][:], h2in[r2][:], AF.Identity, bias=mgc[:, 0:1], scale=scv(SC_INVX)))
                        C.wait("vector", "b0s")
                        C.prod("vector", "b0q", 1, lambda e, dc=dc, half=half, r2=r2:
                               e.tensor_scalar(Rb[dc][:, half * hw:(half + 1) * hw],
                                               qtmp[r2][:], -MG, None, ALU.add))
                C.wait("tensor", "b0q", C.val("b0q"))
                bqm_marks = []
                bdq_marks = []
                for wi, (wsrc, wdst, rmcol) in enumerate(
                        ((w_up, upr_h, 1), (w_gate, gtr_h, 2))):
                    for hc in range(HC):
                        wsl = (wi * HC + hc) % 2
                        C.wait("sync", "bdq", bdq_marks[-2] if len(bdq_marks) >= 2 else 0)
                        C.prod("sync", f"ldA{wsl}", 16, lambda e, wsrc=wsrc, hc=hc, wsl=wsl:
                               e.dma_start(out=wraw[wsl][:], in_=wsrc[hc * 128:(hc + 1) * 128, :]))
                        C.wait("vector", f"ldA{wsl}")
                        C.prodw("vector", "bdq", 1, lambda e, wsl=wsl: e.tensor_reduce(
                            blkmx[:], wraw[wsl][:].rearrange("p (a b) -> p a b", b=32),
                            AX.X, ALU.max, apply_absolute_value=True))
                        C.prodw("vector", "bdq", 1, lambda e: e.tensor_scalar(
                            bscl[:], blkmx[:], 1.0 / 7.0, 1e-12, ALU.mult, ALU.max))
                        C.prodw("vector", "bdq", 1, lambda e: e.reciprocal(binv[:], bscl[:]))
                        C.prod("vector", "bdq", 1, lambda e, wsl=wsl: e.tensor_tensor(
                            wtmp[:].rearrange("p (a b) -> p a b", b=32),
                            wraw[wsl][:].rearrange("p (a b) -> p a b", b=32),
                            binv[:].unsqueeze(2).broadcast_to([128, 32, 32]), ALU.mult))
                        bdq_marks.append(C.val("bdq"))
                        C.prod("vector", "bdq", 1, lambda e: e.tensor_scalar(
                            wtmp[:], wtmp[:], MG, -MG, ALU.add, ALU.add))
                        C.prod("vector", "bdq", 1, lambda e: e.tensor_tensor(
                            wq32[:].rearrange("p (a b) -> p a b", b=32),
                            wtmp[:].rearrange("p (a b) -> p a b", b=32),
                            bscl[:].unsqueeze(2).broadcast_to([128, 32, 32]), ALU.mult))
                        C.wait("tensor", "bdq")
                        C.wait("tensor", "bst", C.val("bst"))
                        for dc in range(DC):
                            C.prod("tensor", "btr", 1, lambda e, dc=dc: e.transpose(
                                wtps[:, dc * 128:(dc + 1) * 128], wq32[:, dc * 128:(dc + 1) * 128],
                                idf[:]))
                        C.wait("scalar", "btr")
                        C.wait("scalar", "bqm", bqm_marks[-2] if len(bqm_marks) >= 2 else 0)
                        C.prod("scalar", "bst", 1, lambda e, wsl=wsl: e.copy(
                            wstr[wsl][:], wtps[:]))
                        C.wait("tensor", "bst", C.val("bst"))
                        ssl = C.ctr("bse") % 2
                        C.ctr_inc("bse")
                        C.wait("scalar", f"sd{ssl}", C.val(f"sd{ssl}"))
                        for tt in range(NTT):
                            bank = C.counts["mmuse"] % 2
                            C.wait("tensor", f"mmfree{bank}", C.val(f"mmfree{bank}"))
                            C.counts["mmuse"] += 1
                            for dc in range(DC):
                                C.prod("tensor", "bqm", 1, lambda e, dc=dc, tt=tt, bank=bank,
                                       wsl=wsl: e.matmul(
                                           mmB[bank][:],
                                           wstr[wsl][:, dc * 128:(dc + 1) * 128],
                                           Rb[dc][:, tt * 512:(tt + 1) * 512],
                                           start=(dc == 0), stop=(dc == DC - 1)))
                            C.wait("scalar", "bqm")
                            C.prod("scalar", f"mmfree{bank}", 1, lambda e, bank=bank, ssl=ssl,
                                   tt=tt: e.copy(spst[ssl][:, tt * 512:(tt + 1) * 512],
                                                 mmB[bank][:]))
                            C.wait("vector", f"mmfree{bank}")
                            bcol = wi * 192 + hc * 8 + tt
                            C.prod("vector", "bsmx0", 1, lambda e, ssl=ssl, tt=tt, bcol=bcol:
                                   e.tensor_reduce(maxbuf[:, bcol:bcol + 1],
                                                   spst[ssl][:, tt * 512:(tt + 1) * 512],
                                                   AX.X, ALU.max, apply_absolute_value=True))
                        C.wait("gpsimd", "mmfree0", C.val("mmfree0"))
                        C.wait("gpsimd", "mmfree1", C.val(f"mmfree1"))
                        C.prod("gpsimd", f"sd{ssl}", 16, lambda e, wdst=wdst, hc=hc, ssl=ssl:
                               e.dma_start(out=wdst[hc, :, :], in_=spst[ssl][:]))
                        bqm_marks.append(C.val("bqm"))
            with ExitStack() as esR2:
                r_ps2 = esR2.enter_context(nc.psum_tensor("r_ps2", [1, 128], F32))
                rb_ps2 = esR2.enter_context(nc.psum_tensor("rb_ps2", [128, 16], F32))
                C.wait("vector", "bsmx0", C.val("bsmx0"))
                C.prod("vector", "bvmx", 1, lambda e: e.tensor_reduce(
                    runmax[:, 1:2], maxbuf[:, 0:192], AX.X, ALU.max))
                C.prodw("vector", "bvmx", 1, lambda e: e.tensor_reduce(
                    runmax[:, 2:3], maxbuf[:, 192:384], AX.X, ALU.max))
                C.prod("vector", "bvmx", 1, lambda e: e.memset(maxbuf[:], 0.0))
                C.wait("tensor", "bvmx", C.val("bvmx"))
                for j, col in enumerate((1, 2)):
                    C.wait("tensor", "varX", C.val("varX"))
                    C.prod("tensor", "tarX", 1, lambda e, col=col: e.matmul(
                        r_ps2[:], runmax[:, col:col + 1], idf[:], start=True, stop=True))
                    C.wait("vector", "tarX")
                    C.prodw("vector", "varX", 1, lambda e, j=j: e.tensor_reduce(
                        ccrow[0:1, j:j + 1], r_ps2[:], AX.X, ALU.max))
                C.wait("gpsimd", "varX")
                C.prod("gpsimd", "ccwX", 16, lambda e: e.dma_start(out=cc_in.ap(), in_=ccrow[:]))
                C.wait("gpsimd", "ccwX")
                C.prod("gpsimd", "arQ", 1, lambda e: e.collective_compute(
                    "AllReduce", ALU.max, replica_groups=[list(range(NCORE))],
                    ins=[cc_in.ap()], outs=[cc_out.ap()]))
                C.wait("gpsimd", "arQ")
                C.prod("gpsimd", "ccrX", 16, lambda e: e.dma_start(out=ccrow[:], in_=cc_out.ap()))
                C.wait("vector", "ccrX")
                C.prodw("vector", "ardX", 1, lambda e: e.reciprocal(rows[0:1, 21:22], scal[0:1, SC_INVX:SC_INVX + 1]))
                C.prodw("vector", "ardX", 1, lambda e: e.tensor_tensor(
                    rows[0:1, 16:17], ccrow[0:1, 0:1], rows[0:1, 21:22], ALU.mult))
                C.prodw("vector", "ardX", 1, lambda e: e.tensor_scalar(
                    rows[0:1, 16:17], rows[0:1, 16:17], 1.0 / 32767.0, None, ALU.mult))
                C.prodw("vector", "ardX", 1, lambda e: e.tensor_tensor(
                    rows[0:1, 17:18], ccrow[0:1, 1:2], rows[0:1, 21:22], ALU.mult))
                C.prodw("vector", "ardX", 1, lambda e: e.tensor_scalar(
                    rows[0:1, 17:18], rows[0:1, 17:18], 1.0 / 32767.0, None, ALU.mult))
                C.prodw("vector", "ardX", 1, lambda e: e.reciprocal(rows[0:1, 18:19], ccrow[0:1, 1:2]))
                C.prodw("vector", "ardX", 1, lambda e: e.tensor_scalar(
                    rows[0:1, 18:19], rows[0:1, 18:19], 32767.0, None, ALU.mult))
                C.prodw("vector", "ardX", 1, lambda e: e.reciprocal(rows[0:1, 19:20], ccrow[0:1, 0:1]))
                C.prodw("vector", "ardX", 1, lambda e: e.tensor_scalar(
                    rows[0:1, 19:20], rows[0:1, 19:20], 32767.0, None, ALU.mult))
                C.prodw("vector", "ardX", 1, lambda e: e.tensor_scalar(
                    rows[0:1, 20:21], rows[0:1, 17:18], -MG, None, ALU.mult))
                C.wait("tensor", "ardX")
                C.prod("tensor", "trbX", 1, lambda e: e.matmul(
                    rb_ps2[:, 0:4], onesrf[:], rows[0:1, 17:21], start=True, stop=True))
                C.wait("vector", "trbX")
                C.prod("vector", "vrbX", 1, lambda e: e.tensor_copy(scal[:, SC_SGT:SC_SGT + 1], rb_ps2[:, 0:1]))
                C.prod("vector", "vrbX", 1, lambda e: e.tensor_copy(scal[:, SC_KG:SC_KG + 1], rb_ps2[:, 1:2]))
                C.prod("vector", "vrbX", 1, lambda e: e.tensor_copy(scal[:, SC_KU:SC_KU + 1], rb_ps2[:, 2:3]))
                C.prodw("vector", "vrbX", 1, lambda e: e.tensor_copy(scal[:, SC_BGT:SC_BGT + 1], rb_ps2[:, 3:4]))
                C.run_block() if False else None

            with ExitStack() as esC:
                sbC = lambda n, sh, dt=F32: esC.enter_context(nc.sbuf_tensor(n, sh, dt))
                gg = [sbC(f"gg{i}", [128, 512]) for i in range(2)]
                sg = [sbC(f"sg{i}", [128, 512]) for i in range(2)]
                t2b = [sbC(f"t2b{i}", [128, 512]) for i in range(2)]
                s2c = [sbC(f"s2c{i}", [128, 512]) for i in range(2)]
                wrawC = [sbC(f"wrawC{i}", [128, HID]) for i in range(2)]
                wtmpC = sbC("wtmpC", [128, HID])
                wqC = sbC("wqC", [128, HID])
                bmxC = sbC("bmxC", [128, 96])
                bivC = sbC("bivC", [128, 96])
                bscC = sbC("bscC", [128, 96])
                wtpsC = esC.enter_context(nc.psum_tensor("wtpsC", [128, 1024], F32))

                for _eng in Ctx.ENGINES:
                    C.wait(_eng, "vrbX")
                    C.wait(_eng, "sd0"); C.wait(_eng, "sd1")
                for rc in range(DC):
                    wsl = rc % 2
                    C.wait("sync", "cdqv", max(0, C.val("cdqv") - 1))
                    C.prod("sync", f"ldA{wsl}", 16, lambda e, rc=rc, wsl=wsl: e.dma_start(
                        out=wrawC[wsl][:], in_=w_down[rc * 128:(rc + 1) * 128, :]))
                    C.wait("vector", f"ldA{wsl}")
                    C.prodw("vector", "cdq", 1, lambda e, wsl=wsl: e.tensor_reduce(
                        bmxC[:], wrawC[wsl][:].rearrange("p (a b) -> p a b", b=32),
                        AX.X, ALU.max, apply_absolute_value=True))
                    C.prodw("vector", "cdq", 1, lambda e: e.tensor_scalar(
                        bscC[:], bmxC[:], 1.0 / 7.0, 1e-12, ALU.mult, ALU.max))
                    C.prodw("vector", "cdq", 1, lambda e: e.reciprocal(bivC[:], bscC[:]))
                    C.prod("vector", "cdq", 1, lambda e, wsl=wsl: e.tensor_tensor(
                        wtmpC[:].rearrange("p (a b) -> p a b", b=32),
                        wrawC[wsl][:].rearrange("p (a b) -> p a b", b=32),
                        bivC[:].unsqueeze(2).broadcast_to([128, 96, 32]), ALU.mult))
                    C.prod("vector", "cdq", 1, lambda e: e.tensor_scalar(
                        wtmpC[:], wtmpC[:], MG, -MG, ALU.add, ALU.add))
                    C.prod("vector", "cdqv", 1, lambda e: e.tensor_tensor(
                        wqC[:].rearrange("p (a b) -> p a b", b=32),
                        wtmpC[:].rearrange("p (a b) -> p a b", b=32),
                        bscC[:].unsqueeze(2).broadcast_to([128, 96, 32]), ALU.mult))
                    C.wait("tensor", "cdqv")
                    for grp in range(HC // 8):
                        C.wait("tensor", "cst", C.val("cst"))
                        for bb in range(8):
                            hh = grp * 8 + bb
                            C.prod("tensor", "ctr", 1, lambda e, hh=hh, bb=bb: e.transpose(
                                wtpsC[:, bb * 128:(bb + 1) * 128],
                                wqC[:, hh * 128:(hh + 1) * 128], idf[:]))
                        C.wait("scalar", "ctr")
                        for bb in range(8):
                            hh = grp * 8 + bb
                            C.prod("scalar", "cst", 1, lambda e, hh=hh, bb=bb, rc=rc: e.copy(
                                wdT[hh][:, rc * 128:(rc + 1) * 128],
                                wtpsC[:, bb * 128:(bb + 1) * 128]))
                grow = [esC.enter_context(nc.sbuf_tensor(f"grow{i}", [128, TOK], BF16))
                        for i in range(2)]
                pprow = [esC.enter_context(nc.sbuf_tensor(f"pprow{i}", [128, TOK], BF16))
                        for i in range(2)]
                for hc in range(HC):
                    r2 = hc % 2
                    C.wait("sync", f"sd{r2}", C.val(f"sd{r2}"))
                    C.wait("sync", "cpp", max(0, C.val("cpp") - NTT))
                    C.prod("sync", f"ldA{r2}", 16, lambda e, hc=hc, r2=r2: e.dma_start(
                        out=grow[r2][:], in_=gtr_h[hc, :, :]))
                    C.wait("vector", f"ldA{r2}")
                    C.wait("vector", f"sd{r2}", C.val(f"sd{r2}"))
                    C.wait("scalar", f"ldA{r2}")
                    C.wait("gpsimd", f"ldA{r2}")
                    for tt in range(NTT):
                        t2_ = (hc * NTT + tt) % 2
                        gsl = grow[r2][:, tt * 512:(tt + 1) * 512]
                        psl = pprow[r2][:, tt * 512:(tt + 1) * 512]
                        C.wait("vector", "csg", max(0, C.val("csg") - 1))
                        C.prod("vector", "cgg", 1, lambda e, gsl=gsl, t2_=t2_: e.tensor_scalar(
                            gg[t2_][:], gsl, scv(SC_KG), MG, ALU.mult, ALU.add))
                        C.wait("scalar", "cgg")
                        C.wait("scalar", "cs2", max(0, C.val("cs2") - 1))
                        C.prod("scalar", "csg", 1, lambda e, t2_=t2_: e.activation(
                            sg[t2_][:], gg[t2_][:], AF.Sigmoid, bias=scv(SC_BGT), scale=scv(SC_SGT)))
                        C.prod("scalar", "ct2", 1, lambda e, t2_=t2_: e.activation(
                            t2b[t2_][:], sg[t2_][:], AF.Identity, bias=mgc[:, 0:1], scale=65536.0))
                        C.wait("vector", "ct2")
                        C.prod("vector", "cs2", 1, lambda e, t2_=t2_: e.tensor_scalar(
                            s2c[t2_][:], t2b[t2_][:], -MG, 65535.0, ALU.add, ALU.min))
                        C.prod("vector", "cpp0", 1, lambda e, t2_=t2_, psl=psl: e.scalar_tensor_tensor(
                            psl, gg[t2_][:], -MG, s2c[t2_][:], ALU.add, ALU.mult))
                        ccol = hc * 8 + tt
                        C.prod("vector", "cpp", 1, lambda e, psl=psl, ccol=ccol: e.tensor_reduce(
                            maxbuf[:, ccol:ccol + 1], psl, AX.X, ALU.max,
                            apply_absolute_value=True))
                    C.wait("gpsimd", "cpp0", C.val("cpp0"))
                    C.prod("gpsimd", f"sd{r2}", 16, lambda e, hc=hc, r2=r2: e.dma_start(
                        out=pp_h[hc, :, :], in_=pprow[r2][:]))

            with ExitStack() as esR3:
                r_ps3 = esR3.enter_context(nc.psum_tensor("r_ps3", [1, 128], F32))
                rb_ps3 = esR3.enter_context(nc.psum_tensor("rb_ps3", [128, 16], F32))
                C.wait("vector", "cpp", C.val("cpp"))
                C.prod("vector", "cpp", 1, lambda e: e.tensor_reduce(
                    runmax[:, 3:4], maxbuf[:, 0:192], AX.X, ALU.max))
                C.prodw("vector", "cpp", 1, lambda e: e.memset(maxbuf[:, 0:192], 0.0))
                C.wait("tensor", "cpp", C.val("cpp"))
                C.prod("tensor", "tarX", 1, lambda e: e.matmul(
                    r_ps3[:], runmax[:, 3:4], idf[:], start=True, stop=True))
                C.wait("vector", "tarX")
                C.prodw("vector", "varX", 1, lambda e: e.tensor_reduce(
                    ccrow[0:1, 0:1], r_ps3[:], AX.X, ALU.max))
                C.wait("gpsimd", "varX")
                C.prod("gpsimd", "ccwX", 16, lambda e: e.dma_start(out=cc_in.ap(), in_=ccrow[:]))
                C.wait("gpsimd", "ccwX")
                C.prod("gpsimd", "arQ", 1, lambda e: e.collective_compute(
                    "AllReduce", ALU.max, replica_groups=[list(range(NCORE))],
                    ins=[cc_in.ap()], outs=[cc_out.ap()]))
                C.wait("gpsimd", "arQ")
                C.prod("gpsimd", "ccrX", 16, lambda e: e.dma_start(out=ccrow[:], in_=cc_out.ap()))
                C.wait("vector", "ccrX")
                C.prodw("vector", "ardX", 1, lambda e: e.reciprocal(rows[0:1, 23:24], ccrow[0:1, 0:1]))
                C.prodw("vector", "ardX", 1, lambda e: e.tensor_scalar(
                    rows[0:1, 23:24], rows[0:1, 23:24], 32767.0, None, ALU.mult))
                C.prodw("vector", "ardX", 1, lambda e: e.tensor_tensor(
                    rows[0:1, 22:23], ccrow[0:1, 0:1], rows[0:1, 17:18], ALU.mult))
                C.prodw("vector", "ardX", 1, lambda e: e.tensor_scalar(
                    rows[0:1, 22:23], rows[0:1, 22:23], 1.0 / (65536.0 * 32767.0), None, ALU.mult))
                C.wait("tensor", "ardX")
                C.prod("tensor", "trbX", 1, lambda e: e.matmul(
                    rb_ps3[:, 0:1], onesrf[:], rows[0:1, 23:24], start=True, stop=True))
                C.wait("vector", "trbX")
                C.prodw("vector", "vrbX", 1, lambda e: e.tensor_copy(scv(SC_K1), rb_ps3[:, 0:1]))

            with ExitStack() as esD:
                sbD = lambda n, sh, dt=F32: esD.enter_context(nc.sbuf_tensor(n, sh, dt))
                t3b = [sbD(f"t3b{i}", [128, 512]) for i in range(2)]
                tub = [sbD(f"tub{i}", [128, 512]) for i in range(2)]
                ugb = [sbD(f"ugb{i}", [128, 512]) for i in range(2)]
                for _eng in Ctx.ENGINES:
                    C.wait(_eng, "vrbX")
                    C.wait(_eng, "sd0"); C.wait(_eng, "sd1")
                prow = [esD.enter_context(nc.sbuf_tensor(f"prow{i}", [128, TOK], BF16))
                        for i in range(2)]
                urow = [esD.enter_context(nc.sbuf_tensor(f"urow{i}", [128, TOK], BF16))
                        for i in range(2)]
                oorow = [esD.enter_context(nc.sbuf_tensor(f"oorow{i}", [128, TOK], BF16))
                         for i in range(2)]
                for hc in range(HC):
                    r2 = hc % 2
                    C.wait("sync", f"sd{r2}", C.val(f"sd{r2}"))
                    C.wait("sync", "doo", max(0, C.val("doo") - NTT))
                    C.prod("sync", f"ldA{r2}", 16, lambda e, hc=hc, r2=r2: e.dma_start(
                        out=prow[r2][:], in_=pp_h[hc, :, :]))
                    C.prod("sync", f"ldA{r2}", 16, lambda e, hc=hc, r2=r2: e.dma_start(
                        out=urow[r2][:], in_=upr_h[hc, :, :]))
                    C.wait("scalar", f"ldA{r2}")
                    C.wait("gpsimd", f"ldA{r2}")
                    C.wait("vector", f"sd{r2}", C.val(f"sd{r2}"))
                    for tt in range(NTT):
                        t2_ = (hc * NTT + tt) % 2
                        psl = prow[r2][:, tt * 512:(tt + 1) * 512]
                        usl = urow[r2][:, tt * 512:(tt + 1) * 512]
                        osl_ = oorow[r2][:, tt * 512:(tt + 1) * 512]
                        C.wait("scalar", "doo0", max(0, C.val("doo0") - 1))
                        C.prod("scalar", "dt3", 1, lambda e, t2_=t2_, psl=psl: e.activation(
                            t3b[t2_][:], psl, AF.Identity, bias=mgc[:, 0:1], scale=scv(SC_K1)))
                        C.prod("scalar", "dtu", 1, lambda e, t2_=t2_, usl=usl: e.activation(
                            tub[t2_][:], usl, AF.Identity, bias=mgc[:, 0:1], scale=scv(SC_KU)))
                        C.prod("scalar", "dug", 1, lambda e, t2_=t2_: e.activation(
                            ugb[t2_][:], tub[t2_][:], AF.Identity, bias=mgc[:, 1:2]))
                        C.wait("vector", "dug")
                        C.prod("vector", "doo0", 1, lambda e, t2_=t2_, osl_=osl_: e.scalar_tensor_tensor(
                            osl_, t3b[t2_][:], -MG, ugb[t2_][:], ALU.add, ALU.mult))
                        dcol = hc * 8 + tt
                        C.prod("vector", "doo", 1, lambda e, osl_=osl_, dcol=dcol: e.tensor_reduce(
                            maxbuf[:, dcol:dcol + 1], osl_, AX.X, ALU.max,
                            apply_absolute_value=True))
                    C.wait("gpsimd", "doo0", C.val("doo0"))
                    C.prod("gpsimd", f"sd{r2}", 16, lambda e, hc=hc, r2=r2: e.dma_start(
                        out=oo_h[hc, :, :], in_=oorow[r2][:]))

            with ExitStack() as esR4:
                r_ps4 = esR4.enter_context(nc.psum_tensor("r_ps4", [1, 128], F32))
                rb_ps4 = esR4.enter_context(nc.psum_tensor("rb_ps4", [128, 16], F32))
                C.wait("vector", "doo", C.val("doo"))
                C.prodw("vector", "doo", 1, lambda e: e.tensor_reduce(
                    runmax[:, 4:5], maxbuf[:, 0:192], AX.X, ALU.max))
                C.wait("tensor", "doo", C.val("doo"))
                C.prod("tensor", "tarX", 1, lambda e: e.matmul(
                    r_ps4[:], runmax[:, 4:5], idf[:], start=True, stop=True))
                C.wait("vector", "tarX")
                C.prodw("vector", "varX", 1, lambda e: e.tensor_reduce(
                    ccrow[0:1, 0:1], r_ps4[:], AX.X, ALU.max))
                C.wait("gpsimd", "varX")
                C.prod("gpsimd", "ccwX", 16, lambda e: e.dma_start(out=cc_in.ap(), in_=ccrow[:]))
                C.wait("gpsimd", "ccwX")
                C.prod("gpsimd", "arQ", 1, lambda e: e.collective_compute(
                    "AllReduce", ALU.max, replica_groups=[list(range(NCORE))],
                    ins=[cc_in.ap()], outs=[cc_out.ap()]))
                C.wait("gpsimd", "arQ")
                C.prod("gpsimd", "ccrX", 16, lambda e: e.dma_start(out=ccrow[:], in_=cc_out.ap()))
                C.wait("vector", "ccrX")
                C.prodw("vector", "ardX", 1, lambda e: e.reciprocal(rows[0:1, 24:25], ccrow[0:1, 0:1]))
                C.prodw("vector", "ardX", 1, lambda e: e.tensor_scalar(
                    rows[0:1, 24:25], rows[0:1, 24:25], 32767.0, None, ALU.mult))
                C.prodw("vector", "ardX", 1, lambda e: e.tensor_tensor(
                    rows[0:1, 25:26], ccrow[0:1, 0:1], rows[0:1, 22:23], ALU.mult))
                C.prodw("vector", "ardX", 1, lambda e: e.tensor_tensor(
                    rows[0:1, 25:26], rows[0:1, 25:26], rows[0:1, 16:17], ALU.mult))
                C.prodw("vector", "ardX", 1, lambda e: e.tensor_scalar(
                    rows[0:1, 25:26], rows[0:1, 25:26], 1.0 / 32767.0, None, ALU.mult))
                C.wait("tensor", "ardX")
                C.prod("tensor", "trbX", 1, lambda e: e.matmul(
                    rb_ps4[:, 0:2], onesrf[:], rows[0:1, 24:26], start=True, stop=True))
                C.wait("vector", "trbX")
                C.prod("vector", "vrbX", 1, lambda e: e.tensor_copy(scv(SC_K2), rb_ps4[:, 0:1]))
                C.prodw("vector", "vrbX", 1, lambda e: e.tensor_copy(scv(SC_SO), rb_ps4[:, 1:2]))
                C.prodw("vector", "vrbX", 1, lambda e: e.tensor_scalar(
                    modv[:, MV_GB * 16: MV_GB * 16 + 16],
                    modv[:, MV_GB * 16: MV_GB * 16 + 16], scv(SC_SO), None, ALU.mult))

            with ExitStack() as esE:
                sbE = lambda n, sh, dt=F32: esE.enter_context(nc.sbuf_tensor(n, sh, dt))
                oin = [sbE(f"oin{i}", [128, 512], BF16) for i in range(2)]
                t4b = [sbE(f"t4b{i}", [128, 512]) for i in range(2)]
                Ob = [sbE(f"Ob{i}", [128, 512], BF16) for i in range(HC)]
                x1t = [sbE(f"x1t{i}", [128, DC * 512]) for i in range(2)]
                outb = [sbE(f"outb{i}", [128, 512]) for i in range(4)]
                mmE = [esE.enter_context(nc.psum_tensor(f"mmE{i}", [128, 512], F32))
                       for i in range(2)]
                for _eng in Ctx.ENGINES:
                    C.wait(_eng, "vrbX")
                    C.wait(_eng, "sd0"); C.wait(_eng, "sd1")
                for tt in range(NTT):
                    s = tt // NT
                    xs = tt % 2
                    C.wait("sync", "mmfree0", C.val("mmfree0"))
                    C.wait("sync", "mmfree1", C.val("mmfree1"))
                    for dc in range(DC):
                        C.prod("sync", f"ex1{xs}", 16, lambda e, dc=dc, tt=tt, xs=xs: e.dma_start(
                            out=x1t[xs][:, dc * 512:(dc + 1) * 512],
                            in_=x1_h[dc, :, tt * 512:(tt + 1) * 512]))
                    for hc in range(HC):
                        r2 = (tt * HC + hc) % 2
                        C.wait("sync", "eO", max(0, C.val("eO") - 1))
                        C.prod("sync", f"ldA{r2}", 16, lambda e, hc=hc, tt=tt, r2=r2: e.dma_start(
                            out=oin[r2][:], in_=oo_h[hc, :, tt * 512:(tt + 1) * 512]))
                        C.wait("scalar", f"ldA{r2}")
                        C.wait("scalar", "eO", max(0, C.val("eO") - 1))
                        C.wait("scalar", "emm", C.val("emm"))
                        C.prod("scalar", "et4", 1, lambda e, r2=r2: e.activation(
                            t4b[r2][:], oin[r2][:], AF.Identity, bias=mgc[:, 0:1], scale=scv(SC_K2)))
                        C.prod("scalar", "eO", 1, lambda e, hc=hc, r2=r2: e.activation(
                            Ob[hc][:], t4b[r2][:], AF.Identity, bias=mgc[:, 1:2]))
                    C.wait("tensor", "eO", C.val("eO"))
                    C.wait("tensor", "cst", C.val("cst"))
                    C.wait("tensor", f"ex1{xs}", C.val(f"ex1{xs}"))
                    for dc in range(DC):
                        bank = C.counts["mmuse"] % 2
                        C.wait("tensor", f"mmfree{bank}", C.val(f"mmfree{bank}"))
                        C.counts["mmuse"] += 1
                        for hc in range(HC):
                            C.prod("tensor", "emm", 1, lambda e, dc=dc, hc=hc, bank=bank:
                                   e.matmul(mmE[bank][:], wdT[hc][:, dc * 128:(dc + 1) * 128],
                                            Ob[hc][:], start=(hc == 0), stop=(hc == HC - 1)))
                        osl = C.ctr("eev") % 4
                        C.wait("vector", "emm")
                        C.wait("vector", f"sd{osl}", C.val(f"sd{osl}"))
                        C.wait("vector", f"ex1{xs}", C.val(f"ex1{xs}"))
                        C.prod("vector", f"mmfree{bank}", 1, lambda e, dc=dc, s=s, xs=xs, osl=osl,
                               bank=bank: e.scalar_tensor_tensor(
                                   outb[osl][:], mmE[bank][:], mv(MV_GB, dc, s),
                                   x1t[xs][:, dc * 512:(dc + 1) * 512], ALU.mult, ALU.add))
                        C.ctr_inc("eev")
                        C.wait("gpsimd", f"mmfree{bank}", C.val(f"mmfree{bank}"))
                        C.prod("gpsimd", f"sd{osl}", 16, lambda e, dc=dc, tt=tt, osl=osl:
                               e.dma_start(out=out_fm[dc, :, tt * 512:(tt + 1) * 512],
                                           in_=outb[osl][:]))
                for i in range(4):
                    C.wait("gpsimd", f"sd{i}", C.val(f"sd{i}"))
                C.run_block("BE")

    return nc



def _prep_inputs(inputs):
    f32 = np.float32
    bf16 = ml_dtypes.bfloat16
    x = np.asarray(inputs["L0_attn_x"], f32)
    b_in = np.asarray(inputs["L0_attn_b_in"], f32)
    b_out = np.asarray(inputs["L0_attn_b_out"], f32)
    w_inT = np.ascontiguousarray(np.asarray(inputs["L0_attn_w_in"], f32).T)
    w_outT = np.ascontiguousarray(np.asarray(inputs["L0_attn_w_out"], f32).T)
    w_gate = np.asarray(inputs["L0_ffn_w_gate"], f32)
    w_up = np.asarray(inputs["L0_ffn_w_up"], f32)
    w_down = np.asarray(inputs["L0_ffn_w_down"], f32)
    c = np.asarray(inputs["L0_c"], f32)
    wmod = np.asarray(inputs["L0_wmod"], f32)

    ident_f = np.eye(128, dtype=f32)
    ident_bf = ident_f.astype(bf16)
    ones_r_f = np.ones((1, 128), f32)
    ones_r_bf = np.ones((1, 512), f32).astype(bf16)
    invD_c = np.full((128, 1), 1.0 / D, f32).astype(bf16)

    q = np.arange(128)[:, None]
    j = np.arange(384)[None, :]
    m_int = np.where((j >= q) & (j <= q + 256), 1.0, 0.0).astype(f32)
    m_first = np.where(np.abs(q - j) <= 128, 1.0, 0.0).astype(f32)
    m_last = np.where(j >= q + 128, 1.0, 0.0).astype(f32)
    masks = np.concatenate([m_int, m_first, m_last], axis=1).astype(bf16)

    in_maps = []
    for core in range(NCORE):
        sl = slice(core * BPC, (core + 1) * BPC)
        x_fmc = np.ascontiguousarray(
            x[sl].reshape(BPC * T, DC, 128).transpose(1, 2, 0))
        c_fmc = np.ascontiguousarray(c[sl].T.reshape(DC, 128, BPC))
        in_maps.append({
            "x_fm": x_fmc, "w_inT": w_inT, "b_in": b_in, "w_outT": w_outT,
            "b_out": b_out, "w_gate": w_gate, "w_up": w_up, "w_down": w_down,
            "c_fm": c_fmc, "wmod": wmod,
            "ident_bf": ident_bf, "ident_f": ident_f, "masks": masks,
            "ones_r_f": ones_r_f, "ones_r_bf": ones_r_bf, "invD_c_bf": invD_c,
        })
    return in_maps


def kernel(**inputs):
    in_maps = _prep_inputs(inputs)
    nc = build_nc(False)
    res = run_bass_kernel_spmd(nc, in_maps, core_ids=list(range(NCORE)))
    out = np.empty((B, T, D), np.float32)
    for core in range(NCORE):
        o = res.results[core]["out_fm"]
        out[core * BPC:(core + 1) * BPC] = o.transpose(2, 0, 1).reshape(BPC, T, D)
    return out


# version 15
# speedup vs baseline: 1.5976x; 1.0447x over previous
import numpy as np
import ml_dtypes
from contextlib import ExitStack

import concourse.bass as bass
import concourse.mybir as mybir
from concourse.bass_utils import run_bass_kernel_spmd

F32 = mybir.dt.float32
BF16 = mybir.dt.bfloat16
AF = mybir.ActivationFunctionType
ALU = mybir.AluOpType
AX = mybir.AxisListType

MAGIC = 12582912.0
NCORE = 8
B, T, D, HID = 16, 2048, 1024, 3072
NH, HDIM = 16, 64
BPC = B // NCORE
NBLK = T // 128
NT = T // 512
DC = D // 128
HC = HID // 128
TOK = BPC * T
NTT = TOK // 512


class Ctx:
    ENGINES = ("sync", "gpsimd", "vector", "scalar", "tensor")

    def __init__(self, nc, es):
        self.nc, self.es = nc, es
        self.sems, self.counts = {}, {}
        self.plan = {e: [] for e in self.ENGINES}

    def sem(self, name):
        if name not in self.sems:
            self.sems[name] = self.es.enter_context(self.nc.semaphore(name))
            self.counts[name] = 0
        return self.sems[name]

    def val(self, name):
        self.sem(name)
        return self.counts[name]

    def ctr(self, name):
        return self.counts.setdefault("#" + name, 0)

    def ctr_inc(self, name):
        self.counts["#" + name] = self.ctr(name) + 1

    def emit(self, eng, fn):
        self.plan[eng].append(fn)

    def wait(self, eng, name, value=None):
        h = self.sem(name)
        v = self.counts[name] if value is None else value
        if v > 0:
            self.emit(eng, lambda e, h=h, v=v: e.wait_ge(h, v))

    def prod(self, eng, name, k, fn):
        h = self.sem(name)
        self.counts[name] += k
        self.emit(eng, lambda e, h=h, k=k, fn=fn: fn(e).then_inc(h, k))

    def prodw(self, eng, name, k, fn):
        self.prod(eng, name, k, fn)
        self.wait(eng, name)

    def run_block(self, label=None):
        plan, self.plan = self.plan, {e: [] for e in self.ENGINES}
        import contextlib
        scope = self.nc.named_scope(label) if label else contextlib.nullcontext()
        with scope, self.nc.Block() as blk:
            blk.sync(lambda e: [f(e) for f in plan["sync"]])
            blk.gpsimd(lambda e: [f(e) for f in plan["gpsimd"]])
            blk.vector(lambda e: [f(e) for f in plan["vector"]])
            blk.scalar(lambda e: [f(e) for f in plan["scalar"]])
            blk.tensor(lambda e: [f(e) for f in plan["tensor"]])


def build_nc(dbg=False):
    nc = bass.Bass("TRN2", target_bir_lowering=False, detect_race_conditions=False)

    x_fm = nc.dram_tensor("x_fm", [DC, 128, TOK], F32, kind="ExternalInput").ap()
    w_inT = nc.dram_tensor("w_inT", [D, 3 * D], F32, kind="ExternalInput").ap()
    b_in = nc.dram_tensor("b_in", [3 * D], F32, kind="ExternalInput").ap()
    w_outT = nc.dram_tensor("w_outT", [D, D], F32, kind="ExternalInput").ap()
    b_out = nc.dram_tensor("b_out", [D], F32, kind="ExternalInput").ap()
    w_gate = nc.dram_tensor("w_gate", [HID, D], F32, kind="ExternalInput").ap()
    w_up = nc.dram_tensor("w_up", [HID, D], F32, kind="ExternalInput").ap()
    w_down = nc.dram_tensor("w_down", [D, HID], F32, kind="ExternalInput").ap()
    c_fm = nc.dram_tensor("c_fm", [DC, 128, BPC], F32, kind="ExternalInput").ap()
    wmod = nc.dram_tensor("wmod", [D, 6 * D], F32, kind="ExternalInput").ap()
    ident_bf = nc.dram_tensor("ident_bf", [128, 128], BF16, kind="ExternalInput").ap()
    ident_f = nc.dram_tensor("ident_f", [128, 128], F32, kind="ExternalInput").ap()
    masks = nc.dram_tensor("masks", [128, 3 * 384], BF16, kind="ExternalInput").ap()
    ones_r_f = nc.dram_tensor("ones_r_f", [1, 128], F32, kind="ExternalInput").ap()
    ones_r_bf = nc.dram_tensor("ones_r_bf", [1, 512], BF16, kind="ExternalInput").ap()
    invD_c_bf = nc.dram_tensor("invD_c_bf", [128, 1], BF16, kind="ExternalInput").ap()

    out_fm = nc.dram_tensor("out_fm", [DC, 128, TOK], F32, kind="ExternalOutput").ap()

    skind = "ExternalOutput" if dbg else "Internal"
    x1_h = nc.dram_tensor("x1_h", [DC, 128, TOK], F32, kind=skind).ap()
    h2_h = nc.dram_tensor("h2_h", [DC, 128, TOK], BF16, kind=skind).ap()
    upr_h = nc.dram_tensor("upr_h", [HC, 128, TOK], BF16, kind=skind).ap()
    gtr_h = nc.dram_tensor("gtr_h", [HC, 128, TOK], BF16, kind=skind).ap()
    pp_h = nc.dram_tensor("pp_h", [HC, 128, TOK], BF16, kind=skind).ap()
    oo_h = nc.dram_tensor("oo_h", [HC, 128, TOK], BF16, kind=skind).ap()
    scal_h = (nc.dram_tensor("scal_h", [1, 16], F32, kind="ExternalOutput").ap()
              if dbg else None)
    if dbg:
        tap_h = nc.dram_tensor("tap_h", [128, 4096], BF16, kind="ExternalOutput").ap()
        tap_q = nc.dram_tensor("tap_q", [128, 4096], BF16, kind="ExternalOutput").ap()
        tap_k = nc.dram_tensor("tap_k", [128, 4096], BF16, kind="ExternalOutput").ap()
        tap_v = nc.dram_tensor("tap_v", [128, 1024], BF16, kind="ExternalOutput").ap()
        tap_c = nc.dram_tensor("tap_c", [128, 1024], BF16, kind="ExternalOutput").ap()
        tap_ab = nc.dram_tensor("tap_ab", [128, 256], F32, kind="ExternalOutput").ap()
        tap_e = nc.dram_tensor("tap_e", [128, 384], BF16, kind="ExternalOutput").ap()
        tap_at = nc.dram_tensor("tap_at", [128, 384], BF16, kind="ExternalOutput").ap()
        tap_d = nc.dram_tensor("tap_d", [128, 128], BF16, kind="ExternalOutput").ap()
        tap_sm = nc.dram_tensor("tap_sm", [128, 4], F32, kind="ExternalOutput").ap()
    cc_in = nc.dram_tensor("cc_in", [1, 16], F32, kind="Internal")
    cc_out = nc.dram_tensor("cc_out", [1, 16], F32, kind="Internal")

    es = ExitStack()
    with es:
        C = Ctx(nc, es)
        sbuf = lambda n, s, dt=F32: es.enter_context(nc.sbuf_tensor(n, s, dt))

        idbf = sbuf("idbf", [128, 128], BF16)
        idf = sbuf("idf", [128, 128], F32)
        msk = sbuf("msk", [128, 3 * 384], BF16)
        onesrf = sbuf("onesrf", [1, 128], F32)
        onesrbf = sbuf("onesrbf", [1, 512], BF16)
        invDc = sbuf("invDc", [128, 1], BF16)
        bq_col = sbuf("bq_col", [128, 8], F32)
        bk_col = sbuf("bk_col", [128, 8], F32)
        bv_row = sbuf("bv_row", [1, 1024], BF16)
        bo_row = sbuf("bo_row", [1, 1024], BF16)
        modv = sbuf("modv", [128, 6 * 16], F32)
        scal = sbuf("scal", [128, 16], F32)
        runmax = sbuf("runmax", [128, 8], F32)
        rows = sbuf("rows", [1, 512], F32)
        ccrow = sbuf("ccrow", [1, 16], F32)
        gascr = sbuf("gascr", [128, 4], F32)
        mgc = sbuf("mgc", [128, 2], F32)
        maxbuf = sbuf("maxbuf", [128, 384], F32)

        MV_ONEPCA, MV_SA, MV_GA, MV_ONEPCB, MV_SB, MV_GB = range(6)

        def mv(which, c, s):
            j = which * 16 + c * BPC + s
            return modv[:, j: j + 1]

        SC_INVX, SC_KG, SC_SGT, SC_BGT, SC_KU, SC_K1, SC_K2, SC_SO = range(8)
        scv = lambda j: scal[:, j: j + 1]

        with ExitStack() as es0:
            sb0 = lambda n, s, dt=F32: es0.enter_context(nc.sbuf_tensor(n, s, dt))
            cfm = sb0("cfm", [128, DC * BPC])
            silc = sb0("silc", [128, DC * BPC])
            wmt = sb0("wmt", [128, 2 * 512])
            modr = sb0("modr", [BPC, 6 * D])
            bcol_raw = sb0("bcol_raw", [128, 16])
            bvr_f = sb0("bvr_f", [1, 1024])
            bor_f = sb0("bor_f", [1, 1024])
            mps = es0.enter_context(nc.psum_tensor("mps", [BPC, 512], F32))
            tps = es0.enter_context(nc.psum_tensor("tps", [128, 96], F32))

            loads = [(idbf, ident_bf), (idf, ident_f), (msk, masks),
                     (onesrf, ones_r_f), (onesrbf, ones_r_bf), (invDc, invD_c_bf),
                     (bvr_f, b_in[2 * D: 3 * D].unsqueeze(0)),
                     (bor_f, b_out.unsqueeze(0))]
            for dst, src in loads:
                C.prod("sync", "p0ld", 16, lambda e, d=dst, s=src: e.dma_start(out=d[:], in_=s))
            C.prod("sync", "p0ld", 16, lambda e: e.dma_start(
                out=cfm[:].rearrange("p (c s) -> p c s", s=BPC),
                in_=c_fm.rearrange("c p s -> p c s"),
                allow_slow_non_contiguous=True))
            C.prod("sync", "p0ld", 16, lambda e: e.dma_start(
                out=bcol_raw[:], in_=b_in[0: 2 * D].rearrange("(c p) -> p c", p=128),
                allow_slow_non_contiguous=True))

            C.wait("vector", "p0ld")
            C.prod("vector", "p0v", 1, lambda e: e.tensor_copy(bv_row[:], bvr_f[:]))
            C.prod("vector", "p0v", 1, lambda e: e.tensor_copy(bo_row[:], bor_f[:]))
            C.prod("vector", "p0v", 1, lambda e: e.tensor_scalar(
                bq_col[:], bcol_raw[:, 0:8], 0.125, None, ALU.mult))
            C.prod("vector", "p0v", 1, lambda e: e.tensor_copy(bk_col[:], bcol_raw[:, 8:16]))
            C.prod("vector", "p0v", 1, lambda e: e.memset(runmax[:], 0.0))
            C.prod("vector", "p0v", 1, lambda e: e.memset(ccrow[:], 0.0))
            C.prod("vector", "p0v", 1, lambda e: e.memset(mgc[:, 0:1], MAGIC))
            C.prod("vector", "p0v", 1, lambda e: e.memset(mgc[:, 1:2], -MAGIC))
            C.prod("vector", "p0v", 1, lambda e: e.memset(maxbuf[:], 0.0))
            C.wait("scalar", "p0ld")
            C.prod("scalar", "p0s", 1, lambda e: e.activation(silc[:], cfm[:], AF.Silu))

            C.wait("tensor", "p0s")
            for j in range(12):
                C.wait("tensor", "p0ev", j)
                for dc in range(DC):
                    i = j * DC + dc
                    slot = i % 2
                    C.wait("sync", "p0mm", max(0, i - 1))
                    C.prod("sync", f"p0wld{slot}", 16, lambda e, j=j, dc=dc, slot=slot: e.dma_start(
                        out=wmt[:, slot * 512: slot * 512 + 512],
                        in_=wmod[dc * 128:(dc + 1) * 128, j * 512:(j + 1) * 512]))
                    C.wait("tensor", f"p0wld{slot}")
                    C.prod("tensor", "p0mm", 1, lambda e, dc=dc, slot=slot: e.matmul(
                        mps[:, 0:512], silc[:, dc * BPC:(dc + 1) * BPC],
                        wmt[:, slot * 512: slot * 512 + 512],
                        start=(dc == 0), stop=(dc == DC - 1)))
                C.wait("vector", "p0mm")
                C.prod("vector", "p0ev", 1, lambda e, j=j: e.tensor_copy(
                    modr[:, j * 512:(j + 1) * 512], mps[:, 0:512]))

            C.wait("tensor", "p0ev")
            for k in range(48):
                C.prod("tensor", "p0tr", 1, lambda e, k=k: e.transpose(
                    tps[:, k * BPC:(k + 1) * BPC],
                    modr[:, k * 128:(k + 1) * 128], idf[0:BPC, 0:BPC]))
            C.wait("vector", "p0tr")
            C.wait("scalar", "p0tr")

            def tcol(v, c):
                k = v * 8 + c
                return tps[:, k * BPC:(k + 1) * BPC]

            for c in range(DC):
                C.prod("vector", "p0d", 1, lambda e, c=c: e.tensor_copy(
                    modv[:, MV_SA * 16 + c * BPC: MV_SA * 16 + (c + 1) * BPC], tcol(0, c)))
                C.prod("vector", "p0d", 1, lambda e, c=c: e.tensor_scalar(
                    modv[:, MV_ONEPCA * 16 + c * BPC: MV_ONEPCA * 16 + (c + 1) * BPC],
                    tcol(1, c), 1.0, None, ALU.add))
                C.prod("vector", "p0d", 1, lambda e, c=c: e.tensor_copy(
                    modv[:, MV_SB * 16 + c * BPC: MV_SB * 16 + (c + 1) * BPC], tcol(3, c)))
                C.prod("vector", "p0d", 1, lambda e, c=c: e.tensor_scalar(
                    modv[:, MV_ONEPCB * 16 + c * BPC: MV_ONEPCB * 16 + (c + 1) * BPC],
                    tcol(4, c), 1.0, None, ALU.add))
                for (vec, mvidx) in ((2, MV_GA), (5, MV_GB)):
                    C.prod("scalar", "p0ga", 1, lambda e, vec=vec, c=c: e.activation(
                        gascr[:, 0:BPC], tcol(vec, c), AF.Abs))
                    C.prod("scalar", "p0ga", 1, lambda e, vec=vec, c=c: e.activation(
                        gascr[:, 2:2 + BPC], tcol(vec, c), AF.Sign))
                    C.wait("vector", "p0ga")
                    C.prodw("vector", "p0gb", 1, lambda e: e.tensor_scalar(
                        gascr[:, 0:BPC], gascr[:, 0:BPC], 0.25, None, ALU.max))
                    C.prod("vector", "p0gb", 1, lambda e, mvidx=mvidx, c=c: e.tensor_tensor(
                        modv[:, mvidx * 16 + c * BPC: mvidx * 16 + (c + 1) * BPC],
                        gascr[:, 0:BPC], gascr[:, 2:2 + BPC], ALU.mult))
                    C.wait("scalar", "p0gb")
            C.run_block("P0")

        with ExitStack() as esA:
            sbA = lambda n, sh, dt=F32: esA.enter_context(nc.sbuf_tensor(n, sh, dt))
            winT = [sbA(f"winT{c}", [128, 3 * D], BF16) for c in range(DC)]
            woutT = [sbA(f"woutT{c}", [128, D], BF16) for c in range(DC)]
            xfm = [sbA(f"xfm{i}", [128, DC * 512]) for i in range(2)]
            hbuf = [sbA(f"hbuf{i}", [128, DC * 512], BF16) for i in range(2)]
            QT = [sbA(f"QT{i}", [128, DC * 512], BF16) for i in range(2)]
            KT = [sbA(f"KT{i}", [128, DC * 512], BF16) for i in range(3)]
            VT = [sbA(f"VT{i}", [128, 1024], BF16) for i in range(9)]
            ctxT = [sbA(f"ctxT{i}", [128, DC * 128], BF16) for i in range(2)]
            xbf = sbA("xbf", [128, 1024], BF16)
            xsq = sbA("xsq", [128, 1024], BF16)
            ubuf = sbA("ubuf", [128, 1024])
            bcsb = sbA("bcsb", [128, 256])
            h2st = [sbA(f"h2st{i}", [128, 1024], BF16) for i in range(2)]
            expb = [sbA(f"expb{i}", [128, 384], BF16) for i in range(2)]
            mexp = [sbA(f"mexp{i}", [128, 384], BF16) for i in range(2)]
            atb = [sbA(f"atb{i}", [128, 384], BF16) for i in range(2)]
            Dt = [sbA(f"Dt{i}", [128, 128], BF16) for i in range(2)]
            nmx = sbA("nmx", [128, 4])
            sume = sbA("sume", [128, 4])

            st_ps = esA.enter_context(nc.psum_tensor("st_ps", [128, 256], F32))
            mm_ps = [esA.enter_context(nc.psum_tensor(f"mm_ps{i}", [128, 512], F32))
                     for i in range(2)]
            sc_ps = [esA.enter_context(nc.psum_tensor(f"sc_ps{i}", [128, 384], F32))
                     for i in range(2)]
            at_ps = [esA.enter_context(nc.psum_tensor(f"at_ps{i}", [128, 384], F32))
                     for i in range(2)]
            cx_ps = esA.enter_context(nc.psum_tensor("cx_ps", [128, 128], F32))

            for _eng in Ctx.ENGINES:
                C.wait(_eng, "p0gb")
                C.wait(_eng, "p0ld")
                C.wait(_eng, "p0v")

            for c in range(DC):
                C.prod("gpsimd", "wina", 16, lambda e, c=c: e.dma_start(
                    out=winT[c][:], in_=w_inT[c * 128:(c + 1) * 128, :]))
            for c in range(DC):
                C.prod("gpsimd", "wouta", 16, lambda e, c=c: e.dma_start(
                    out=woutT[c][:], in_=w_outT[c * 128:(c + 1) * 128, :]))

            tok0 = lambda s, t4: s * T + t4 * 512
            C.counts.setdefault("mmuse", 0)
            qkv_marks = {}

            def acquire_mm_bank():
                bank = C.counts["mmuse"] % 2
                C.wait("tensor", f"mmfree{bank}", C.val(f"mmfree{bank}"))
                C.counts["mmuse"] += 1
                return bank

            def emit_ln(s, b, second):
                t4, b4 = b // 4, b % 4
                gt4 = s * NT + t4
                slot, r2 = gt4 % 2, b % 2
                xin = (xfm[slot].ap()
                       .rearrange("p (c w) -> p c w", w=512)[:, :, b4 * 128:(b4 + 1) * 128])
                C.wait("vector", "tst", C.val("tst"))
                C.prod("vector", "vxbf", 1, lambda e, xin=xin: e.tensor_copy(
                    xbf[:].rearrange("p (c w) -> p c w", w=128), xin))
                C.wait("scalar", "tst", C.val("tst"))
                C.prod("scalar", "sxsq", 1, lambda e, xin=xin: e.activation(
                    xsq[:].rearrange("p (c w) -> p c w", w=128), xin, AF.Square))
                C.wait("tensor", "vbc", C.val("vbc"))
                C.wait("tensor", "vxbf")
                C.wait("tensor", "sxsq")
                for dc in range(DC):
                    C.prod("tensor", "tst", 1, lambda e, dc=dc: e.matmul(
                        st_ps[0:1, 0:128], invDc[:], xbf[:, dc * 128:(dc + 1) * 128],
                        start=(dc == 0), stop=False))
                for dc in range(DC):
                    C.prod("tensor", "tst", 1, lambda e, dc=dc: e.matmul(
                        st_ps[0:1, 128:256], invDc[:], xsq[:, dc * 128:(dc + 1) * 128],
                        start=False, stop=(dc == DC - 1)))
                C.wait("vector", "tst")
                C.prodw("vector", "vst", 1, lambda e: e.tensor_copy(rows[0:1, 0:256], st_ps[0:1, 0:256]))
                C.prodw("vector", "vst", 1, lambda e: e.tensor_tensor(
                    rows[0:1, 256:384], rows[0:1, 0:128], rows[0:1, 0:128], ALU.mult))
                C.prodw("vector", "vst", 1, lambda e: e.tensor_tensor(
                    rows[0:1, 256:384], rows[0:1, 128:256], rows[0:1, 256:384], ALU.subtract))
                C.prodw("vector", "vst", 1, lambda e: e.tensor_scalar(
                    rows[0:1, 256:384], rows[0:1, 256:384], 1e-5, None, ALU.add))
                C.wait("scalar", "vst")
                C.prodw("scalar", "sst", 1, lambda e: e.activation(
                    rows[0:1, 384:512], rows[0:1, 256:384], AF.Ln))
                C.prodw("scalar", "sst", 1, lambda e: e.activation(
                    rows[0:1, 384:512], rows[0:1, 384:512], AF.Exp, scale=-0.5))
                C.wait("vector", "sst")
                C.prodw("vector", "vst2", 1, lambda e: e.tensor_tensor(
                    rows[0:1, 256:384], rows[0:1, 0:128], rows[0:1, 384:512], ALU.mult))
                C.prodw("vector", "vst2", 1, lambda e: e.tensor_scalar(
                    rows[0:1, 256:384], rows[0:1, 256:384], -1.0, None, ALU.mult))
                C.wait("tensor", "vst2")
                C.prod("tensor", "tbc", 1, lambda e: e.matmul(
                    st_ps[:, 0:128], onesrf[:], rows[0:1, 384:512], start=True, stop=False))
                C.prod("tensor", "tbc", 1, lambda e: e.matmul(
                    st_ps[:, 128:256], onesrf[:], rows[0:1, 256:384], start=False, stop=True))
                C.wait("vector", "tbc")
                C.wait("vector", "gu", C.val("gu"))
                C.prod("vector", "vbc", 1, lambda e: e.tensor_copy(bcsb[:], st_ps[:]))
                C.wait("gpsimd", "vbc")
                C.wait("gpsimd", "vh", C.val("vh"))
                C.wait("gpsimd", "vh2", C.val("vh2"))
                a_bc = bcsb[:, 0:128].unsqueeze(1).broadcast_to([128, 8, 128])
                b_bc = bcsb[:, 128:256].unsqueeze(1).broadcast_to([128, 8, 128])
                C.prod("gpsimd", "gu", 1, lambda e, xin=xin, a_bc=a_bc: e.tensor_tensor(
                    ubuf[:].rearrange("p (c w) -> p c w", w=128), xin, a_bc, ALU.mult))
                C.prod("gpsimd", "gu", 1, lambda e, b_bc=b_bc: e.tensor_tensor(
                    ubuf[:].rearrange("p (c w) -> p c w", w=128),
                    ubuf[:].rearrange("p (c w) -> p c w", w=128), b_bc, ALU.add))
                C.wait("vector", "gu")
                if second:
                    C.wait("vector", f"h2sp{r2}", C.val(f"h2sp{r2}"))
                for c in range(DC):
                    u2c = ubuf[:, c * 128:(c + 1) * 128]
                    if not second:
                        hdst = hbuf[slot][:, c * 512 + b4 * 128: c * 512 + b4 * 128 + 128]
                        C.prod("vector", "vh", 1, lambda e, u2c=u2c, hdst=hdst, c=c, s=s:
                               e.tensor_scalar(hdst, u2c, mv(MV_ONEPCA, c, s),
                                               mv(MV_SA, c, s), ALU.mult, ALU.add))
                    else:
                        hdst = h2st[r2][:, c * 128:(c + 1) * 128]
                        C.prod("vector", "vh2", 1, lambda e, u2c=u2c, hdst=hdst, c=c, s=s:
                               e.tensor_scalar(hdst, u2c, mv(MV_ONEPCB, c, s),
                                               mv(MV_SB, c, s), ALU.mult, ALU.add))
                if not second:
                    C.wait("vector", "vh")
                    for c in range(DC):
                        hdst = hbuf[slot][:, c * 512 + b4 * 128: c * 512 + b4 * 128 + 128]
                        xdst = xfm[slot][:, c * 512 + b4 * 128: c * 512 + b4 * 128 + 128]
                        C.prod("vector", "vx1", 1, lambda e, hdst=hdst, xdst=xdst, c=c, s=s:
                               e.scalar_tensor_tensor(xdst, hdst, mv(MV_GA, c, s), xdst,
                                                      ALU.mult, ALU.add))
                else:
                    C.wait("vector", "vh2")
                    mcol = s * NBLK + b
                    C.prod("vector", "vh2m", 1, lambda e, r2=r2, mcol=mcol: e.tensor_reduce(
                        maxbuf[:, mcol:mcol + 1], h2st[r2][:], AX.X, ALU.max,
                        apply_absolute_value=True))
                    C.wait("gpsimd", "vh2", C.val("vh2"))
                    tk = s * T + b * 128
                    for c in range(DC):
                        C.prod("gpsimd", f"h2sp{r2}", 16, lambda e, c=c, tk=tk, r2=r2: e.dma_start(
                            out=h2_h[c, :, tk: tk + 128], in_=h2st[r2][:, c * 128:(c + 1) * 128]))

            def qkv_groups(s, t4):
                gt4 = s * NT + t4
                slot = gt4 % 2

                def qk_group(ec):
                    def go():
                        bank = acquire_mm_bank()
                        for dc in range(DC):
                            C.prod("tensor", "tmm", 1, lambda e, ec=ec, dc=dc, bank=bank,
                                   slot=slot: e.matmul(
                                       mm_ps[bank][:],
                                       winT[dc][:, ec * 128:(ec + 1) * 128],
                                       hbuf[slot][:, dc * 512:(dc + 1) * 512],
                                       start=(dc == 0), stop=(dc == DC - 1)))
                        C.wait("scalar", "tmm")
                        if ec < 8:
                            dst = QT[slot][:, ec * 512:(ec + 1) * 512]
                            bias, scale = bq_col[:, ec:ec + 1], 0.125
                        else:
                            dst = KT[gt4 % 3][:, (ec - 8) * 512:(ec - 7) * 512]
                            bias, scale = bk_col[:, ec - 8:ec - 7], 1.0
                        C.prod("scalar", f"mmfree{bank}", 1, lambda e, dst=dst, bias=bias,
                               scale=scale, bank=bank: e.activation(
                                   dst, mm_ps[bank][:], AF.Identity, bias=bias, scale=scale))
                    return go

                def v_group(b4, g):
                    def go():
                        b = t4 * 4 + b4
                        gb = s * NBLK + b
                        bank = acquire_mm_bank()
                        for dc in range(DC):
                            C.prod("tensor", "tmm", 1, lambda e, b4=b4, g=g, dc=dc, bank=bank,
                                   slot=slot: e.matmul(
                                       mm_ps[bank][:],
                                       hbuf[slot][:, dc * 512 + b4 * 128: dc * 512 + b4 * 128 + 128],
                                       winT[dc][:, 2 * D + g * 512: 2 * D + (g + 1) * 512],
                                       start=(dc == 0), stop=False))
                        C.prod("tensor", "tmm", 1, lambda e, g=g, bank=bank: e.matmul(
                            mm_ps[bank][:], onesrbf[0:1, 0:128],
                            bv_row[0:1, g * 512:(g + 1) * 512], start=False, stop=True))
                        C.wait("scalar", "tmm")
                        C.prod("scalar", f"mmfree{bank}", 1, lambda e, gb=gb, g=g, bank=bank:
                               e.activation(VT[gb % 9][:, g * 512:(g + 1) * 512],
                                            mm_ps[bank][:], AF.Identity))
                    return go

                C.wait("tensor", "wina", 16 * DC)
                C.wait("tensor", "vh", C.val("vh"))
                for ec in range(8, 16):
                    yield qk_group(ec)
                for g in range(2):
                    yield v_group(0, g)
                for ec in range(8):
                    yield qk_group(ec)
                for b4 in range(1, 4):
                    for g in range(2):
                        yield v_group(b4, g)

            attmk = {}

            def emit_attention(s, b, filler):
                t4 = b // 4
                gt4 = s * NT + t4
                kb0 = min(max(b - 1, 0), NBLK - 3)
                mvar = 1 if b == 0 else (2 if b == NBLK - 1 else 0)
                C.wait("tensor", "mmfree0", C.val("mmfree0"))
                C.wait("tensor", "mmfree1", C.val("mmfree1"))
                G0 = C.ctr("hctr")

                def St(h):
                    G = G0 + h
                    sl = G % 2
                    hc, hr = h // 2, (h % 2) * 64
                    if ("se", G - 2) in attmk:
                        C.wait("tensor", "sexp", attmk[("se", G - 2)])
                    qap = QT[gt4 % 2][hr:hr + 64,
                                      hc * 512 + (b % 4) * 128: hc * 512 + (b % 4) * 128 + 128]
                    for kk in range(3):
                        kb = kb0 + kk
                        kap = KT[(s * NT + kb // 4) % 3][hr:hr + 64,
                                                        hc * 512 + (kb % 4) * 128: hc * 512 + (kb % 4) * 128 + 128]
                        C.prod("tensor", "tsc", 1, lambda e, sl=sl, kk=kk, qap=qap, kap=kap:
                               e.matmul(sc_ps[sl][:, kk * 128:(kk + 1) * 128], qap, kap,
                                        start=(kk == 0), stop=(kk == 2)))
                    attmk[("st", G)] = C.val("tsc")

                def Vn(h):
                    G = G0 + h
                    sl = G % 2
                    C.wait("vector", "tsc", attmk[("st", G)])
                    C.prod("vector", "vnm", 1, lambda e, sl=sl: e.tensor_reduce(
                        nmx[:, sl:sl + 1], sc_ps[sl][:], AX.X, ALU.max, negate=True))
                    attmk[("vn", G)] = C.val("vnm")

                def Se(h):
                    G = G0 + h
                    sl = G % 2
                    C.wait("scalar", "vnm", attmk[("vn", G)])
                    if ("ta", G - 2) in attmk:
                        C.wait("scalar", "tat", attmk[("ta", G - 2)])
                    C.prod("scalar", "sexp", 1, lambda e, sl=sl: e.activation(
                        expb[sl][:], sc_ps[sl][:], AF.Exp, bias=nmx[:, sl:sl + 1]))
                    attmk[("se", G)] = C.val("sexp")

                def Vd(h):
                    G = G0 + h
                    sl = G % 2
                    C.wait("vector", "sexp", attmk[("se", G)])
                    if ("ta", G - 2) in attmk:
                        C.wait("vector", "tat", attmk[("ta", G - 2)])
                    C.prod("vector", "vmk", 1, lambda e, sl=sl, mvar=mvar: e.scalar_tensor_tensor(
                        mexp[sl][:], expb[sl][:], 1.0, msk[:, mvar * 384:(mvar + 1) * 384],
                        ALU.mult, ALU.mult, accum_out=sume[:, sl:sl + 1]))
                    C.wait("vector", "vmk")
                    C.prodw("vector", "vrc", 1, lambda e, sl=sl: e.reciprocal(
                        sume[:, sl + 2:sl + 3], sume[:, sl:sl + 1]))
                    C.prod("vector", "vD", 1, lambda e, sl=sl: e.tensor_scalar(
                        Dt[sl][:], idbf[:], sume[:, sl + 2:sl + 3], None, ALU.mult))
                    attmk[("vd", G)] = C.val("vD")

                def Ta(h):
                    G = G0 + h
                    sl = G % 2
                    C.wait("tensor", "vD", attmk[("vd", G)])
                    if ("va", G - 2) in attmk:
                        C.wait("tensor", "vat", attmk[("va", G - 2)])
                    for kk in range(3):
                        C.prod("tensor", "tat", 1, lambda e, sl=sl, kk=kk: e.matmul(
                            at_ps[sl][:, kk * 128:(kk + 1) * 128],
                            mexp[sl][:, kk * 128:(kk + 1) * 128], Dt[sl][:],
                            start=(kk == 0), stop=(kk == 2)))
                    attmk[("ta", G)] = C.val("tat")

                def Va(h):
                    G = G0 + h
                    sl = G % 2
                    C.wait("vector", "tat", attmk[("ta", G)])
                    if ("tc", G - 2) in attmk:
                        C.wait("vector", "tcx", attmk[("tc", G - 2)])
                    C.prod("vector", "vat", 1, lambda e, sl=sl: e.tensor_copy(
                        atb[sl][:], at_ps[sl][:]))
                    attmk[("va", G)] = C.val("vat")

                def Tc(h):
                    G = G0 + h
                    sl = G % 2
                    hr = (h % 2) * 64
                    C.wait("tensor", "vat", attmk[("va", G)])
                    if h % 2 == 0 and ("vc", G - 1) in attmk:
                        C.wait("tensor", "vcx", attmk[("vc", G - 1)])
                    for kk in range(3):
                        C.prod("tensor", "tcx", 1, lambda e, kk=kk, h=h, hr=hr, kb0=kb0, sl=sl,
                               s=s: e.matmul(
                                   cx_ps[hr:hr + 64, :],
                                   VT[(s * NBLK + kb0 + kk) % 9][:, h * 64:(h + 1) * 64],
                                   atb[sl][:, kk * 128:(kk + 1) * 128],
                                   start=(kk == 0), stop=(kk == 2)))
                    attmk[("tc", G)] = C.val("tcx")

                def Vc(h):
                    G = G0 + h
                    C.wait("vector", "tcx", attmk[("tc", G)])
                    C.prod("vector", "vcx", 1, lambda e, h=h, b=b: e.tensor_copy(
                        ctxT[b % 2][:, (h // 2) * 128:(h // 2 + 1) * 128], cx_ps[:]))
                    attmk[("vc", G)] = C.val("vcx")
                    attmk[("vc", G - 1)] = C.val("vcx")

                for i in range(NH + 2):
                    if i < NH:
                        St(i)
                    if 0 <= i - 1 < NH:
                        Vn(i - 1)
                        Se(i - 1)
                    if 0 <= i - 2 < NH:
                        h = i - 2
                        Vd(h)
                        Ta(h)
                        Va(h)
                        Tc(h)
                        if h % 2 == 1:
                            Vc(h)
                    filler()
                for h in range(NH):
                    C.ctr_inc("hctr")

            def emit_proj(s, b):
                t4, b4 = b // 4, b % 4
                slot = (s * NT + t4) % 2
                C.wait("tensor", "wouta", 16 * DC)
                C.wait("tensor", "vcx", C.val("vcx"))
                for dc in range(DC):
                    bank = acquire_mm_bank()
                    for ec in range(DC):
                        C.prod("tensor", "tmm", 1, lambda e, dc=dc, ec=ec, bank=bank, b=b:
                               e.matmul(mm_ps[bank][:, 0:128],
                                        woutT[ec][:, dc * 128:(dc + 1) * 128],
                                        ctxT[b % 2][:, ec * 128:(ec + 1) * 128],
                                        start=(ec == 0), stop=False))
                    C.prod("tensor", "tmm", 1, lambda e, dc=dc, bank=bank: e.matmul(
                        mm_ps[bank][:, 0:128], bo_row[0:1, dc * 128:(dc + 1) * 128],
                        onesrbf[0:1, 0:128], start=False, stop=True))
                    C.wait("vector", "tmm")
                    xdst = xfm[slot][:, dc * 512 + b4 * 128: dc * 512 + b4 * 128 + 128]
                    C.prod("vector", f"mmfree{bank}", 1, lambda e, dc=dc, xdst=xdst, bank=bank,
                           s=s: e.scalar_tensor_tensor(xdst, mm_ps[bank][:, 0:128],
                                                       mv(MV_GA, dc, s), xdst, ALU.mult, ALU.add))

            def spill_x1(s, t4):
                slot = (s * NT + t4) % 2
                C.wait("gpsimd", "mmfree0", C.val("mmfree0"))
                C.wait("gpsimd", "mmfree1", C.val("mmfree1"))
                for c in range(DC):
                    C.prod("gpsimd", f"x1sp{slot}", 16, lambda e, c=c, s=s, t4=t4, slot=slot:
                           e.dma_start(out=x1_h[c, :, tok0(s, t4): tok0(s, t4) + 512],
                                       in_=xfm[slot][:, c * 512:(c + 1) * 512]))

            def process_block(s, b, filler):
                emit_attention(s, b, filler)
                emit_proj(s, b)
                emit_ln(s, b, second=True)
                if b % 4 == 3:
                    spill_x1(s, b // 4)

            pending = []
            for s in range(BPC):
                for t4 in range(NT):
                    gt4 = s * NT + t4
                    slot = gt4 % 2
                    C.wait("sync", f"x1sp{slot}", C.val(f"x1sp{slot}"))
                    C.wait("sync", "vh2m", C.val("vh2m"))
                    for c in range(DC):
                        C.prod("sync", f"qx{slot}", 16, lambda e, c=c, s=s, t4=t4, slot=slot:
                               e.dma_start(out=xfm[slot][:, c * 512:(c + 1) * 512],
                                           in_=x_fm[c, :, tok0(s, t4): tok0(s, t4) + 512]))
                    C.wait("vector", f"qx{slot}")
                    C.wait("scalar", f"qx{slot}")
                    C.wait("gpsimd", f"qx{slot}")
                    for b4 in range(4):
                        emit_ln(s, t4 * 4 + b4, second=False)
                    work = qkv_groups(s, t4)

                    def filler():
                        g = next(work, None)
                        if g is not None:
                            g()

                    for (ps, pb) in pending:
                        process_block(ps, pb, filler)
                    for g in work:
                        g()
                    pending = [(s, t4 * 4 + b4) for b4 in range(4)]
            for (ps, pb) in pending:
                process_block(ps, pb, lambda: None)
            C.run_block("A")

        with ExitStack() as esR:
            r_ps = esR.enter_context(nc.psum_tensor("r_ps", [1, 128], F32))
            rb_ps = esR.enter_context(nc.psum_tensor("rb_ps", [128, 16], F32))

            C.wait("vector", "vh2m", C.val("vh2m"))
            C.prodw("vector", "vh2m", 1, lambda e: e.tensor_reduce(
                runmax[:, 0:1], maxbuf[:, 0:BPC * NBLK], AX.X, ALU.max))
            C.wait("tensor", "vh2m", C.val("vh2m"))
            C.prod("tensor", "tarX", 1, lambda e: e.matmul(
                r_ps[:], runmax[:, 0:1], idf[:], start=True, stop=True))
            C.wait("vector", "tarX")
            C.prodw("vector", "varX", 1, lambda e: e.tensor_reduce(
                ccrow[0:1, 0:1], r_ps[:], AX.X, ALU.max))
            C.wait("gpsimd", "varX")
            C.prod("gpsimd", "ccwX", 16, lambda e: e.dma_start(out=cc_in.ap(), in_=ccrow[:]))
            C.wait("gpsimd", "ccwX")
            C.prod("gpsimd", "arQ", 1, lambda e: e.collective_compute(
                "AllReduce", ALU.max, replica_groups=[list(range(NCORE))],
                ins=[cc_in.ap()], outs=[cc_out.ap()]))
            C.wait("gpsimd", "arQ")
            C.prod("gpsimd", "ccrX", 16, lambda e: e.dma_start(out=ccrow[:], in_=cc_out.ap()))
            C.wait("vector", "ccrX")
            C.prodw("vector", "ardX", 1, lambda e: e.reciprocal(rows[0:1, 0:1], ccrow[0:1, 0:1]))
            C.prodw("vector", "ardX", 1, lambda e: e.tensor_scalar(
                rows[0:1, 0:1], rows[0:1, 0:1], 32767.0, None, ALU.mult))
            C.wait("tensor", "ardX")
            C.prod("tensor", "trbX", 1, lambda e: e.matmul(
                rb_ps[:, 0:1], onesrf[:], rows[0:1, 0:1], start=True, stop=True))
            C.wait("vector", "trbX")
            C.prod("vector", "vrbX", 1, lambda e: e.tensor_copy(scv(SC_INVX), rb_ps[:, 0:1]))
            if dbg:
                C.wait("gpsimd", "vrbX")
                C.prod("gpsimd", "dbgsc", 16, lambda e: e.dma_start(out=scal_h, in_=ccrow[:]))
            C.run_block("AR1")

        MG = MAGIC
        with ExitStack() as esF:
            sbF = lambda n, sh, dt=F32: esF.enter_context(nc.sbuf_tensor(n, sh, dt))
            wdT = [sbF(f"wdT{h}", [128, D], BF16) for h in range(HC)]

            with ExitStack() as esB:
                sbB = lambda n, sh, dt=F32: esB.enter_context(nc.sbuf_tensor(n, sh, dt))
                Rb = [sbB(f"Rb{c}", [128, TOK], BF16) for c in range(DC)]
                wraw = [sbB(f"wraw{i}", [128, D]) for i in range(2)]
                wtmp = sbB("wtmp", [128, D])
                wq32 = sbB("wq32", [128, D])
                blkmx = sbB("blkmx", [128, 32])
                binv = sbB("binv", [128, 32])
                bscl = sbB("bscl", [128, 32])
                wstr = [sbB(f"wstr{i}", [128, DC * 128], BF16) for i in range(2)]
                qtmp = [sbB(f"qtmp{i}", [128, 1024]) for i in range(2)]
                spst = [sbB(f"spst{i}", [128, TOK], BF16) for i in range(2)]
                mmB = [esB.enter_context(nc.psum_tensor(f"mmB{i}", [128, 512], F32))
                       for i in range(2)]
                wtps = esB.enter_context(nc.psum_tensor("wtps", [128, 1024], F32))

                for _eng in Ctx.ENGINES:
                    for nm in ("x1sp0", "x1sp1", "h2sp0", "h2sp1"):
                        C.wait(_eng, nm)
                    C.wait(_eng, "vrbX")

                h2in = [esB.enter_context(nc.sbuf_tensor(f"h2in{i}", [128, 1024], BF16))
                        for i in range(2)]
                for dc in range(DC):
                    for half in range(4):
                        hw = 1024
                        r2 = (dc * 4 + half) % 2
                        C.wait("sync", "b0q", max(0, C.val("b0q") - 1))
                        C.prod("sync", f"ldA{r2}", 16, lambda e, dc=dc, half=half, r2=r2:
                               e.dma_start(out=h2in[r2][:],
                                           in_=h2_h[dc, :, half * hw:(half + 1) * hw]))
                        C.wait("scalar", f"ldA{r2}")
                        C.wait("scalar", "b0q", max(0, C.val("b0q") - 1))
                        C.prod("scalar", "b0s", 1, lambda e, r2=r2: e.activation(
                            qtmp[r2][:], h2in[r2][:], AF.Identity, bias=mgc[:, 0:1], scale=scv(SC_INVX)))
                        C.wait("vector", "b0s")
                        C.prod("vector", "b0q", 1, lambda e, dc=dc, half=half, r2=r2:
                               e.tensor_scalar(Rb[dc][:, half * hw:(half + 1) * hw],
                                               qtmp[r2][:], -MG, None, ALU.add))
                C.wait("tensor", "b0q", C.val("b0q"))
                bqm_marks = []
                bdq_marks = []
                for wi, (wsrc, wdst, rmcol) in enumerate(
                        ((w_up, upr_h, 1), (w_gate, gtr_h, 2))):
                    for hc in range(HC):
                        wsl = (wi * HC + hc) % 2
                        C.wait("sync", "bdq", bdq_marks[-2] if len(bdq_marks) >= 2 else 0)
                        C.prod("sync", f"ldA{wsl}", 16, lambda e, wsrc=wsrc, hc=hc, wsl=wsl:
                               e.dma_start(out=wraw[wsl][:], in_=wsrc[hc * 128:(hc + 1) * 128, :]))
                        C.wait("vector", f"ldA{wsl}")
                        C.prodw("vector", "bdq", 1, lambda e, wsl=wsl: e.tensor_reduce(
                            blkmx[:], wraw[wsl][:].rearrange("p (a b) -> p a b", b=32),
                            AX.X, ALU.max, apply_absolute_value=True))
                        C.prodw("vector", "bdq", 1, lambda e: e.tensor_scalar(
                            bscl[:], blkmx[:], 1.0 / 7.0, 1e-12, ALU.mult, ALU.max))
                        C.prodw("vector", "bdq", 1, lambda e: e.reciprocal(binv[:], bscl[:]))
                        C.prod("vector", "bdq", 1, lambda e, wsl=wsl: e.tensor_tensor(
                            wtmp[:].rearrange("p (a b) -> p a b", b=32),
                            wraw[wsl][:].rearrange("p (a b) -> p a b", b=32),
                            binv[:].unsqueeze(2).broadcast_to([128, 32, 32]), ALU.mult))
                        bdq_marks.append(C.val("bdq"))
                        C.prod("vector", "bdq", 1, lambda e: e.tensor_scalar(
                            wtmp[:], wtmp[:], MG, -MG, ALU.add, ALU.add))
                        C.prod("vector", "bdq", 1, lambda e: e.tensor_tensor(
                            wq32[:].rearrange("p (a b) -> p a b", b=32),
                            wtmp[:].rearrange("p (a b) -> p a b", b=32),
                            bscl[:].unsqueeze(2).broadcast_to([128, 32, 32]), ALU.mult))
                        C.wait("tensor", "bdq")
                        C.wait("tensor", "bst", C.val("bst"))
                        for dc in range(DC):
                            C.prod("tensor", "btr", 1, lambda e, dc=dc: e.transpose(
                                wtps[:, dc * 128:(dc + 1) * 128], wq32[:, dc * 128:(dc + 1) * 128],
                                idf[:]))
                        C.wait("scalar", "btr")
                        C.wait("scalar", "bqm", bqm_marks[-2] if len(bqm_marks) >= 2 else 0)
                        C.prod("scalar", "bst", 1, lambda e, wsl=wsl: e.copy(
                            wstr[wsl][:], wtps[:]))
                        C.wait("tensor", "bst", C.val("bst"))
                        ssl = C.ctr("bse") % 2
                        C.ctr_inc("bse")
                        C.wait("scalar", f"sd{ssl}", C.val(f"sd{ssl}"))
                        for tt in range(NTT):
                            bank = C.counts["mmuse"] % 2
                            C.wait("tensor", f"mmfree{bank}", C.val(f"mmfree{bank}"))
                            C.counts["mmuse"] += 1
                            for dc in range(DC):
                                C.prod("tensor", "bqm", 1, lambda e, dc=dc, tt=tt, bank=bank,
                                       wsl=wsl: e.matmul(
                                           mmB[bank][:],
                                           wstr[wsl][:, dc * 128:(dc + 1) * 128],
                                           Rb[dc][:, tt * 512:(tt + 1) * 512],
                                           start=(dc == 0), stop=(dc == DC - 1)))
                            C.wait("scalar", "bqm")
                            C.prod("scalar", f"mmfree{bank}", 1, lambda e, bank=bank, ssl=ssl,
                                   tt=tt: e.copy(spst[ssl][:, tt * 512:(tt + 1) * 512],
                                                 mmB[bank][:]))
                            C.wait("vector", f"mmfree{bank}")
                            bcol = wi * 192 + hc * 8 + tt
                            C.prod("vector", "bsmx0", 1, lambda e, ssl=ssl, tt=tt, bcol=bcol:
                                   e.tensor_reduce(maxbuf[:, bcol:bcol + 1],
                                                   spst[ssl][:, tt * 512:(tt + 1) * 512],
                                                   AX.X, ALU.max, apply_absolute_value=True))
                        C.wait("gpsimd", "mmfree0", C.val("mmfree0"))
                        C.wait("gpsimd", "mmfree1", C.val(f"mmfree1"))
                        C.prod("gpsimd", f"sd{ssl}", 16, lambda e, wdst=wdst, hc=hc, ssl=ssl:
                               e.dma_start(out=wdst[hc, :, :], in_=spst[ssl][:]))
                        bqm_marks.append(C.val("bqm"))
            with ExitStack() as esR2:
                r_ps2 = esR2.enter_context(nc.psum_tensor("r_ps2", [1, 128], F32))
                rb_ps2 = esR2.enter_context(nc.psum_tensor("rb_ps2", [128, 16], F32))
                C.wait("vector", "bsmx0", C.val("bsmx0"))
                C.prod("vector", "bvmx", 1, lambda e: e.tensor_reduce(
                    runmax[:, 1:2], maxbuf[:, 0:192], AX.X, ALU.max))
                C.prodw("vector", "bvmx", 1, lambda e: e.tensor_reduce(
                    runmax[:, 2:3], maxbuf[:, 192:384], AX.X, ALU.max))
                C.prod("vector", "bvmx", 1, lambda e: e.memset(maxbuf[:], 0.0))
                C.wait("tensor", "bvmx", C.val("bvmx"))
                for j, col in enumerate((1, 2)):
                    C.wait("tensor", "varX", C.val("varX"))
                    C.prod("tensor", "tarX", 1, lambda e, col=col: e.matmul(
                        r_ps2[:], runmax[:, col:col + 1], idf[:], start=True, stop=True))
                    C.wait("vector", "tarX")
                    C.prodw("vector", "varX", 1, lambda e, j=j: e.tensor_reduce(
                        ccrow[0:1, j:j + 1], r_ps2[:], AX.X, ALU.max))
                C.wait("gpsimd", "varX")
                C.prod("gpsimd", "ccwX", 16, lambda e: e.dma_start(out=cc_in.ap(), in_=ccrow[:]))
                C.wait("gpsimd", "ccwX")
                C.prod("gpsimd", "arQ", 1, lambda e: e.collective_compute(
                    "AllReduce", ALU.max, replica_groups=[list(range(NCORE))],
                    ins=[cc_in.ap()], outs=[cc_out.ap()]))
                C.wait("gpsimd", "arQ")
                C.prod("gpsimd", "ccrX", 16, lambda e: e.dma_start(out=ccrow[:], in_=cc_out.ap()))
                C.wait("vector", "ccrX")
                C.prodw("vector", "ardX", 1, lambda e: e.reciprocal(rows[0:1, 21:22], scal[0:1, SC_INVX:SC_INVX + 1]))
                C.prodw("vector", "ardX", 1, lambda e: e.tensor_tensor(
                    rows[0:1, 16:17], ccrow[0:1, 0:1], rows[0:1, 21:22], ALU.mult))
                C.prodw("vector", "ardX", 1, lambda e: e.tensor_scalar(
                    rows[0:1, 16:17], rows[0:1, 16:17], 1.0 / 32767.0, None, ALU.mult))
                C.prodw("vector", "ardX", 1, lambda e: e.tensor_tensor(
                    rows[0:1, 17:18], ccrow[0:1, 1:2], rows[0:1, 21:22], ALU.mult))
                C.prodw("vector", "ardX", 1, lambda e: e.tensor_scalar(
                    rows[0:1, 17:18], rows[0:1, 17:18], 1.0 / 32767.0, None, ALU.mult))
                C.prodw("vector", "ardX", 1, lambda e: e.reciprocal(rows[0:1, 18:19], ccrow[0:1, 1:2]))
                C.prodw("vector", "ardX", 1, lambda e: e.tensor_scalar(
                    rows[0:1, 18:19], rows[0:1, 18:19], 32767.0, None, ALU.mult))
                C.prodw("vector", "ardX", 1, lambda e: e.reciprocal(rows[0:1, 19:20], ccrow[0:1, 0:1]))
                C.prodw("vector", "ardX", 1, lambda e: e.tensor_scalar(
                    rows[0:1, 19:20], rows[0:1, 19:20], 32767.0, None, ALU.mult))
                C.prodw("vector", "ardX", 1, lambda e: e.tensor_scalar(
                    rows[0:1, 20:21], rows[0:1, 17:18], -MG, None, ALU.mult))
                C.wait("tensor", "ardX")
                C.prod("tensor", "trbX", 1, lambda e: e.matmul(
                    rb_ps2[:, 0:4], onesrf[:], rows[0:1, 17:21], start=True, stop=True))
                C.wait("vector", "trbX")
                C.prod("vector", "vrbX", 1, lambda e: e.tensor_copy(scal[:, SC_SGT:SC_SGT + 1], rb_ps2[:, 0:1]))
                C.prod("vector", "vrbX", 1, lambda e: e.tensor_copy(scal[:, SC_KG:SC_KG + 1], rb_ps2[:, 1:2]))
                C.prod("vector", "vrbX", 1, lambda e: e.tensor_copy(scal[:, SC_KU:SC_KU + 1], rb_ps2[:, 2:3]))
                C.prodw("vector", "vrbX", 1, lambda e: e.tensor_copy(scal[:, SC_BGT:SC_BGT + 1], rb_ps2[:, 3:4]))
                C.run_block() if False else None

            with ExitStack() as esC:
                sbC = lambda n, sh, dt=F32: esC.enter_context(nc.sbuf_tensor(n, sh, dt))
                gg = [sbC(f"gg{i}", [128, 512]) for i in range(2)]
                sg = [sbC(f"sg{i}", [128, 512]) for i in range(2)]
                t2b = [sbC(f"t2b{i}", [128, 512]) for i in range(2)]
                s2c = [sbC(f"s2c{i}", [128, 512]) for i in range(2)]
                wrawC = [sbC(f"wrawC{i}", [128, HID]) for i in range(2)]
                wtmpC = sbC("wtmpC", [128, HID])
                wqC = sbC("wqC", [128, HID])
                bmxC = sbC("bmxC", [128, 96])
                bivC = sbC("bivC", [128, 96])
                bscC = sbC("bscC", [128, 96])
                wtpsC = esC.enter_context(nc.psum_tensor("wtpsC", [128, 1024], F32))

                for _eng in Ctx.ENGINES:
                    C.wait(_eng, "vrbX")
                    C.wait(_eng, "sd0"); C.wait(_eng, "sd1")
                for rc in range(DC):
                    wsl = rc % 2
                    C.wait("sync", "cdqv", max(0, C.val("cdqv") - 1))
                    C.prod("sync", f"ldA{wsl}", 16, lambda e, rc=rc, wsl=wsl: e.dma_start(
                        out=wrawC[wsl][:], in_=w_down[rc * 128:(rc + 1) * 128, :]))
                    C.wait("vector", f"ldA{wsl}")
                    C.prodw("vector", "cdq", 1, lambda e, wsl=wsl: e.tensor_reduce(
                        bmxC[:], wrawC[wsl][:].rearrange("p (a b) -> p a b", b=32),
                        AX.X, ALU.max, apply_absolute_value=True))
                    C.prodw("vector", "cdq", 1, lambda e: e.tensor_scalar(
                        bscC[:], bmxC[:], 1.0 / 7.0, 1e-12, ALU.mult, ALU.max))
                    C.prodw("vector", "cdq", 1, lambda e: e.reciprocal(bivC[:], bscC[:]))
                    C.prod("vector", "cdq", 1, lambda e, wsl=wsl: e.tensor_tensor(
                        wtmpC[:].rearrange("p (a b) -> p a b", b=32),
                        wrawC[wsl][:].rearrange("p (a b) -> p a b", b=32),
                        bivC[:].unsqueeze(2).broadcast_to([128, 96, 32]), ALU.mult))
                    C.prod("vector", "cdq", 1, lambda e: e.tensor_scalar(
                        wtmpC[:], wtmpC[:], MG, -MG, ALU.add, ALU.add))
                    C.prod("vector", "cdqv", 1, lambda e: e.tensor_tensor(
                        wqC[:].rearrange("p (a b) -> p a b", b=32),
                        wtmpC[:].rearrange("p (a b) -> p a b", b=32),
                        bscC[:].unsqueeze(2).broadcast_to([128, 96, 32]), ALU.mult))
                    C.wait("tensor", "cdqv")
                    for grp in range(HC // 8):
                        C.wait("tensor", "cst", C.val("cst"))
                        for bb in range(8):
                            hh = grp * 8 + bb
                            C.prod("tensor", "ctr", 1, lambda e, hh=hh, bb=bb: e.transpose(
                                wtpsC[:, bb * 128:(bb + 1) * 128],
                                wqC[:, hh * 128:(hh + 1) * 128], idf[:]))
                        C.wait("scalar", "ctr")
                        for bb in range(8):
                            hh = grp * 8 + bb
                            C.prod("scalar", "cst", 1, lambda e, hh=hh, bb=bb, rc=rc: e.copy(
                                wdT[hh][:, rc * 128:(rc + 1) * 128],
                                wtpsC[:, bb * 128:(bb + 1) * 128]))
                grow = [esC.enter_context(nc.sbuf_tensor(f"grow{i}", [128, TOK], BF16))
                        for i in range(2)]
                pprow = [esC.enter_context(nc.sbuf_tensor(f"pprow{i}", [128, TOK], BF16))
                        for i in range(2)]
                for hc in range(HC):
                    r2 = hc % 2
                    C.wait("sync", f"sd{r2}", C.val(f"sd{r2}"))
                    C.wait("sync", "cpp", max(0, C.val("cpp") - NTT))
                    C.prod("sync", f"ldA{r2}", 16, lambda e, hc=hc, r2=r2: e.dma_start(
                        out=grow[r2][:], in_=gtr_h[hc, :, :]))
                    C.wait("vector", f"ldA{r2}")
                    C.wait("vector", f"sd{r2}", C.val(f"sd{r2}"))
                    C.wait("scalar", f"ldA{r2}")
                    C.wait("gpsimd", f"ldA{r2}")
                    for tt in range(NTT):
                        t2_ = (hc * NTT + tt) % 2
                        gsl = grow[r2][:, tt * 512:(tt + 1) * 512]
                        psl = pprow[r2][:, tt * 512:(tt + 1) * 512]
                        C.wait("vector", "csg", max(0, C.val("csg") - 1))
                        C.prod("vector", "cgg", 1, lambda e, gsl=gsl, t2_=t2_: e.tensor_scalar(
                            gg[t2_][:], gsl, scv(SC_KG), MG, ALU.mult, ALU.add))
                        C.wait("scalar", "cgg")
                        C.wait("scalar", "cs2", max(0, C.val("cs2") - 1))
                        C.prod("scalar", "csg", 1, lambda e, t2_=t2_: e.activation(
                            sg[t2_][:], gg[t2_][:], AF.Sigmoid, bias=scv(SC_BGT), scale=scv(SC_SGT)))
                        C.prod("scalar", "ct2", 1, lambda e, t2_=t2_: e.activation(
                            t2b[t2_][:], sg[t2_][:], AF.Identity, bias=mgc[:, 0:1], scale=65536.0))
                        C.wait("vector", "ct2")
                        C.prod("vector", "cs2", 1, lambda e, t2_=t2_: e.tensor_scalar(
                            s2c[t2_][:], t2b[t2_][:], -MG, 65535.0, ALU.add, ALU.min))
                        C.prod("vector", "cpp0", 1, lambda e, t2_=t2_, psl=psl: e.scalar_tensor_tensor(
                            psl, gg[t2_][:], -MG, s2c[t2_][:], ALU.add, ALU.mult))
                        ccol = hc * 8 + tt
                        C.prod("vector", "cpp", 1, lambda e, psl=psl, ccol=ccol: e.tensor_reduce(
                            maxbuf[:, ccol:ccol + 1], psl, AX.X, ALU.max,
                            apply_absolute_value=True))
                    C.wait("gpsimd", "cpp0", C.val("cpp0"))
                    C.prod("gpsimd", f"sd{r2}", 16, lambda e, hc=hc, r2=r2: e.dma_start(
                        out=pp_h[hc, :, :], in_=pprow[r2][:]))

            with ExitStack() as esR3:
                r_ps3 = esR3.enter_context(nc.psum_tensor("r_ps3", [1, 128], F32))
                rb_ps3 = esR3.enter_context(nc.psum_tensor("rb_ps3", [128, 16], F32))
                C.wait("vector", "cpp", C.val("cpp"))
                C.prod("vector", "cpp", 1, lambda e: e.tensor_reduce(
                    runmax[:, 3:4], maxbuf[:, 0:192], AX.X, ALU.max))
                C.prodw("vector", "cpp", 1, lambda e: e.memset(maxbuf[:, 0:192], 0.0))
                C.wait("tensor", "cpp", C.val("cpp"))
                C.prod("tensor", "tarX", 1, lambda e: e.matmul(
                    r_ps3[:], runmax[:, 3:4], idf[:], start=True, stop=True))
                C.wait("vector", "tarX")
                C.prodw("vector", "varX", 1, lambda e: e.tensor_reduce(
                    ccrow[0:1, 0:1], r_ps3[:], AX.X, ALU.max))
                C.wait("gpsimd", "varX")
                C.prod("gpsimd", "ccwX", 16, lambda e: e.dma_start(out=cc_in.ap(), in_=ccrow[:]))
                C.wait("gpsimd", "ccwX")
                C.prod("gpsimd", "arQ", 1, lambda e: e.collective_compute(
                    "AllReduce", ALU.max, replica_groups=[list(range(NCORE))],
                    ins=[cc_in.ap()], outs=[cc_out.ap()]))
                C.wait("gpsimd", "arQ")
                C.prod("gpsimd", "ccrX", 16, lambda e: e.dma_start(out=ccrow[:], in_=cc_out.ap()))
                C.wait("vector", "ccrX")
                C.prodw("vector", "ardX", 1, lambda e: e.reciprocal(rows[0:1, 23:24], ccrow[0:1, 0:1]))
                C.prodw("vector", "ardX", 1, lambda e: e.tensor_scalar(
                    rows[0:1, 23:24], rows[0:1, 23:24], 32767.0, None, ALU.mult))
                C.prodw("vector", "ardX", 1, lambda e: e.tensor_tensor(
                    rows[0:1, 22:23], ccrow[0:1, 0:1], rows[0:1, 17:18], ALU.mult))
                C.prodw("vector", "ardX", 1, lambda e: e.tensor_scalar(
                    rows[0:1, 22:23], rows[0:1, 22:23], 1.0 / (65536.0 * 32767.0), None, ALU.mult))
                C.wait("tensor", "ardX")
                C.prod("tensor", "trbX", 1, lambda e: e.matmul(
                    rb_ps3[:, 0:1], onesrf[:], rows[0:1, 23:24], start=True, stop=True))
                C.wait("vector", "trbX")
                C.prodw("vector", "vrbX", 1, lambda e: e.tensor_copy(scv(SC_K1), rb_ps3[:, 0:1]))

            with ExitStack() as esD:
                sbD = lambda n, sh, dt=F32: esD.enter_context(nc.sbuf_tensor(n, sh, dt))
                t3b = [sbD(f"t3b{i}", [128, 512]) for i in range(2)]
                tub = [sbD(f"tub{i}", [128, 512]) for i in range(2)]
                ugb = [sbD(f"ugb{i}", [128, 512]) for i in range(2)]
                for _eng in Ctx.ENGINES:
                    C.wait(_eng, "vrbX")
                    C.wait(_eng, "sd0"); C.wait(_eng, "sd1")
                prow = [esD.enter_context(nc.sbuf_tensor(f"prow{i}", [128, TOK], BF16))
                        for i in range(2)]
                urow = [esD.enter_context(nc.sbuf_tensor(f"urow{i}", [128, TOK], BF16))
                        for i in range(2)]
                oorow = [esD.enter_context(nc.sbuf_tensor(f"oorow{i}", [128, TOK], BF16))
                         for i in range(2)]
                for hc in range(HC):
                    r2 = hc % 2
                    C.wait("sync", f"sd{r2}", C.val(f"sd{r2}"))
                    C.wait("sync", "doo", max(0, C.val("doo") - NTT))
                    C.prod("sync", f"ldA{r2}", 16, lambda e, hc=hc, r2=r2: e.dma_start(
                        out=prow[r2][:], in_=pp_h[hc, :, :]))
                    C.prod("sync", f"ldA{r2}", 16, lambda e, hc=hc, r2=r2: e.dma_start(
                        out=urow[r2][:], in_=upr_h[hc, :, :]))
                    C.wait("scalar", f"ldA{r2}")
                    C.wait("gpsimd", f"ldA{r2}")
                    C.wait("vector", f"sd{r2}", C.val(f"sd{r2}"))
                    for tt in range(NTT):
                        t2_ = (hc * NTT + tt) % 2
                        psl = prow[r2][:, tt * 512:(tt + 1) * 512]
                        usl = urow[r2][:, tt * 512:(tt + 1) * 512]
                        osl_ = oorow[r2][:, tt * 512:(tt + 1) * 512]
                        C.wait("scalar", "doo0", max(0, C.val("doo0") - 1))
                        C.prod("scalar", "dt3", 1, lambda e, t2_=t2_, psl=psl: e.activation(
                            t3b[t2_][:], psl, AF.Identity, bias=mgc[:, 0:1], scale=scv(SC_K1)))
                        C.prod("scalar", "dtu", 1, lambda e, t2_=t2_, usl=usl: e.activation(
                            tub[t2_][:], usl, AF.Identity, bias=mgc[:, 0:1], scale=scv(SC_KU)))
                        C.prod("scalar", "dug", 1, lambda e, t2_=t2_: e.activation(
                            ugb[t2_][:], tub[t2_][:], AF.Identity, bias=mgc[:, 1:2]))
                        C.wait("vector", "dug")
                        C.prod("vector", "doo0", 1, lambda e, t2_=t2_, osl_=osl_: e.scalar_tensor_tensor(
                            osl_, t3b[t2_][:], -MG, ugb[t2_][:], ALU.add, ALU.mult))
                        dcol = hc * 8 + tt
                        C.prod("vector", "doo", 1, lambda e, osl_=osl_, dcol=dcol: e.tensor_reduce(
                            maxbuf[:, dcol:dcol + 1], osl_, AX.X, ALU.max,
                            apply_absolute_value=True))
                    C.wait("gpsimd", "doo0", C.val("doo0"))
                    C.prod("gpsimd", f"sd{r2}", 16, lambda e, hc=hc, r2=r2: e.dma_start(
                        out=oo_h[hc, :, :], in_=oorow[r2][:]))

            with ExitStack() as esR4:
                r_ps4 = esR4.enter_context(nc.psum_tensor("r_ps4", [1, 128], F32))
                rb_ps4 = esR4.enter_context(nc.psum_tensor("rb_ps4", [128, 16], F32))
                C.wait("vector", "doo", C.val("doo"))
                C.prodw("vector", "doo", 1, lambda e: e.tensor_reduce(
                    runmax[:, 4:5], maxbuf[:, 0:192], AX.X, ALU.max))
                C.wait("tensor", "doo", C.val("doo"))
                C.prod("tensor", "tarX", 1, lambda e: e.matmul(
                    r_ps4[:], runmax[:, 4:5], idf[:], start=True, stop=True))
                C.wait("vector", "tarX")
                C.prodw("vector", "varX", 1, lambda e: e.tensor_reduce(
                    ccrow[0:1, 0:1], r_ps4[:], AX.X, ALU.max))
                C.wait("gpsimd", "varX")
                C.prod("gpsimd", "ccwX", 16, lambda e: e.dma_start(out=cc_in.ap(), in_=ccrow[:]))
                C.wait("gpsimd", "ccwX")
                C.prod("gpsimd", "arQ", 1, lambda e: e.collective_compute(
                    "AllReduce", ALU.max, replica_groups=[list(range(NCORE))],
                    ins=[cc_in.ap()], outs=[cc_out.ap()]))
                C.wait("gpsimd", "arQ")
                C.prod("gpsimd", "ccrX", 16, lambda e: e.dma_start(out=ccrow[:], in_=cc_out.ap()))
                C.wait("vector", "ccrX")
                C.prodw("vector", "ardX", 1, lambda e: e.reciprocal(rows[0:1, 24:25], ccrow[0:1, 0:1]))
                C.prodw("vector", "ardX", 1, lambda e: e.tensor_scalar(
                    rows[0:1, 24:25], rows[0:1, 24:25], 32767.0, None, ALU.mult))
                C.prodw("vector", "ardX", 1, lambda e: e.tensor_tensor(
                    rows[0:1, 25:26], ccrow[0:1, 0:1], rows[0:1, 22:23], ALU.mult))
                C.prodw("vector", "ardX", 1, lambda e: e.tensor_tensor(
                    rows[0:1, 25:26], rows[0:1, 25:26], rows[0:1, 16:17], ALU.mult))
                C.prodw("vector", "ardX", 1, lambda e: e.tensor_scalar(
                    rows[0:1, 25:26], rows[0:1, 25:26], 1.0 / 32767.0, None, ALU.mult))
                C.wait("tensor", "ardX")
                C.prod("tensor", "trbX", 1, lambda e: e.matmul(
                    rb_ps4[:, 0:2], onesrf[:], rows[0:1, 24:26], start=True, stop=True))
                C.wait("vector", "trbX")
                C.prod("vector", "vrbX", 1, lambda e: e.tensor_copy(scv(SC_K2), rb_ps4[:, 0:1]))
                C.prodw("vector", "vrbX", 1, lambda e: e.tensor_copy(scv(SC_SO), rb_ps4[:, 1:2]))
                C.prodw("vector", "vrbX", 1, lambda e: e.tensor_scalar(
                    modv[:, MV_GB * 16: MV_GB * 16 + 16],
                    modv[:, MV_GB * 16: MV_GB * 16 + 16], scv(SC_SO), None, ALU.mult))

            with ExitStack() as esE:
                sbE = lambda n, sh, dt=F32: esE.enter_context(nc.sbuf_tensor(n, sh, dt))
                oin = [sbE(f"oin{i}", [128, 512], BF16) for i in range(2)]
                t4b = [sbE(f"t4b{i}", [128, 512]) for i in range(2)]
                Ob = [sbE(f"Ob{i}", [128, 512], BF16) for i in range(HC)]
                x1t = [sbE(f"x1t{i}", [128, DC * 512]) for i in range(2)]
                outb = [sbE(f"outb{i}", [128, 512]) for i in range(4)]
                mmE = [esE.enter_context(nc.psum_tensor(f"mmE{i}", [128, 512], F32))
                       for i in range(DC)]
                for _eng in Ctx.ENGINES:
                    C.wait(_eng, "vrbX")
                    C.wait(_eng, "sd0"); C.wait(_eng, "sd1")
                obmark = {}
                evm = {}
                evt = {}
                for tt in range(NTT):
                    s = tt // NT
                    xs = tt % 2
                    if tt - 2 in evt:
                        C.wait("sync", "mmfree0", evt[tt - 2])
                    for dc in range(DC):
                        C.prod("sync", f"ex1{xs}", 16, lambda e, dc=dc, tt=tt, xs=xs: e.dma_start(
                            out=x1t[xs][:, dc * 512:(dc + 1) * 512],
                            in_=x1_h[dc, :, tt * 512:(tt + 1) * 512]))
                    qm = {}
                    for hc in range(HC):
                        r2 = (tt * HC + hc) % 2
                        C.wait("sync", "eO", max(0, C.val("eO") - 1))
                        C.prod("sync", f"ldA{r2}", 16, lambda e, hc=hc, tt=tt, r2=r2: e.dma_start(
                            out=oin[r2][:], in_=oo_h[hc, :, tt * 512:(tt + 1) * 512]))
                        C.wait("scalar", f"ldA{r2}")
                        C.wait("scalar", "eO", max(0, C.val("eO") - 1))
                        if (tt - 1, hc) in obmark:
                            C.wait("scalar", "emm", obmark[(tt - 1, hc)])
                        C.prod("scalar", "et4", 1, lambda e, r2=r2: e.activation(
                            t4b[r2][:], oin[r2][:], AF.Identity, bias=mgc[:, 0:1], scale=scv(SC_K2)))
                        C.prod("scalar", "eO", 1, lambda e, hc=hc, r2=r2: e.activation(
                            Ob[hc][:], t4b[r2][:], AF.Identity, bias=mgc[:, 1:2]))
                        qm[hc] = C.val("eO")
                    if tt == 0:
                        C.wait("tensor", "cst", C.val("cst"))
                    for hc in range(HC):
                        C.wait("tensor", "eO", qm[hc])
                        for dc in range(DC):
                            if hc == 0 and (tt - 1, dc) in evm:
                                C.wait("tensor", "mmfree0", evm[(tt - 1, dc)])
                            C.prod("tensor", "emm", 1, lambda e, dc=dc, hc=hc:
                                   e.matmul(mmE[dc][:], wdT[hc][:, dc * 128:(dc + 1) * 128],
                                            Ob[hc][:], start=(hc == 0), stop=(hc == HC - 1)))
                        obmark[(tt, hc)] = C.val("emm")
                    C.wait("vector", "emm")
                    C.wait("vector", f"ex1{xs}", C.val(f"ex1{xs}"))
                    for dc in range(DC):
                        osl = C.ctr("eev") % 4
                        C.wait("vector", f"sd{osl}", C.val(f"sd{osl}"))
                        C.prod("vector", "mmfree0", 1, lambda e, dc=dc, s=s, xs=xs, osl=osl:
                               e.scalar_tensor_tensor(
                                   outb[osl][:], mmE[dc][:], mv(MV_GB, dc, s),
                                   x1t[xs][:, dc * 512:(dc + 1) * 512], ALU.mult, ALU.add))
                        evm[(tt, dc)] = C.val("mmfree0")
                        C.ctr_inc("eev")
                        C.wait("gpsimd", "mmfree0", C.val("mmfree0"))
                        C.prod("gpsimd", f"sd{osl}", 16, lambda e, dc=dc, tt=tt, osl=osl:
                               e.dma_start(out=out_fm[dc, :, tt * 512:(tt + 1) * 512],
                                           in_=outb[osl][:]))
                    evt[tt] = C.val("mmfree0")
                for i in range(4):
                    C.wait("gpsimd", f"sd{i}", C.val(f"sd{i}"))
                C.run_block("BE")

    return nc



def _prep_inputs(inputs):
    f32 = np.float32
    bf16 = ml_dtypes.bfloat16
    x = np.asarray(inputs["L0_attn_x"], f32)
    b_in = np.asarray(inputs["L0_attn_b_in"], f32)
    b_out = np.asarray(inputs["L0_attn_b_out"], f32)
    w_inT = np.ascontiguousarray(np.asarray(inputs["L0_attn_w_in"], f32).T)
    w_outT = np.ascontiguousarray(np.asarray(inputs["L0_attn_w_out"], f32).T)
    w_gate = np.asarray(inputs["L0_ffn_w_gate"], f32)
    w_up = np.asarray(inputs["L0_ffn_w_up"], f32)
    w_down = np.asarray(inputs["L0_ffn_w_down"], f32)
    c = np.asarray(inputs["L0_c"], f32)
    wmod = np.asarray(inputs["L0_wmod"], f32)

    ident_f = np.eye(128, dtype=f32)
    ident_bf = ident_f.astype(bf16)
    ones_r_f = np.ones((1, 128), f32)
    ones_r_bf = np.ones((1, 512), f32).astype(bf16)
    invD_c = np.full((128, 1), 1.0 / D, f32).astype(bf16)

    q = np.arange(128)[:, None]
    j = np.arange(384)[None, :]
    m_int = np.where((j >= q) & (j <= q + 256), 1.0, 0.0).astype(f32)
    m_first = np.where(np.abs(q - j) <= 128, 1.0, 0.0).astype(f32)
    m_last = np.where(j >= q + 128, 1.0, 0.0).astype(f32)
    masks = np.concatenate([m_int, m_first, m_last], axis=1).astype(bf16)

    in_maps = []
    for core in range(NCORE):
        sl = slice(core * BPC, (core + 1) * BPC)
        x_fmc = np.ascontiguousarray(
            x[sl].reshape(BPC * T, DC, 128).transpose(1, 2, 0))
        c_fmc = np.ascontiguousarray(c[sl].T.reshape(DC, 128, BPC))
        in_maps.append({
            "x_fm": x_fmc, "w_inT": w_inT, "b_in": b_in, "w_outT": w_outT,
            "b_out": b_out, "w_gate": w_gate, "w_up": w_up, "w_down": w_down,
            "c_fm": c_fmc, "wmod": wmod,
            "ident_bf": ident_bf, "ident_f": ident_f, "masks": masks,
            "ones_r_f": ones_r_f, "ones_r_bf": ones_r_bf, "invD_c_bf": invD_c,
        })
    return in_maps


def kernel(**inputs):
    in_maps = _prep_inputs(inputs)
    nc = build_nc(False)
    res = run_bass_kernel_spmd(nc, in_maps, core_ids=list(range(NCORE)))
    out = np.empty((B, T, D), np.float32)
    for core in range(NCORE):
        o = res.results[core]["out_fm"]
        out[core * BPC:(core + 1) * BPC] = o.transpose(2, 0, 1).reshape(BPC, T, D)
    return out
